# version 04
# baseline (speedup 1.0000x reference)
import contextlib
import math

import numpy as np
import ml_dtypes

import concourse.bass as bass
import concourse.mybir as mybir
from concourse.bass_utils import run_bass_kernel_spmd
from concourse.aot_env import allow_ici_all_to_all

F32 = mybir.dt.float32
BF16 = mybir.dt.bfloat16
AF = mybir.ActivationFunctionType
ALU = mybir.AluOpType
AX = mybir.AxisListType

NCORE = 8
T = 8192
TPC = T // NCORE
D = 2048
NH = 16
HPC = NH // NCORE
HD = 128
INTER = 8192
CD = D // 128
HC = INTER // 128
NEG = -1.0e30
SC = 1.0 / math.sqrt(128.0)
EPS = 1e-5
NTT = TPC // 128
NTB = T // 512
NQB = NTB



def host_constants():
    c = {}
    c["id_bf16"] = np.eye(128, dtype=ml_dtypes.bfloat16)
    c["id_f32"] = np.eye(128, dtype=np.float32)
    c["ones_1x128"] = np.ones((1, 128), dtype=ml_dtypes.bfloat16)
    c["ones_128x1"] = np.ones((128, 1), dtype=ml_dtypes.bfloat16)
    c["ones_1x128_f32"] = np.ones((1, 128), dtype=np.float32)
    c["epsb"] = np.full((128, 1), 1e-5, dtype=np.float32)
    r = np.arange(128)[:, None]
    col = np.arange(512)[None, :]
    mA = np.stack([(col <= 128 * m + r) for m in range(4)])
    c["maskA"] = np.where(mA, 0.0, NEG).astype(np.float32)
    mB = np.stack([(col >= 128 * m + r) for m in range(4)])
    c["maskB"] = np.where(mB, 0.0, NEG).astype(np.float32)
    base = 10000.0
    inv = 1.0 / base ** (np.arange(0, 64, 2, dtype=np.float64) / 64)
    fr = np.outer(inv, np.arange(T, dtype=np.float64))
    c["ropecos"] = np.tile(np.cos(fr), (4, 1)).astype(ml_dtypes.bfloat16)
    sin = np.tile(np.sin(fr), (4, 1))
    sign = np.repeat(np.array([-1.0, 1.0, -1.0, 1.0]), 32)[:, None]
    c["ropesin"] = (sin * sign).astype(ml_dtypes.bfloat16)
    mswap = np.zeros((128, 128), dtype=np.float32)
    for i in range(128):
        mswap[i ^ 32, i] = 1.0
    c["mswap"] = mswap.astype(ml_dtypes.bfloat16)
    mrot = np.zeros((128, 128), dtype=np.float32)
    for i in range(128):
        mrot[(i + 64) % 128, i] = 1.0
    c["mrot"] = mrot.astype(ml_dtypes.bfloat16)
    h = np.array([[1.0]])
    while h.shape[0] < 128:
        h = np.block([[h, h], [h, -h]])
    c["h128"] = h.astype(ml_dtypes.bfloat16)
    return c


_CONSTS = None


def prep_inputs(inp, core):
    global _CONSTS
    d = {}
    h0 = HPC * core
    cc = np.ascontiguousarray
    asnp = np.asarray

    d["l0_wq"] = cc(asnp(inp["L0_attn_Wq"])[h0:h0 + HPC])
    d["l0_wk"] = cc(asnp(inp["L0_attn_Wk"])[h0:h0 + HPC])
    d["l0_wv"] = cc(asnp(inp["L0_attn_Wv"])[h0:h0 + HPC])
    d["l0_bq"] = cc(asnp(inp["L0_attn_bq"])[h0:h0 + HPC].T)
    d["l0_bk"] = cc(asnp(inp["L0_attn_bk"])[h0:h0 + HPC].T)
    d["l0_bv"] = cc(asnp(inp["L0_attn_bv"])[h0:h0 + HPC].T)
    d["l0_wo"] = cc(asnp(inp["L0_attn_Wo"]))
    d["l0_bo"] = cc(asnp(inp["L0_attn_bo"]).reshape(CD, 128).T)
    d["l0_wg"] = cc(asnp(inp["L0_ffn_W_gate"]).T)
    d["l0_wu"] = cc(asnp(inp["L0_ffn_W_up"]).T)
    d["l0_wd"] = cc(asnp(inp["L0_ffn_W_down"]).T)
    d["l0_su_ug"] = cc(asnp(inp["L0_ffn_SU_ug"]).reshape(CD, 128).T)
    d["l0_su_dp"] = cc(asnp(inp["L0_ffn_SU_dp"]).reshape(HC, 128).T)
    d["l0_ws_up"] = cc(asnp(inp["L0_ffn_Wscale_ug"])[:INTER].reshape(HC, 128).T)
    d["l0_ws_g"] = cc(asnp(inp["L0_ffn_Wscale_ug"])[INTER:].reshape(HC, 128).T)
    d["l0_ws_dp"] = cc(asnp(inp["L0_ffn_Wscale_dp"]).reshape(CD, 128).T)

    sl = slice(h0 * 64, (h0 + HPC) * 64)
    d["l1_wqs"] = cc(asnp(inp["L1_attn_Wq_sem"])[sl].T)
    d["l1_wks"] = cc(asnp(inp["L1_attn_Wk_sem"])[sl].T)
    d["l1_wqg"] = cc(asnp(inp["L1_attn_Wq_geo"])[sl].T)
    d["l1_wkg"] = cc(asnp(inp["L1_attn_Wk_geo"])[sl].T)
    slv = slice(h0 * HD, (h0 + HPC) * HD)
    d["l1_wv"] = cc(asnp(inp["L1_attn_Wv"])[slv].T)
    d["l1_wo"] = cc(asnp(inp["L1_attn_Wo"]).T)
    d["l1_wg"] = cc(asnp(inp["L1_ffn_w_gate"]).T)
    d["l1_wu"] = cc(asnp(inp["L1_ffn_w_up"]).T)
    d["l1_wd"] = cc(asnp(inp["L1_ffn_w_down"]).T)

    x = asnp(inp["L0_attn_x"]).reshape(T, D)
    d["x0"] = cc(x[core * TPC:(core + 1) * TPC]).astype(np.float32)
    if _CONSTS is None:
        _CONSTS = host_constants()
    d.update(_CONSTS)
    return d



class Prog:
    def __init__(self, layers=(0, 1), dbg=(), phases=("p0", "p1", "p2", "p3", "p4")):
        self.nc = bass.Bass("TRN2", target_bir_lowering=False)
        self.ctx = contextlib.ExitStack()
        self.dbg = set(dbg)
        self.phases = set(phases)
        self.layers = layers
        self.dram = {}
        self.g = {}
        self.semv = {}

    def sem(self, name):
        if name not in self.semv:
            h = self.ctx.enter_context(self.nc.semaphore(name))
            self.semv[name] = [h, 0]
        return self.semv[name][0]

    def bump(self, name, k=1):
        self.sem(name)
        self.semv[name][1] += k
        return self.semv[name][1]

    def val(self, name):
        self.sem(name)
        return self.semv[name][1]

    def sbuf(self, name, shape, dtype):
        return self.ctx.enter_context(self.nc.sbuf_tensor(name, shape, dtype))

    def psum(self, name, shape, dtype):
        return self.ctx.enter_context(self.nc.psum_tensor(name, shape, dtype))

    def din(self, name, shape, dtype=F32):
        t = self.nc.dram_tensor(name, shape, dtype, kind="ExternalInput")
        self.dram[name] = t
        return t

    def dout(self, name, shape, dtype=F32):
        t = self.nc.dram_tensor(name, shape, dtype, kind="ExternalOutput")
        self.dram[name] = t
        return t

    def dint(self, name, shape, dtype=F32):
        t = self.nc.dram_tensor(name, shape, dtype)
        self.dram[name] = t
        return t

    def barrier(self):
        nc = self.nc
        s = self.sem("bar")
        base = self.semv["bar"][1]
        with nc.Block() as blk:
            @blk.sync
            def _(e):
                e.nop().then_inc(s, 1)
                e.wait_ge(s, base + 5)

            @blk.tensor
            def _(e):
                e.nop().then_inc(s, 1)
                e.wait_ge(s, base + 5)

            @blk.vector
            def _(e):
                e.nop().then_inc(s, 1)
                e.wait_ge(s, base + 5)

            @blk.scalar
            def _(e):
                e.nop().then_inc(s, 1)
                e.wait_ge(s, base + 5)

            @blk.gpsimd
            def _(e):
                e.nop().then_inc(s, 1)
                e.wait_ge(s, base + 5)
        self.semv["bar"][1] = base + 5


def build(layers=(0, 1), dbg=(), phases=("p0", "p1", "p2", "p3", "p4")):
    p = Prog(layers, dbg, phases)
    nc = p.nc

    p.din("l0_wq", [HPC, D, HD]); p.din("l0_wk", [HPC, D, HD]); p.din("l0_wv", [HPC, D, HD])
    p.din("l0_bq", [128, HPC]); p.din("l0_bk", [128, HPC]); p.din("l0_bv", [128, HPC])
    p.din("l0_wo", [D, D]); p.din("l0_bo", [128, CD])
    p.din("l0_wg", [D, INTER]); p.din("l0_wu", [D, INTER]); p.din("l0_wd", [INTER, D])
    p.din("l0_su_ug", [128, CD]); p.din("l0_su_dp", [128, HC])
    p.din("l0_ws_up", [128, HC]); p.din("l0_ws_g", [128, HC]); p.din("l0_ws_dp", [128, CD])
    p.din("l1_wqs", [D, 128]); p.din("l1_wks", [D, 128])
    p.din("l1_wqg", [D, 128]); p.din("l1_wkg", [D, 128])
    p.din("l1_wv", [D, HPC * HD]); p.din("l1_wo", [D, D])
    p.din("l1_wg", [D, INTER]); p.din("l1_wu", [D, INTER]); p.din("l1_wd", [INTER, D])
    p.din("x0", [TPC, D])
    p.din("id_bf16", [128, 128], BF16); p.din("id_f32", [128, 128], F32)
    p.din("ones_1x128", [1, 128], BF16); p.din("ones_128x1", [128, 1], BF16)
    p.din("ones_1x128_f32", [1, 128], F32)
    p.din("maskA", [4, 128, 512]); p.din("maskB", [4, 128, 512])
    p.din("ropecos", [128, T], BF16); p.din("ropesin", [128, T], BF16)
    p.din("h128", [128, 128], BF16); p.din("epsb", [128, 1])
    p.din("mswap", [128, 128], BF16); p.din("mrot", [128, 128], BF16)

    p.dout("y", [TPC, D])
    p.dint("x1", [TPC, D])
    p.dint("ag_in", [D, TPC], BF16)
    p.dint("ag_out", [NCORE, D, TPC], BF16)
    p.dint("a2a_in", [NCORE, HPC * HD, TPC], BF16)
    p.dint("a2a_out", [NCORE, HPC * HD, TPC], BF16)
    p.dint("ffn_out", [D, TPC], F32)
    p.dint("wbf", [3 * D * INTER], BF16)

    if "xn_full" in p.dbg:
        p.dout("dbg_xn_full", [NCORE, D, TPC], BF16)
    if "z" in p.dbg:
        p.dout("dbg_z", [128, CD * TPC], BF16)
    if "ffn_out" in p.dbg:
        p.dout("dbg_ffn_out", [D, TPC], F32)
    if "qkv" in p.dbg:
        p.dout("dbg_q", [128, HPC * T], BF16)
        p.dout("dbg_k", [128, HPC * T], BF16)
        p.dout("dbg_v", [128, (T // 128) * HPC * HD], BF16)
    if "yattn" in p.dbg:
        p.dout("dbg_yattn", [NCORE, HPC * HD, TPC], BF16)
    if "x1" in p.dbg:
        p.dout("dbg_x1", [TPC, D])

    g = p.g
    g["id_bf16"] = p.sbuf("g_id_bf16", [128, 128], BF16)
    g["id_f32"] = p.sbuf("g_id_f32", [128, 128], F32)
    g["ones_1x128"] = p.sbuf("g_ones_1x128", [1, 128], BF16)
    g["ones_128x1"] = p.sbuf("g_ones_128x1", [128, 1], BF16)
    g["ones_1x128_f32"] = p.sbuf("g_ones_1x128_f32", [1, 128], F32)
    g["h128"] = p.sbuf("g_h128", [128, 128], BF16)
    g["epsb"] = p.sbuf("g_epsb", [128, 1], F32)
    g["mswap"] = p.sbuf("g_mswap", [128, 128], BF16)
    g["mrot"] = p.sbuf("g_mrot", [128, 128], BF16)
    g["su_ug"] = p.sbuf("g_su_ug", [128, CD], F32)
    g["su_sup"] = p.sbuf("g_su_sup", [128, HC], F32)
    g["ws_g"] = p.sbuf("g_ws_g", [128, HC], F32)
    g["ws_dp"] = p.sbuf("g_ws_dp", [128, CD], F32)
    g["bq"] = p.sbuf("g_bq", [128, HPC], F32)
    g["bk"] = p.sbuf("g_bk", [128, HPC], F32)
    g["bv"] = p.sbuf("g_bv", [128, HPC], F32)
    g["bo"] = p.sbuf("g_bo", [128, CD], F32)

    with nc.Block() as blk:
        @blk.sync
        def _(e):
            for name in ("id_bf16", "id_f32", "ones_1x128", "ones_128x1",
                         "ones_1x128_f32", "h128", "epsb", "mswap", "mrot"):
                e.dma_start(out=g[name][:], in_=p.dram[name].ap()).then_inc(p.sem("ldc"), 16)
                p.bump("ldc", 16)
            for src, dst in (("l0_su_ug", "su_ug"), ("l0_ws_g", "ws_g"),
                             ("l0_ws_dp", "ws_dp"), ("l0_bq", "bq"),
                             ("l0_bk", "bk"), ("l0_bv", "bv"), ("l0_bo", "bo")):
                e.dma_start(out=g[dst][:], in_=p.dram[src].ap()).then_inc(p.sem("ldc"), 16)
                p.bump("ldc", 16)
            e.dma_start(out=g["su_sup"][:], in_=p.dram["l0_su_dp"].ap()).then_inc(p.sem("ldc"), 16)
            p.bump("ldc", 16)

        @blk.vector
        def _(e):
            e.wait_ge(p.sem("ldc"), p.val("ldc"))
            e.tensor_scalar(g["ws_g"][:], g["ws_g"][:], 1.0 / math.sqrt(D), None, ALU.mult)
            e.tensor_scalar(g["ws_dp"][:], g["ws_dp"][:], 1.0 / math.sqrt(INTER), None, ALU.mult)
            e.nop().then_inc(p.sem("vconst"), 1)
            p.bump("vconst", 1)

    wsu_tmp = p.sbuf("g_wsu_tmp", [128, HC], F32)
    with nc.Block() as blk:
        @blk.sync
        def _(e):
            e.dma_start(out=wsu_tmp[:], in_=p.dram["l0_ws_up"].ap()).then_inc(p.sem("ldc"), 16)
            p.bump("ldc", 16)

        @blk.vector
        def _(e):
            e.wait_ge(p.sem("ldc"), p.val("ldc"))
            e.tensor_tensor(g["su_sup"][:], g["su_sup"][:], wsu_tmp[:], ALU.mult
                            ).then_inc(p.sem("vconst"), 1)
            p.bump("vconst", 1)
            e.wait_ge(p.sem("vconst"), p.val("vconst"))
            e.tensor_scalar(g["su_sup"][:], g["su_sup"][:], 1.0 / math.sqrt(D), None, ALU.mult
                            ).then_inc(p.sem("vconst"), 1)
            p.bump("vconst", 1)

    for li in layers:
        emit_layer(p, li)

    p.ctx.close()
    return p


def emit_layer(p, li):
    last = (li == max(p.layers))
    if li == 0:
        x_in = p.dram["x0"]
    elif 0 not in p.layers:
        x_in = p.din("x1_in", [TPC, D])
    else:
        x_in = p.dram["x1"]
    x_out = p.dram["y"] if last else p.dram["x1"]
    with p.nc.sbuf_tensor(f"z_sb_{li}", [128, CD * TPC], BF16) as z_sb:
        p.g["z_sb"] = z_sb
        if "p0" in p.phases:
            emit_p0(p, li, x_in)
        if "p1" in p.phases:
            emit_p1(p, li)
    if "p2" in p.phases:
        emit_p2p3(p, li)
    if "p4" in p.phases:
        emit_p4(p, li, x_in, x_out)
    emit_final_touch(p, li, x_out)
    p.barrier()



def emit_p0(p, li, x_in):
    nc = p.nc
    g = p.g
    z_sb = p.g["z_sb"]

    with contextlib.ExitStack() as ph:
        xt = [ph.enter_context(nc.sbuf_tensor(f"p0x{i}_{li}", [128, D], F32)) for i in range(2)]
        sq = ph.enter_context(nc.sbuf_tensor(f"p0sq_{li}", [128, D], F32))
        xn = [ph.enter_context(nc.sbuf_tensor(f"p0xn{i}_{li}", [128, D], BF16)) for i in range(2)]
        st = ph.enter_context(nc.sbuf_tensor(f"p0st_{li}", [128, 16], F32))
        if li == 0:
            stage = [ph.enter_context(nc.sbuf_tensor(f"p0stage{i}_{li}", [128, CD * 512], BF16))
                     for i in range(2)]
            zb = [ph.enter_context(nc.sbuf_tensor(f"p0zb{i}_{li}", [128, CD * 512], BF16))
                  for i in range(2)]
            psz = [ph.enter_context(nc.psum_tensor(f"p0pz{i}_{li}", [128, 512], F32))
                   for i in range(2)]
        pst = [ph.enter_context(nc.psum_tensor(f"p0pt{i}_{li}", [128, 128], BF16))
               for i in range(4)]

        sld = f"p0ld_{li}"
        sdve = f"p0dve_{li}"
        sact = f"p0act_{li}"
        spe = f"p0pe_{li}"
        sst = f"p0sst_{li}"
        szdve = f"p0zdve_{li}"
        szpe = f"p0zpe_{li}"
        scc = f"cc_ag_{li}"


        def act_n(i, k):
            return 2 * i + k

        def dve_stats_n(i, k):
            return 23 * i + k

        def dve_n(i):
            return 23 * i + 23

        def pe_n(i, j):
            return 16 * i + j

        base_ld = p.val(sld)
        base_act = p.val(sact)
        base_dve = p.val(sdve)
        base_pe = p.val(spe)
        base_st = p.val(sst)
        base_zdve = p.val(szdve)
        base_zpe = p.val(szpe)
        vconst_target = p.val("vconst")

        nbt = NTT // 4

        with nc.Block() as blk:
            @blk.sync
            def _(e):
                e.wait_ge(p.sem("ldc"), p.val("ldc"))
                for i in range(NTT):
                    if i >= 2:
                        e.wait_ge(p.sem(sdve), base_dve + dve_stats_n(i - 2, 7))
                    e.dma_start(out=xt[i % 2][:],
                                in_=x_in.ap()[i * 128:(i + 1) * 128, :]
                                ).then_inc(p.sem(sld), 16)
                    p.bump(sld, 16)

            @blk.scalar
            def _(e):
                for i in range(NTT):
                    e.wait_ge(p.sem(sld), base_ld + 16 * (i + 1))
                    e.activation(sq[:], xt[i % 2][:], AF.Square,
                                 accum_out=st[:, 8 * (i % 2) + 1:8 * (i % 2) + 2]
                                 ).then_inc(p.sem(sact), 1)
                    p.bump(sact, 1)
                    e.wait_ge(p.sem(sdve), base_dve + dve_stats_n(i, 4))
                    e.activation(st[:, 8 * (i % 2) + 5:8 * (i % 2) + 6],
                                 st[:, 8 * (i % 2) + 4:8 * (i % 2) + 5],
                                 AF.Sqrt, bias=g["epsb"][:]).then_inc(p.sem(sact), 1)
                    p.bump(sact, 1)

            @blk.vector
            def _(e):
                e.wait_ge(p.sem("vconst"), vconst_target)
                for i in range(NTT):
                    o = 8 * (i % 2)
                    e.wait_ge(p.sem(sld), base_ld + 16 * (i + 1))
                    e.tensor_reduce(st[:, o:o + 1], xt[i % 2][:], AX.X, ALU.add
                                    ).then_inc(p.sem(sdve), 1)
                    p.bump(sdve, 1)
                    e.wait_ge(p.sem(sdve), base_dve + dve_stats_n(i, 1))
                    e.tensor_scalar(st[:, o + 2:o + 3], st[:, o:o + 1],
                                    1.0 / D, None, ALU.mult).then_inc(p.sem(sdve), 1)
                    p.bump(sdve, 1)
                    e.wait_ge(p.sem(sdve), base_dve + dve_stats_n(i, 2))
                    e.tensor_tensor(st[:, o + 3:o + 4], st[:, o + 2:o + 3],
                                    st[:, o + 2:o + 3], ALU.mult).then_inc(p.sem(sdve), 1)
                    p.bump(sdve, 1)
                    e.wait_ge(p.sem(sact), base_act + act_n(i, 1))
                    e.wait_ge(p.sem(sdve), base_dve + dve_stats_n(i, 3))
                    e.scalar_tensor_tensor(st[:, o + 4:o + 5], st[:, o + 1:o + 2],
                                           1.0 / D, st[:, o + 3:o + 4],
                                           ALU.mult, ALU.subtract).then_inc(p.sem(sdve), 1)
                    p.bump(sdve, 1)
                    e.wait_ge(p.sem(sact), base_act + act_n(i, 2))
                    e.reciprocal(st[:, o + 6:o + 7], st[:, o + 5:o + 6]
                                 ).then_inc(p.sem(sdve), 1)
                    p.bump(sdve, 1)
                    e.wait_ge(p.sem(sdve), base_dve + dve_stats_n(i, 5))
                    e.tensor_scalar(st[:, o + 7:o + 8], st[:, o + 2:o + 3],
                                    st[:, o + 6:o + 7], -1.0, ALU.mult, ALU.mult
                                    ).then_inc(p.sem(sdve), 1)
                    p.bump(sdve, 1)
                    e.wait_ge(p.sem(sdve), base_dve + dve_stats_n(i, 6))
                    e.tensor_scalar(xn[i % 2][:], xt[i % 2][:],
                                    st[:, o + 6:o + 7], st[:, o + 7:o + 8],
                                    ALU.mult, ALU.add).then_inc(p.sem(sdve), 1)
                    p.bump(sdve, 1)
                    for j in range(CD):
                        e.wait_ge(p.sem(spe), base_pe + pe_n(i, j + 1))
                        if li == 0:
                            dst = stage[i // 4][:, j * 512 + (i % 4) * 128:
                                               j * 512 + (i % 4) * 128 + 128]
                        else:
                            dst = z_sb[:, j * TPC + i * 128: j * TPC + i * 128 + 128]
                        e.tensor_copy(dst, pst[j % 4][:]).then_inc(p.sem(sdve), 1)
                        p.bump(sdve, 1)

            @blk.tensor
            def _(e):
                e.wait_ge(p.sem("ldc"), p.val("ldc"))
                for i in range(NTT):
                    e.wait_ge(p.sem(sdve), base_dve + dve_stats_n(i, 7))
                    for j in range(CD):
                        n = 16 * i + j
                        if n >= 4:
                            ip, jp = (n - 4) // 16, (n - 4) % 16
                            e.wait_ge(p.sem(sdve), base_dve + 23 * ip + 7 + jp + 1)
                        e.transpose(pst[j % 4][:], xn[i % 2][:, j * 128:(j + 1) * 128],
                                    g["id_bf16"][:]).then_inc(p.sem(spe), 1)
                        p.bump(spe, 1)

            @blk.gpsimd
            def _(e):
                for bidx in range(nbt):
                    e.wait_ge(p.sem(sdve), base_dve + dve_n(4 * bidx + 3))
                    for j in range(CD):
                        if li == 0:
                            sap = stage[bidx][:, j * 512:(j + 1) * 512]
                        else:
                            sap = z_sb[:, j * TPC + bidx * 512: j * TPC + bidx * 512 + 512]
                        dst = p.dram["ag_in"].ap().rearrange(
                            "(c p) t -> c p t", p=128)[j, :, bidx * 512:(bidx + 1) * 512]
                        e.dma_start(out=dst, in_=sap).then_inc(p.sem(sst), 16)
                        p.bump(sst, 16)
                e.wait_ge(p.sem(sst), base_st + 16 * CD * nbt)
                e.collective_compute(
                    "AllGather", ALU.bypass,
                    replica_groups=[list(range(NCORE))],
                    ins=[p.dram["ag_in"].ap().opt()],
                    outs=[p.dram["ag_out"].ap().rearrange("r c t -> (r c) t").opt()],
                ).then_inc(p.sem(scc), 1)
                p.bump(scc, 1)
                if "xn_full" in p.dbg and li == 0:
                    e.wait_ge(p.sem(scc), 1)
                    e.dma_start(out=p.dram["dbg_xn_full"].ap().rearrange("r c t -> (r c) t"),
                                in_=p.dram["ag_out"].ap().rearrange("r c t -> (r c) t")
                                ).then_inc(p.sem(sst), 16)
                    p.bump(sst, 16)

        if li == 0:
            with nc.Block() as blk:
                @blk.vector
                def _(e):
                    for bidx in range(nbt):
                        e.wait_ge(p.sem(sdve), base_dve + dve_n(4 * bidx + 3))
                        if bidx >= 1:
                            e.wait_ge(p.sem(szpe), base_zpe + bidx * CD)
                        for c in range(CD):
                            ins = e.tensor_scalar(zb[0][:, c * 512:(c + 1) * 512],
                                                  stage[bidx][:, c * 512:(c + 1) * 512],
                                                  g["su_ug"][:, c:c + 1], None, ALU.mult)
                            if c == CD - 1:
                                ins.then_inc(p.sem(szdve), 1)
                                p.bump(szdve, 1)
                        e.wait_ge(p.sem(szdve), base_zdve + bidx * 21 + 1)
                        cur, nxt = 0, 1
                        for si, hstep in enumerate((1, 2, 4, 8)):
                            for c in range(CD):
                                pr = c ^ hstep
                                ca, cb = (c, pr) if (c & hstep) == 0 else (pr, c)
                                op = ALU.add if (c & hstep) == 0 else ALU.subtract
                                ins = e.tensor_tensor(
                                    zb[nxt][:, c * 512:(c + 1) * 512],
                                    zb[cur][:, ca * 512:(ca + 1) * 512],
                                    zb[cur][:, cb * 512:(cb + 1) * 512],
                                    op)
                                if c == CD - 1:
                                    ins.then_inc(p.sem(szdve), 1)
                                    p.bump(szdve, 1)
                            e.wait_ge(p.sem(szdve), base_zdve + bidx * 21 + 2 + si)
                            cur, nxt = nxt, cur
                        for c in range(CD):
                            e.wait_ge(p.sem(szpe), base_zpe + bidx * CD + c + 1)
                            e.tensor_copy(
                                z_sb[:, c * TPC + bidx * 512: c * TPC + bidx * 512 + 512],
                                psz[c % 2][:]).then_inc(p.sem(szdve), 1)
                            p.bump(szdve, 1)

                @blk.tensor
                def _(e):
                    for bidx in range(nbt):
                        e.wait_ge(p.sem(szdve), base_zdve + bidx * 21 + 5)
                        for c in range(CD):
                            if c >= 2:
                                e.wait_ge(p.sem(szdve), base_zdve + bidx * 21 + 5 + (c - 2) + 1)
                            e.matmul(psz[c % 2][:], g["h128"][:],
                                     zb[0][:, c * 512:(c + 1) * 512],
                                     start=True, stop=True, skip_group_check=True
                                     ).then_inc(p.sem(szpe), 1)
                            p.bump(szpe, 1)

        if "z" in p.dbg and li == 0:
            with nc.Block() as blk:
                @blk.gpsimd
                def _(e):
                    e.wait_ge(p.sem(szdve), p.val(szdve))
                    e.dma_start(out=p.dram["dbg_z"].ap(), in_=z_sb[:]).then_inc(
                        p.sem(sst), 16)
                    p.bump(sst, 16)
                    e.wait_ge(p.sem(sst), p.val(sst))

    p.barrier()



OFF_U = D * INTER
OFF_D = 2 * D * INTER


def emit_p1(p, li):
    nc = p.nc
    g = p.g
    z_sb = g["z_sb"]
    pre = f"l{li}_"
    wg_d, wu_d, wd_d = p.dram[pre + "wg"], p.dram[pre + "wu"], p.dram[pre + "wd"]
    wbf = p.dram["wbf"]

    for tb in (0, 1):
        pass0 = (tb == 0)
        with nc.sbuf_tensor(f"h_sb_{li}_{tb}", [128, HC * 512], BF16) as h_sb:
            p.g["h_sb"] = h_sb
            emit_p1_gateup(p, li, tb, pass0)
            p.barrier()
            if li == 0:
                emit_p1_fwht2(p, li, tb)
                p.barrier()
            emit_p1_down(p, li, tb, pass0)
            p.barrier()

    if "ffn_out" in p.dbg:
        with nc.Block() as blk:
            @blk.gpsimd
            def _(e):
                s = p.sem(f"p1dbg_{li}")
                e.dma_start(out=p.dram["dbg_ffn_out"].ap(),
                            in_=p.dram["ffn_out"].ap()).then_inc(s, 16)
                p.bump(f"p1dbg_{li}", 16)
                e.wait_ge(s, p.val(f"p1dbg_{li}"))


def emit_p1_gateup(p, li, tb, pass0):
    nc = p.nc
    g = p.g
    z_sb = g["z_sb"]
    pre = f"l{li}_"
    wg_d, wu_d = p.dram[pre + "wg"], p.dram[pre + "wu"]
    wbf = p.dram["wbf"]

    sld = f"p1ld_{li}_{tb}"
    sldu = f"p1ldu_{li}_{tb}"
    sdve = f"p1dve_{li}_{tb}"
    sact = f"p1act_{li}_{tb}"
    spe = f"p1pe_{li}_{tb}"
    swb = f"p1wb_{li}_{tb}"
    b_ld, b_ldu, b_dve, b_act, b_pe, b_wb = (p.val(sld), p.val(sldu), p.val(sdve),
                                             p.val(sact), p.val(spe), p.val(swb))
    vconst_t = p.val("vconst")
    vo = 2 if pass0 else 1

    h_sb = p.g["h_sb"]
    with contextlib.ExitStack() as ph:
        wstage = [ph.enter_context(nc.sbuf_tensor(f"p1ws{i}_{li}_{tb}", [128, 4096], F32))
                  for i in range(2)] if pass0 else None
        wb = [ph.enter_context(nc.sbuf_tensor(f"p1wb{i}_{li}_{tb}", [128, 4096], BF16))
              for i in range(2)]
        sg = [ph.enter_context(nc.sbuf_tensor(f"p1sg{i}_{li}_{tb}", [128, 512], BF16))
              for i in range(2)]
        psG = [ph.enter_context(nc.psum_tensor(f"p1pg{i}_{li}_{tb}", [128, 512], F32))
               for i in range(2)]
        psU = [ph.enter_context(nc.psum_tensor(f"p1pu{i}_{li}_{tb}", [128, 512], F32))
               for i in range(2)]

        with nc.Block() as blk:
            @blk.sync
            def _(e):
                for hc in range(HC):
                    if pass0:
                        if hc >= 2:
                            e.wait_ge(p.sem(sdve), b_dve + vo * (hc - 2) + 1)
                        e.dma_start(
                            out=wstage[hc % 2][:, 0:2048],
                            in_=wg_d.ap().rearrange("(c p) h -> p c h", p=128)
                            [:, :, hc * 128:(hc + 1) * 128]
                        ).then_inc(p.sem(sld), 16)
                        p.bump(sld, 16)
                    else:
                        if hc >= 2:
                            e.wait_ge(p.sem(spe), b_pe + 32 * (hc - 1))
                        for half, off in ((0, hc * 262144), (1, OFF_U + hc * 262144)):
                            e.dma_start(
                                out=wb[hc % 2][:, half * 2048:(half + 1) * 2048],
                                in_=wbf.ap()[off:off + 262144].rearrange(
                                    "(q w) -> q w", q=128)
                            ).then_inc(p.sem(sld), 16)
                            p.bump(sld, 16)

            @blk.vector
            def _(e):
                e.wait_ge(p.sem("vconst"), vconst_t)
                for hc in range(HC):
                    if pass0:
                        e.wait_ge(p.sem(sld), b_ld + 16 * (hc + 1))
                        e.wait_ge(p.sem(sldu), b_ldu + 16 * (hc + 1))
                        if hc >= 2:
                            e.wait_ge(p.sem(spe), b_pe + 32 * (hc - 1))
                            e.wait_ge(p.sem(swb), b_wb + 32 * (hc - 1))
                        e.tensor_copy(wb[hc % 2][:], wstage[hc % 2][:]
                                      ).then_inc(p.sem(sdve), 1)
                        p.bump(sdve, 1)
                    e.wait_ge(p.sem(sact), b_act + hc + 1)
                    e.wait_ge(p.sem(spe), b_pe + 32 * (hc + 1))
                    dst = h_sb[:, hc * 512:(hc + 1) * 512]
                    if li == 0:
                        e.scalar_tensor_tensor(dst, sg[hc % 2][:],
                                               g["su_sup"][:, hc:hc + 1],
                                               psU[hc % 2][:],
                                               ALU.mult, ALU.mult
                                               ).then_inc(p.sem(sdve), 1)
                    else:
                        e.tensor_tensor(dst, sg[hc % 2][:], psU[hc % 2][:],
                                        ALU.mult).then_inc(p.sem(sdve), 1)
                    p.bump(sdve, 1)

            @blk.scalar
            def _(e):
                e.wait_ge(p.sem("vconst"), vconst_t)
                for hc in range(HC):
                    if pass0:
                        if hc >= 2:
                            e.wait_ge(p.sem(sdve), b_dve + vo * (hc - 2) + 1)
                        e.dma_start(
                            out=wstage[hc % 2][:, 2048:4096],
                            in_=wu_d.ap().rearrange("(c p) h -> p c h", p=128)
                            [:, :, hc * 128:(hc + 1) * 128]
                        ).then_inc(p.sem(sldu), 16)
                        p.bump(sldu, 16)
                    e.wait_ge(p.sem(spe), b_pe + 32 * hc + 16)
                    if hc >= 2:
                        e.wait_ge(p.sem(sdve), b_dve + vo * (hc - 2) + vo)
                    if li == 0:
                        e.activation(sg[hc % 2][:], psG[hc % 2][:], AF.Silu,
                                     scale=g["ws_g"][:, hc:hc + 1]
                                     ).then_inc(p.sem(sact), 1)
                    else:
                        e.activation(sg[hc % 2][:], psG[hc % 2][:], AF.Silu
                                     ).then_inc(p.sem(sact), 1)
                    p.bump(sact, 1)

            @blk.tensor
            def _(e):
                for hc in range(HC):
                    if pass0:
                        e.wait_ge(p.sem(sdve), b_dve + vo * hc + 1)
                    else:
                        e.wait_ge(p.sem(sld), b_ld + 32 * (hc + 1))
                    if hc >= 2:
                        e.wait_ge(p.sem(sact), b_act + hc - 1)
                        e.wait_ge(p.sem(sdve), b_dve + vo * (hc - 2) + vo)
                    for cc in range(CD):
                        e.matmul(psG[hc % 2][:], wb[hc % 2][:, cc * 128:(cc + 1) * 128],
                                 z_sb[:, cc * TPC + tb * 512: cc * TPC + tb * 512 + 512],
                                 start=(cc == 0), stop=(cc == CD - 1),
                                 skip_group_check=True).then_inc(p.sem(spe), 1)
                        p.bump(spe, 1)
                    for cc in range(CD):
                        e.matmul(psU[hc % 2][:],
                                 wb[hc % 2][:, 2048 + cc * 128: 2048 + (cc + 1) * 128],
                                 z_sb[:, cc * TPC + tb * 512: cc * TPC + tb * 512 + 512],
                                 start=(cc == 0), stop=(cc == CD - 1),
                                 skip_group_check=True).then_inc(p.sem(spe), 1)
                        p.bump(spe, 1)

            @blk.gpsimd
            def _(e):
                if pass0:
                    for hc in range(HC):
                        e.wait_ge(p.sem(sdve), b_dve + vo * hc + 1)
                        for half, off in ((0, hc * 262144), (1, OFF_U + hc * 262144)):
                            e.dma_start(
                                out=wbf.ap()[off:off + 262144].rearrange(
                                    "(q w) -> q w", q=128),
                                in_=wb[hc % 2][:, half * 2048:(half + 1) * 2048]
                            ).then_inc(p.sem(swb), 16)
                            p.bump(swb, 16)


def emit_p1_fwht2(p, li, tb):
    nc = p.nc
    g = p.g
    h_sb = p.g["h_sb"]
    sdve = f"p1fdve_{li}_{tb}"
    spe = f"p1fpe_{li}_{tb}"
    b_dve, b_pe = p.val(sdve), p.val(spe)

    with contextlib.ExitStack() as ph:
        h2 = ph.enter_context(nc.sbuf_tensor(f"p1h2_{li}_{tb}", [128, HC * 256], BF16))
        psZ2 = [ph.enter_context(nc.psum_tensor(f"p1pz{i}_{li}_{tb}", [128, 512], F32))
                for i in range(2)]

        nst = 0
        with nc.Block() as blk:
            @blk.vector
            def _(e):
                cnt = 0
                for sub in range(2):
                    for si, hstep in enumerate((1, 2, 4, 8, 16, 32)):
                        src_is_h = (si % 2 == 0)
                        for c in range(HC):
                            pr = c ^ hstep
                            ca, cb = (c, pr) if (c & hstep) == 0 else (pr, c)
                            op = ALU.add if (c & hstep) == 0 else ALU.subtract
                            if src_is_h:
                                sa = h_sb[:, ca * 512 + sub * 256: ca * 512 + sub * 256 + 256]
                                sb_ = h_sb[:, cb * 512 + sub * 256: cb * 512 + sub * 256 + 256]
                                dst = h2[:, c * 256:(c + 1) * 256]
                            else:
                                sa = h2[:, ca * 256:(ca + 1) * 256]
                                sb_ = h2[:, cb * 256:(cb + 1) * 256]
                                dst = h_sb[:, c * 512 + sub * 256: c * 512 + sub * 256 + 256]
                            ins = e.tensor_tensor(dst, sa, sb_, op)
                            if c == HC - 1:
                                ins.then_inc(p.sem(sdve), 1)
                                p.bump(sdve, 1)
                                cnt += 1
                                e.wait_ge(p.sem(sdve), b_dve + cnt)
                nevac = 0
                for hc in range(HC):
                    e.wait_ge(p.sem(spe), b_pe + hc + 1)
                    e.tensor_copy(h_sb[:, hc * 512:(hc + 1) * 512],
                                  psZ2[hc % 2][:]).then_inc(p.sem(sdve), 1)
                    p.bump(sdve, 1)

            @blk.tensor
            def _(e):
                e.wait_ge(p.sem(sdve), b_dve + 12)
                for hc in range(HC):
                    if hc >= 2:
                        e.wait_ge(p.sem(sdve), b_dve + 12 + (hc - 2) + 1)
                    e.matmul(psZ2[hc % 2][:], g["h128"][:],
                             h_sb[:, hc * 512:(hc + 1) * 512],
                             start=True, stop=True, skip_group_check=True
                             ).then_inc(p.sem(spe), 1)
                    p.bump(spe, 1)


def emit_p1_down(p, li, tb, pass0):
    nc = p.nc
    g = p.g
    h_sb = p.g["h_sb"]
    pre = f"l{li}_"
    wd_d = p.dram[pre + "wd"]
    wbf = p.dram["wbf"]

    sld = f"p1dld_{li}_{tb}"
    sdve = f"p1ddve_{li}_{tb}"
    spe = f"p1dpe_{li}_{tb}"
    swb = f"p1dwb_{li}_{tb}"
    sst = f"p1dst_{li}_{tb}"
    b_ld, b_dve, b_pe, b_wb, b_st = (p.val(sld), p.val(sdve), p.val(spe),
                                     p.val(swb), p.val(sst))
    vconst_t = p.val("vconst")
    vo = 5 if pass0 else 1

    with contextlib.ExitStack() as ph:
        wdb = [ph.enter_context(nc.sbuf_tensor(f"p1wd{i}_{li}_{tb}", [128, 8192], BF16))
               for i in range(2)]
        wstage = ph.enter_context(nc.sbuf_tensor(f"p1dws_{li}_{tb}", [128, 2048], F32)) \
            if pass0 else None
        dn = [ph.enter_context(nc.sbuf_tensor(f"p1dn{i}_{li}_{tb}", [128, 512], F32))
              for i in range(2)]
        psD = [ph.enter_context(nc.psum_tensor(f"p1pd{i}_{li}_{tb}", [128, 512], F32))
               for i in range(2)]

        with nc.Block() as blk:
            @blk.sync
            def _(e):
                for oc in range(CD):
                    if pass0:
                        for grp in range(4):
                            n = 4 * oc + grp
                            if n >= 1:
                                e.wait_ge(p.sem(sdve), b_dve + vo * ((n - 1) // 4)
                                          + ((n - 1) % 4) + 1)
                            e.dma_start(
                                out=wstage[:],
                                in_=wd_d.ap().rearrange("(s q) o -> q s o", q=128)
                                [:, 16 * grp:16 * (grp + 1), oc * 128:(oc + 1) * 128]
                            ).then_inc(p.sem(sld), 16)
                            p.bump(sld, 16)
                    else:
                        if oc >= 2:
                            e.wait_ge(p.sem(spe), b_pe + 64 * (oc - 1))
                        off = OFF_D + oc * 1048576
                        e.dma_start(
                            out=wdb[oc % 2][:],
                            in_=wbf.ap()[off:off + 1048576].rearrange(
                                "(q w) -> q w", q=128)
                        ).then_inc(p.sem(sld), 16)
                        p.bump(sld, 16)

            @blk.vector
            def _(e):
                e.wait_ge(p.sem("vconst"), vconst_t)
                for oc in range(CD):
                    if pass0:
                        for grp in range(4):
                            n = 4 * oc + grp
                            e.wait_ge(p.sem(sld), b_ld + 16 * (n + 1))
                            if oc >= 2 and grp == 0:
                                e.wait_ge(p.sem(spe), b_pe + 64 * (oc - 1))
                                e.wait_ge(p.sem(swb), b_wb + 16 * (oc - 1))
                            e.tensor_copy(wdb[oc % 2][:, grp * 2048:(grp + 1) * 2048],
                                          wstage[:]).then_inc(p.sem(sdve), 1)
                            p.bump(sdve, 1)
                    e.wait_ge(p.sem(spe), b_pe + 64 * (oc + 1))
                    if oc >= 2:
                        e.wait_ge(p.sem(sst), b_st + 16 * (oc - 1))
                    if li == 0:
                        e.tensor_scalar(dn[oc % 2][:], psD[oc % 2][:],
                                        g["ws_dp"][:, oc:oc + 1], None, ALU.mult
                                        ).then_inc(p.sem(sdve), 1)
                    else:
                        e.tensor_copy(dn[oc % 2][:], psD[oc % 2][:]
                                      ).then_inc(p.sem(sdve), 1)
                    p.bump(sdve, 1)

            @blk.tensor
            def _(e):
                for oc in range(CD):
                    if oc >= 2:
                        e.wait_ge(p.sem(sdve), b_dve + vo * (oc - 2) + vo)
                    for sc in range(HC):
                        if pass0:
                            grp = sc // 16
                            e.wait_ge(p.sem(sdve), b_dve + vo * oc + grp + 1)
                        elif sc == 0:
                            e.wait_ge(p.sem(sld), b_ld + 16 * (oc + 1))
                        e.matmul(psD[oc % 2][:], wdb[oc % 2][:, sc * 128:(sc + 1) * 128],
                                 h_sb[:, sc * 512:(sc + 1) * 512],
                                 start=(sc == 0), stop=(sc == HC - 1),
                                 skip_group_check=True).then_inc(p.sem(spe), 1)
                        p.bump(spe, 1)

            @blk.gpsimd
            def _(e):
                for oc in range(CD):
                    if pass0:
                        e.wait_ge(p.sem(sdve), b_dve + vo * oc + 4)
                        off = OFF_D + oc * 1048576
                        e.dma_start(
                            out=wbf.ap()[off:off + 1048576].rearrange(
                                "(q w) -> q w", q=128),
                            in_=wdb[oc % 2][:]).then_inc(p.sem(swb), 16)
                        p.bump(swb, 16)
                    e.wait_ge(p.sem(sdve), b_dve + vo * oc + vo)
                    e.dma_start(
                        out=p.dram["ffn_out"].ap().rearrange("(c q) t -> c q t", q=128)
                        [oc, :, tb * 512:(tb + 1) * 512],
                        in_=dn[oc % 2][:]).then_inc(p.sem(sst), 16)
                    p.bump(sst, 16)
                e.wait_ge(p.sem(sst), b_st + 16 * CD)



def emit_final_touch(p, li, x_out):
    if "p4" in p.phases:
        return
    nc = p.nc
    with nc.Block() as blk:
        @blk.gpsimd
        def _(e):
            s = p.sem(f"ft_{li}")
            e.dma_start(out=x_out.ap(), in_=p.dram["x0"].ap()).then_inc(s, 16)
            p.bump(f"ft_{li}", 16)
            e.wait_ge(s, p.val(f"ft_{li}"))


def emit_p2p3(p, li):
    nc = p.nc
    g = p.g
    pre = f"l{li}_"

    with contextlib.ExitStack() as ph:
        Q_sb = ph.enter_context(nc.sbuf_tensor(f"q_sb_{li}", [128, HPC * T], BF16))
        K_sb = ph.enter_context(nc.sbuf_tensor(f"k_sb_{li}", [128, HPC * T], BF16))
        V_tm = ph.enter_context(nc.sbuf_tensor(f"v_tm_{li}", [128, (T // 128) * HPC * HD], BF16))
        maskA = ph.enter_context(nc.sbuf_tensor(f"mA_{li}", [128, 4 * 512], F32))
        maskB = ph.enter_context(nc.sbuf_tensor(f"mB_{li}", [128, 4 * 512], F32))
        emit_p2(p, li, Q_sb, K_sb, V_tm, maskA, maskB)
        p.barrier()
        if "p3" in p.phases:
            emit_p3(p, li, Q_sb, K_sb, V_tm, maskA, maskB)
            p.barrier()


def emit_p2(p, li, Q_sb, K_sb, V_tm, maskA, maskB):
    nc = p.nc
    g = p.g
    pre = f"l{li}_"

    if li == 0:
        wsets = [(p.dram["l0_wq"], 0), (p.dram["l0_wq"], 1),
                 (p.dram["l0_wk"], 0), (p.dram["l0_wk"], 1),
                 (p.dram["l0_wv"], 0), (p.dram["l0_wv"], 1)]
    else:
        wsets = [(p.dram["l1_wqg"], None), (p.dram["l1_wqs"], None),
                 (p.dram["l1_wkg"], None), (p.dram["l1_wks"], None),
                 (p.dram["l1_wv"], 0), (p.dram["l1_wv"], 1)]

    sld = f"p2ld_{li}"
    swl = f"p2wl_{li}"
    sdve = f"p2dve_{li}"
    sact = f"p2act_{li}"
    spe = f"p2pe_{li}"
    b_ld, b_wl, b_dve, b_act, b_pe = (p.val(sld), p.val(swl), p.val(sdve),
                                      p.val(sact), p.val(spe))
    scc = f"cc_ag_{li}"
    cc_t = p.val(scc)
    vconst_t = p.val("vconst")

    if li == 0:
        DVE_TB = 14
        PE_TB = 96 + 8
    else:
        DVE_TB = 2 * 7 + 2 * 3 + 2 * 5
        PE_TB = 96 + 8 + 6

    with contextlib.ExitStack() as ph:
        qkvw = ph.enter_context(nc.sbuf_tensor(f"p2w_{li}", [128, 6 * CD * 128], BF16))
        wstage = ph.enter_context(nc.sbuf_tensor(f"p2ws_{li}", [128, CD * 128], F32))
        xnb = [ph.enter_context(nc.sbuf_tensor(f"p2xn{i}_{li}", [128, CD * 512], BF16))
               for i in range(2)]
        vtmp = ph.enter_context(nc.sbuf_tensor(f"p2vt_{li}", [128, 512], BF16))
        gtmp = ph.enter_context(nc.sbuf_tensor(f"p2gt_{li}", [128, 512], BF16))
        swtmp = ph.enter_context(nc.sbuf_tensor(f"p2sw_{li}", [128, 512], BF16))
        t1tmp = ph.enter_context(nc.sbuf_tensor(f"p2t1_{li}", [128, 512], BF16))
        t2tmp = ph.enter_context(nc.sbuf_tensor(f"p2t2_{li}", [128, 512], BF16))
        roptmp = ph.enter_context(nc.sbuf_tensor(f"p2rp_{li}", [128, 512], BF16))
        ctab = [ph.enter_context(nc.sbuf_tensor(f"p2ct{i}_{li}", [128, 512], BF16))
                for i in range(2)]
        stab = [ph.enter_context(nc.sbuf_tensor(f"p2st{i}_{li}", [128, 512], BF16))
                for i in range(2)]
        psQ = [ph.enter_context(nc.psum_tensor(f"p2pq{i}_{li}", [128, 512], F32))
               for i in range(2)]
        psT = [ph.enter_context(nc.psum_tensor(f"p2pt{i}_{li}", [128, 128], BF16))
               for i in range(2)]
        psR = ph.enter_context(nc.psum_tensor(f"p2pr_{li}", [128, 512], F32)) \
            if li == 1 else None

        with nc.Block() as blk:
            @blk.sync
            def _(e):
                for m in range(4):
                    e.dma_start(out=maskA[:, m * 512:(m + 1) * 512],
                                in_=p.dram["maskA"].ap()[m]).then_inc(p.sem(swl), 16)
                    p.bump(swl, 16)
                    e.dma_start(out=maskB[:, m * 512:(m + 1) * 512],
                                in_=p.dram["maskB"].ap()[m]).then_inc(p.sem(swl), 16)
                    p.bump(swl, 16)
                for s, (wd, hsel) in enumerate(wsets):
                    if s >= 1:
                        e.wait_ge(p.sem(sdve), b_dve + s)
                    if li == 0:
                        src = wd.ap()[hsel].rearrange("(c q) d -> q c d", q=128)
                    else:
                        if hsel is None:
                            src = wd.ap().rearrange("(c q) d -> q c d", q=128)
                        else:
                            src = wd.ap().rearrange("(c q) d -> q c d", q=128)[
                                :, :, hsel * 128:(hsel + 1) * 128]
                    e.dma_start(out=wstage[:], in_=src).then_inc(p.sem(swl), 16)
                    p.bump(swl, 16)
                e.wait_ge(p.sem(scc), cc_t)
                for tb in range(NTB):
                    if tb >= 2:
                        e.wait_ge(p.sem(spe), b_pe + PE_TB * (tb - 1))
                    r, half = tb // 2, tb % 2
                    e.dma_start(
                        out=xnb[tb % 2][:],
                        in_=p.dram["ag_out"].ap().rearrange(
                            "r (c q) t -> q (r c) t", q=128)
                        [:, r * CD:(r + 1) * CD, half * 512:(half + 1) * 512]
                    ).then_inc(p.sem(sld), 16)
                    p.bump(sld, 16)
                    if li == 1:
                        e.dma_start(out=ctab[tb % 2][:],
                                    in_=p.dram["ropecos"].ap()[:, tb * 512:(tb + 1) * 512]
                                    ).then_inc(p.sem(sld), 16)
                        p.bump(sld, 16)
                        e.dma_start(out=stab[tb % 2][:],
                                    in_=p.dram["ropesin"].ap()[:, tb * 512:(tb + 1) * 512]
                                    ).then_inc(p.sem(sld), 16)
                        p.bump(sld, 16)

            @blk.vector
            def _(e):
                e.wait_ge(p.sem("vconst"), vconst_t)
                for s in range(6):
                    e.wait_ge(p.sem(swl), b_wl + 128 + 16 * (s + 1))
                    e.tensor_copy(qkvw[:, s * 2048:(s + 1) * 2048], wstage[:]
                                  ).then_inc(p.sem(sdve), 1)
                    p.bump(sdve, 1)
                ld_per_tb = 3 if li == 1 else 1
                for tb in range(NTB):
                    dvetb = b_dve + 6 + DVE_TB * tb
                    n = 0

                    def bump1(ins):
                        nonlocal n
                        n += 1
                        ins.then_inc(p.sem(sdve), 1)
                        p.bump(sdve, 1)

                    def wself():
                        e.wait_ge(p.sem(sdve), dvetb + n)

                    col = lambda h: h * T + tb * 512
                    if li == 0:
                        for s in range(6):
                            e.wait_ge(p.sem(spe), b_pe + PE_TB * tb + 16 * (s + 1))
                            ps = psQ[s % 2][:]
                            if s < 2:
                                dst = Q_sb[:, col(s):col(s) + 512]
                                bias = g["bq"][:, s:s + 1]
                            elif s < 4:
                                dst = K_sb[:, col(s - 2):col(s - 2) + 512]
                                bias = g["bk"][:, s - 2:s - 2 + 1]
                            else:
                                dst = vtmp[:]
                                bias = g["bv"][:, s - 4:s - 4 + 1]
                            bump1(e.tensor_scalar(dst, ps, bias, None, ALU.add))
                            if s >= 4:
                                wself()
                                h = s - 4
                                for sub in range(4):
                                    kc = tb * 4 + sub
                                    e.wait_ge(p.sem(spe),
                                              b_pe + PE_TB * tb + 96 + 4 * h + sub + 1)
                                    bump1(e.tensor_copy(
                                        V_tm[:, kc * 256 + h * 128: kc * 256 + h * 128 + 128],
                                        psT[sub % 2][:]))
                    else:
                        pe_off = {0: 16, 1: 34, 2: 51, 3: 69, 4: 86, 5: 102}
                        pe_extra = {0: (17, 18), 2: (52, 53)}
                        pe_rot = {1: 35, 3: 70}
                        for s in range(6):
                            e.wait_ge(p.sem(spe), b_pe + PE_TB * tb + pe_off[s])
                            ps = psQ[s % 2]
                            dstT = Q_sb if s < 2 else (K_sb if s < 4 else None)
                            hcolA = col(0)
                            hcolB = col(1)
                            if s in (0, 2):
                                e.wait_ge(p.sem(sld), b_ld + 16 * ld_per_tb * tb + 48)
                                bump1(e.tensor_copy(gtmp[:], ps[:]))
                                wself()
                                e.wait_ge(p.sem(spe), b_pe + PE_TB * tb + pe_extra[s][0])
                                bump1(e.tensor_copy(swtmp[:], psR[:]))
                                bump1(e.tensor_tensor(t1tmp[:], gtmp[:], ctab[tb % 2][:],
                                                      ALU.mult))
                                wself()
                                bump1(e.tensor_tensor(t2tmp[:], swtmp[:], stab[tb % 2][:],
                                                      ALU.mult))
                                wself()
                                bump1(e.tensor_tensor(roptmp[:], t1tmp[:], t2tmp[:],
                                                      ALU.add))
                                wself()
                                bump1(e.tensor_copy(dstT[0:64, hcolA:hcolA + 512],
                                                    roptmp[0:64, :]))
                                e.wait_ge(p.sem(spe), b_pe + PE_TB * tb + pe_extra[s][1])
                                bump1(e.tensor_copy(dstT[0:64, hcolB:hcolB + 512],
                                                    psR[0:64, :]))
                            elif s in (1, 3):
                                bump1(e.tensor_copy(gtmp[:], ps[:]))
                                wself()
                                e.wait_ge(p.sem(spe), b_pe + PE_TB * tb + pe_rot[s])
                                bump1(e.tensor_copy(dstT[64:128, hcolA:hcolA + 512],
                                                    psR[64:128, :]))
                                bump1(e.tensor_copy(dstT[64:128, hcolB:hcolB + 512],
                                                    gtmp[64:128, :]))
                            else:
                                h = s - 4
                                bump1(e.tensor_copy(vtmp[:], ps[:]))
                                wself()
                                for sub in range(4):
                                    kc = tb * 4 + sub
                                    e.wait_ge(p.sem(spe),
                                              b_pe + PE_TB * tb + 102 + 4 * h + sub + 1)
                                    bump1(e.tensor_copy(
                                        V_tm[:, kc * 256 + h * 128: kc * 256 + h * 128 + 128],
                                        psT[sub % 2][:]))
                    assert n == DVE_TB, (li, tb, n)

            @blk.tensor
            def _(e):
                e.wait_ge(p.sem(sdve), b_dve + 6)
                ld_per_tb = 3 if li == 1 else 1
                for tb in range(NTB):
                    e.wait_ge(p.sem(sld), b_ld + 16 * ld_per_tb * tb + 16)
                    dvetb = b_dve + 6 + DVE_TB * tb
                    for s in range(6):
                        if tb > 0 or s >= 2:
                            if s >= 2:
                                pn = dve_evac_done(li, tb, s - 2)
                            else:
                                pn = dve_evac_done(li, tb - 1, s + 4)
                            e.wait_ge(p.sem(sdve), b_dve + 6 + pn)
                        for cc in range(CD):
                            e.matmul(psQ[s % 2][:],
                                     qkvw[:, s * 2048 + cc * 128: s * 2048 + (cc + 1) * 128],
                                     xnb[tb % 2][:, cc * 512:(cc + 1) * 512],
                                     start=(cc == 0), stop=(cc == CD - 1),
                                     skip_group_check=True).then_inc(p.sem(spe), 1)
                            p.bump(spe, 1)
                        if li == 1 and s in (0, 2):
                            e.wait_ge(p.sem(sdve),
                                      b_dve + 6 + dve_set_start(li, tb, s) + 1)
                            e.matmul(psR[:], g["mswap"][:], gtmp[:],
                                     start=True, stop=True, skip_group_check=True
                                     ).then_inc(p.sem(spe), 1)
                            p.bump(spe, 1)
                            e.wait_ge(p.sem(sdve),
                                      b_dve + 6 + dve_set_start(li, tb, s) + 5)
                            e.matmul(psR[:], g["mrot"][:], roptmp[:],
                                     start=True, stop=True, skip_group_check=True
                                     ).then_inc(p.sem(spe), 1)
                            p.bump(spe, 1)
                        if li == 1 and s in (1, 3):
                            e.wait_ge(p.sem(sdve),
                                      b_dve + 6 + dve_set_start(li, tb, s) + 1)
                            e.matmul(psR[:], g["mrot"][:], gtmp[:],
                                     start=True, stop=True, skip_group_check=True
                                     ).then_inc(p.sem(spe), 1)
                            p.bump(spe, 1)
                    for h in range(2):
                        s = 4 + h
                        e.wait_ge(p.sem(sdve), b_dve + 6 + dve_vtmp_done(li, tb, h))
                        for sub in range(4):
                            n96 = 4 * h + sub
                            if n96 >= 2:
                                e.wait_ge(p.sem(sdve),
                                          b_dve + 6 + dve_pst_done(li, tb, n96 - 2))
                            e.transpose(psT[sub % 2][:],
                                        vtmp[:, sub * 128:(sub + 1) * 128],
                                        g["id_bf16"][:]).then_inc(p.sem(spe), 1)
                            p.bump(spe, 1)

    if "qkv" in p.dbg:
        with nc.Block() as blk:
            @blk.gpsimd
            def _(e):
                s = p.sem(f"p2dbg_{li}")
                e.wait_ge(p.sem(spe), p.val(spe))
                e.wait_ge(p.sem(sdve), p.val(sdve))
                e.dma_start(out=p.dram["dbg_q"].ap(), in_=Q_sb[:]).then_inc(p.sem(f"p2dbg_{li}"), 16)
                p.bump(f"p2dbg_{li}", 16)
                e.dma_start(out=p.dram["dbg_k"].ap(), in_=K_sb[:]).then_inc(p.sem(f"p2dbg_{li}"), 16)
                p.bump(f"p2dbg_{li}", 16)
                e.dma_start(out=p.dram["dbg_v"].ap(), in_=V_tm[:]).then_inc(p.sem(f"p2dbg_{li}"), 16)
                p.bump(f"p2dbg_{li}", 16)
                e.wait_ge(p.sem(f"p2dbg_{li}"), p.val(f"p2dbg_{li}"))


_L1_ENDS = [7, 10, 17, 20, 25, 30]
_L1_STARTS = [0, 7, 10, 17, 20, 25]
_L0_ENDS = [1, 2, 3, 4, 9, 14]
_L0_STARTS = [0, 1, 2, 3, 4, 9]


def dve_evac_done(li, tb, s):
    if tb < 0:
        return 0
    DVE_TB = 14 if li == 0 else 30
    base = DVE_TB * tb
    return base + (_L0_ENDS if li == 0 else _L1_ENDS)[s]


def dve_set_start(li, tb, s):
    DVE_TB = 14 if li == 0 else 30
    return DVE_TB * tb + (_L0_STARTS if li == 0 else _L1_STARTS)[s]


def dve_vtmp_done(li, tb, h):
    return dve_set_start(li, tb, 4 + h) + 1


def dve_pst_done(li, tb, n96):
    h, sub = n96 // 4, n96 % 4
    return dve_set_start(li, tb, 4 + h) + 1 + sub + 1



def emit_p3(p, li, Q_sb, K_sb, V_tm, maskA, maskB):
    nc = p.nc
    g = p.g
    sdve = f"p3dve_{li}"
    sact = f"p3act_{li}"
    spe = f"p3pe_{li}"
    sst = f"p3st_{li}"
    scc2 = f"cc_a2a_{li}"

    with contextlib.ExitStack() as ph:
        probsT = [ph.enter_context(nc.sbuf_tensor(f"p3pr{i}_{li}", [128, 512], BF16))
                  for i in range(2)]
        zr = ph.enter_context(nc.sbuf_tensor(f"p3zr_{li}", [1, 512], F32))
        bc = ph.enter_context(nc.sbuf_tensor(f"p3bc_{li}", [128, 512], F32))
        ybuf = [ph.enter_context(nc.sbuf_tensor(f"p3y{i}_{li}", [128, 512], BF16))
                for i in range(2)]
        psB = [ph.enter_context(nc.psum_tensor(f"p3pb{i}_{li}", [128, 512], F32))
               for i in range(3)]
        psZ = ph.enter_context(nc.psum_tensor(f"p3pz_{li}", [1, 512], F32))
        psPV = ph.enter_context(nc.psum_tensor(f"p3pv_{li}", [128, 512], F32))
        psBC = ph.enter_context(nc.psum_tensor(f"p3pbc_{li}", [128, 512], F32))

        import os
        blocks = [(h, B) for h in range(HPC) for B in range(NQB)]
        nblk = int(os.environ.get("P3_BLOCKS", "0"))
        if nblk:
            blocks = blocks[:nblk]

        def b_cnt(B):
            return 4 * B + 4

        pe_c = [0]
        dve_c = [0]
        act_c = [0]
        st_c = [0]
        for (h, B) in blocks:
            pe_c.append(pe_c[-1] + 3 * b_cnt(B) + 2)
            dve_c.append(dve_c[-1] + 7)
            act_c.append(act_c[-1] + b_cnt(B))
            st_c.append(st_c[-1] + 16)
        b_pe, b_dve, b_act, b_st = p.val(spe), p.val(sdve), p.val(sact), p.val(sst)

        def mm1_at(j):
            return 1 if j == 0 else 3 * j - 1

        def pv_at(B, j):
            nb = b_cnt(B)
            return 3 * (j + 1) + 1 if j < nb - 1 else 3 * nb

        def ztail_at(B):
            return 3 * b_cnt(B) - 1

        def qcol(h, B):
            return h * T + B * 512

        with nc.Block() as blk:
            @blk.tensor
            def _(e):
                for bi, (h, B) in enumerate(blocks):
                    npe = b_pe + pe_c[bi]
                    ndve = b_dve + dve_c[bi]
                    nact = b_act + act_c[bi]
                    nb = b_cnt(B)
                    if bi > 0:
                        e.wait_ge(p.sem(sdve), b_dve + dve_c[bi])
                    for j in range(nb):
                        if j >= 3:
                            e.wait_ge(p.sem(sact), nact + j - 2)
                        e.matmul(psB[j % 3][:],
                                 K_sb[:, h * T + 128 * j: h * T + 128 * j + 128],
                                 Q_sb[:, qcol(h, B): qcol(h, B) + 512],
                                 start=True, stop=True, skip_group_check=True
                                 ).then_inc(p.sem(spe), 1)
                        p.bump(spe, 1)
                        if j >= 1:
                            jj = j - 1
                            e.wait_ge(p.sem(sact), nact + jj + 1)
                            e.matmul(psZ[:], g["ones_128x1"][:], probsT[jj % 2][:],
                                     start=(jj == 0), stop=False,
                                     skip_group_check=True).then_inc(p.sem(spe), 1)
                            p.bump(spe, 1)
                            e.matmul(psPV[:],
                                     V_tm[:, jj * 256 + h * 128: jj * 256 + h * 128 + 128],
                                     probsT[jj % 2][:],
                                     start=(jj == 0), stop=False,
                                     skip_group_check=True).then_inc(p.sem(spe), 1)
                            p.bump(spe, 1)
                    jj = nb - 1
                    e.wait_ge(p.sem(sact), nact + jj + 1)
                    e.matmul(psZ[:], g["ones_128x1"][:], probsT[jj % 2][:],
                             start=(jj == 0), stop=True,
                             skip_group_check=True).then_inc(p.sem(spe), 1)
                    p.bump(spe, 1)
                    e.matmul(psPV[:],
                             V_tm[:, jj * 256 + h * 128: jj * 256 + h * 128 + 128],
                             probsT[jj % 2][:],
                             start=(jj == 0), stop=True,
                             skip_group_check=True).then_inc(p.sem(spe), 1)
                    p.bump(spe, 1)
                    e.wait_ge(p.sem(sdve), ndve + 5)
                    e.matmul(psBC[:], g["ones_1x128_f32"][:], zr[:],
                             start=True, stop=True, skip_group_check=True
                             ).then_inc(p.sem(spe), 1)
                    p.bump(spe, 1)
                    e.matmul(psBC[:], g["ones_1x128_f32"][:], zr[:],
                             start=True, stop=True, skip_group_check=True
                             ).then_inc(p.sem(spe), 1)
                    p.bump(spe, 1)
                    assert p.val(spe) == b_pe + pe_c[bi + 1], (bi, p.val(spe))

            @blk.scalar
            def _(e):
                for bi, (h, B) in enumerate(blocks):
                    npe = b_pe + pe_c[bi]
                    ndve = b_dve + dve_c[bi]
                    nb = b_cnt(B)
                    for j in range(nb):
                        if j >= 4 * B:
                            e.wait_ge(p.sem(sdve), ndve + (j - 4 * B) + 1)
                        else:
                            e.wait_ge(p.sem(spe), npe + mm1_at(j))
                        if j >= 2:
                            e.wait_ge(p.sem(spe), npe + pv_at(B, j - 2))
                        elif bi > 0:
                            e.wait_ge(p.sem(spe), b_pe + pe_c[bi])
                        e.activation(probsT[j % 2][:], psB[j % 3][:], AF.Exp,
                                     scale=SC).then_inc(p.sem(sact), 1)
                        p.bump(sact, 1)

            @blk.vector
            def _(e):
                for bi, (h, B) in enumerate(blocks):
                    npe = b_pe + pe_c[bi]
                    ndve = b_dve + dve_c[bi]
                    nact = b_act + act_c[bi]
                    nb = b_cnt(B)
                    for m in range(4):
                        j = 4 * B + m
                        e.wait_ge(p.sem(spe), npe + mm1_at(j))
                        e.tensor_tensor(psB[j % 3][:], psB[j % 3][:],
                                        maskB[:, 512 * m: 512 * m + 512], ALU.add
                                        ).then_inc(p.sem(sdve), 1)
                        p.bump(sdve, 1)
                    e.wait_ge(p.sem(spe), npe + ztail_at(B) + 1)
                    e.reciprocal(zr[:], psZ[:]).then_inc(p.sem(sdve), 1)
                    p.bump(sdve, 1)
                    e.wait_ge(p.sem(spe), b_pe + pe_c[bi + 1])
                    e.tensor_copy(bc[:], psBC[:]).then_inc(p.sem(sdve), 1)
                    p.bump(sdve, 1)
                    e.wait_ge(p.sem(sdve), ndve + 6)
                    if bi >= 2:
                        e.wait_ge(p.sem(sst), b_st + st_c[bi - 1])
                    e.tensor_tensor(ybuf[bi % 2][:], psPV[:], bc[:], ALU.mult
                                    ).then_inc(p.sem(sdve), 1)
                    p.bump(sdve, 1)

            @blk.gpsimd
            def _(e):
                for bi, (h, B) in enumerate(blocks):
                    e.wait_ge(p.sem(sdve), b_dve + dve_c[bi + 1])
                    e.dma_start(
                        out=p.dram["a2a_in"].ap()[B // 2,
                                                  h * 128:(h + 1) * 128,
                                                  (B % 2) * 512:(B % 2) * 512 + 512],
                        in_=ybuf[bi % 2][:]).then_inc(p.sem(sst), 16)
                    p.bump(sst, 16)
                e.wait_ge(p.sem(sst), b_st + st_c[-1])
                with allow_ici_all_to_all():
                    e.collective_compute(
                        "AllToAll",
                        ALU.bypass,
                        replica_groups=[list(range(NCORE))],
                        ins=[p.dram["a2a_in"].ap().rearrange("r x t -> (r x) t").opt()],
                        outs=[p.dram["a2a_out"].ap().rearrange("r x t -> (r x) t").opt()],
                    ).then_inc(p.sem(scc2), 1)
                p.bump(scc2, 1)
                if "yattn" in p.dbg:
                    e.wait_ge(p.sem(scc2), p.val(scc2))
                    e.dma_start(out=p.dram["dbg_yattn"].ap().rearrange("r x t -> (r x) t"),
                                in_=p.dram["a2a_out"].ap().rearrange("r x t -> (r x) t")
                                ).then_inc(p.sem(sst), 16)
                    p.bump(sst, 16)
                    e.wait_ge(p.sem(sst), p.val(sst))



def emit_p4(p, li, x_in, x_out):
    nc = p.nc
    g = p.g
    pre = f"l{li}_"
    wo_d = p.dram[pre + "wo"]

    sld = f"p4ld_{li}"
    swl = f"p4wl_{li}"
    sdve = f"p4dve_{li}"
    spe = f"p4pe_{li}"
    sst = f"p4st_{li}"
    b_ld, b_wl, b_dve, b_pe, b_st = (p.val(sld), p.val(swl), p.val(sdve),
                                     p.val(spe), p.val(sst))
    scc2 = f"cc_a2a_{li}"
    cc2_t = p.val(scc2)

    with contextlib.ExitStack() as ph:
        yab = ph.enter_context(nc.sbuf_tensor(f"p4y_{li}", [128, CD * 512], BF16))
        fm = ph.enter_context(nc.sbuf_tensor(f"p4fm_{li}", [128, CD * 512], F32))
        wostage = ph.enter_context(nc.sbuf_tensor(f"p4ws_{li}", [128, 2048], F32))
        wob = [ph.enter_context(nc.sbuf_tensor(f"p4wb{i}_{li}", [128, 2048], BF16))
               for i in range(2)]
        ffnb = [ph.enter_context(nc.sbuf_tensor(f"p4fb{i}_{li}", [128, 512], F32))
                for i in range(2)]
        xold = [ph.enter_context(nc.sbuf_tensor(f"p4xo{i}_{li}", [128, D], F32))
                for i in range(2)]
        xnew = [ph.enter_context(nc.sbuf_tensor(f"p4xn{i}_{li}", [128, D], F32))
                for i in range(2)]
        psO = [ph.enter_context(nc.psum_tensor(f"p4po{i}_{li}", [128, 512], F32))
               for i in range(2)]
        psTr = [ph.enter_context(nc.psum_tensor(f"p4pt{i}_{li}", [128, 128], F32))
                for i in range(2)]

        PE_TB = 256 + 64
        DVE_TB = 32 + 64

        with nc.Block() as blk:
            @blk.sync
            def _(e):
                e.wait_ge(p.sem(scc2), cc2_t)
                for tb in range(2):
                    if tb >= 1:
                        e.wait_ge(p.sem(spe), b_pe + PE_TB * tb)
                    e.dma_start(
                        out=yab[:],
                        in_=p.dram["a2a_out"].ap().rearrange(
                            "r (q x) t -> x (r q) t", x=128)
                        [:, :, tb * 512:(tb + 1) * 512]).then_inc(p.sem(sld), 16)
                    p.bump(sld, 16)
                    for oc in range(CD):
                        if tb > 0 or oc >= 2:
                            prev = (tb, oc - 2) if oc >= 2 else (tb - 1, CD - 2)
                            e.wait_ge(p.sem(sdve),
                                      b_dve + DVE_TB * prev[0] + 2 * prev[1] + 2)
                        e.dma_start(
                            out=ffnb[oc % 2][:],
                            in_=p.dram["ffn_out"].ap().rearrange(
                                "(c q) t -> c q t", q=128)
                            [oc, :, tb * 512:(tb + 1) * 512]).then_inc(p.sem(sld), 16)
                        p.bump(sld, 16)
                    for tt in range(4):
                        if tb > 0 or tt >= 2:
                            prev = (tb, tt - 2) if tt >= 2 else (tb - 1, 2)
                            e.wait_ge(p.sem(sdve),
                                      b_dve + DVE_TB * prev[0] + 32 + 16 * (prev[1] + 1))
                        i = tb * 4 + tt
                        e.dma_start(out=xold[tt % 2][:],
                                    in_=x_in.ap()[i * 128:(i + 1) * 128, :]
                                    ).then_inc(p.sem(sld), 16)
                        p.bump(sld, 16)

            @blk.gpsimd
            def _(e):
                for tb in range(2):
                    for oc in range(CD):
                        n = tb * CD + oc
                        if n >= 1:
                            pt, po = (n - 1) // CD, (n - 1) % CD
                            e.wait_ge(p.sem(sdve), b_dve + DVE_TB * pt + 2 * po + 1)
                        if n >= 2:
                            pt, po = (n - 2) // CD, (n - 2) % CD
                            e.wait_ge(p.sem(spe), b_pe + PE_TB * pt + 16 * (po + 1))
                        e.dma_start(
                            out=wostage[:],
                            in_=wo_d.ap().rearrange("(c q) o -> q c o", q=128)
                            [:, :, oc * 128:(oc + 1) * 128]).then_inc(p.sem(swl), 16)
                        p.bump(swl, 16)
                e.wait_ge(p.sem(sst), b_st + 16 * 8)
                if li == 0 and "x1" in p.dbg:
                    s = p.sem(f"p4dbg_{li}")
                    e.dma_start(out=p.dram["dbg_x1"].ap(), in_=x_out.ap()
                                ).then_inc(s, 16)
                    p.bump(f"p4dbg_{li}", 16)
                    e.wait_ge(s, p.val(f"p4dbg_{li}"))

            @blk.vector
            def _(e):
                for tb in range(2):
                    for oc in range(CD):
                        n = tb * CD + oc
                        e.wait_ge(p.sem(swl), b_wl + 16 * (n + 1))
                        e.tensor_copy(wob[n % 2][:], wostage[:]).then_inc(p.sem(sdve), 1)
                        p.bump(sdve, 1)
                        e.wait_ge(p.sem(spe), b_pe + PE_TB * tb + 16 * (oc + 1))
                        e.wait_ge(p.sem(sld), b_ld + 16 * (21 * tb + 1 + oc + 1))
                        if li == 0:
                            e.scalar_tensor_tensor(
                                fm[:, oc * 512:(oc + 1) * 512], psO[oc % 2][:],
                                g["bo"][:, oc:oc + 1], ffnb[oc % 2][:],
                                ALU.add, ALU.add).then_inc(p.sem(sdve), 1)
                        else:
                            e.tensor_tensor(
                                fm[:, oc * 512:(oc + 1) * 512], psO[oc % 2][:],
                                ffnb[oc % 2][:], ALU.add).then_inc(p.sem(sdve), 1)
                        p.bump(sdve, 1)
                    for tt in range(4):
                        e.wait_ge(p.sem(sld), b_ld + 16 * (21 * tb + 17 + tt + 1))
                        i = tb * 4 + tt
                        if i >= 2:
                            e.wait_ge(p.sem(sst), b_st + 16 * (i - 1))
                        for oc in range(CD):
                            ntr = 16 * tt + oc
                            e.wait_ge(p.sem(spe),
                                      b_pe + PE_TB * tb + 256 + ntr + 1)
                            e.tensor_tensor(
                                xnew[tt % 2][:, oc * 128:(oc + 1) * 128],
                                psTr[ntr % 2][:],
                                xold[tt % 2][:, oc * 128:(oc + 1) * 128],
                                ALU.add).then_inc(p.sem(sdve), 1)
                            p.bump(sdve, 1)


            @blk.scalar
            def _(e):
                for tb in range(2):
                    for tt in range(4):
                        i = tb * 4 + tt
                        e.wait_ge(p.sem(sdve),
                                  b_dve + DVE_TB * tb + 32 + 16 * (tt + 1))
                        if i >= 2:
                            e.wait_ge(p.sem(sst), b_st + 16 * (i - 1))
                        e.dma_start(out=x_out.ap()[i * 128:(i + 1) * 128, :],
                                    in_=xnew[tt % 2][:]).then_inc(p.sem(sst), 16)
                        p.bump(sst, 16)

            @blk.tensor
            def _(e):
                for tb in range(2):
                    e.wait_ge(p.sem(sld), b_ld + 16 * (21 * tb + 1))
                    for oc in range(CD):
                        n = tb * CD + oc
                        e.wait_ge(p.sem(sdve), b_dve + DVE_TB * tb + 2 * oc + 1)
                        if n >= 2:
                            pt, po = (n - 2) // CD, (n - 2) % CD
                            e.wait_ge(p.sem(sdve), b_dve + DVE_TB * pt + 2 * po + 2)
                        for c in range(CD):
                            e.matmul(psO[oc % 2][:],
                                     wob[n % 2][:, c * 128:(c + 1) * 128],
                                     yab[:, c * 512:(c + 1) * 512],
                                     start=(c == 0), stop=(c == CD - 1),
                                     skip_group_check=True).then_inc(p.sem(spe), 1)
                            p.bump(spe, 1)
                    for tt in range(4):
                        for oc in range(CD):
                            ntr = 16 * tt + oc
                            e.wait_ge(p.sem(sdve), b_dve + DVE_TB * tb + 2 * oc + 2)
                            if ntr >= 2:
                                e.wait_ge(p.sem(sdve),
                                          b_dve + DVE_TB * tb + 32 + ntr - 1)
                            e.transpose(psTr[ntr % 2][:],
                                        fm[:, oc * 512 + tt * 128: oc * 512 + tt * 128 + 128],
                                        g["id_f32"][:]).then_inc(p.sem(spe), 1)
                            p.bump(spe, 1)


def kernel(**inp):
    prog = build()
    in_maps = [prep_inputs(inp, c) for c in range(NCORE)]
    res = run_bass_kernel_spmd(prog.nc, in_maps, core_ids=list(range(NCORE)))
    out = np.concatenate([res.results[c]["y"] for c in range(NCORE)], axis=0)
    return out.reshape(1, T, D).astype(np.float32)


# version 05
# speedup vs baseline: 1.1867x; 1.1867x over previous
import contextlib
import math

import numpy as np
import ml_dtypes

import concourse.bass as bass
import concourse.mybir as mybir
from concourse.bass_utils import run_bass_kernel_spmd
from concourse.aot_env import allow_ici_all_to_all

F32 = mybir.dt.float32
BF16 = mybir.dt.bfloat16
AF = mybir.ActivationFunctionType
ALU = mybir.AluOpType
AX = mybir.AxisListType

NCORE = 8
T = 8192
TPC = T // NCORE
D = 2048
NH = 16
HPC = NH // NCORE
HD = 128
INTER = 8192
CD = D // 128
HC = INTER // 128
NEG = -1.0e30
SC = 1.0 / math.sqrt(128.0)
EPS = 1e-5
NTT = TPC // 128
NTB = T // 512
NQB = NTB



def host_constants():
    c = {}
    c["id_bf16"] = np.eye(128, dtype=ml_dtypes.bfloat16)
    c["id_f32"] = np.eye(128, dtype=np.float32)
    c["ones_1x128"] = np.ones((1, 128), dtype=ml_dtypes.bfloat16)
    c["ones_128x1"] = np.ones((128, 1), dtype=ml_dtypes.bfloat16)
    c["ones_1x128_f32"] = np.ones((1, 128), dtype=np.float32)
    c["epsb"] = np.full((128, 1), 1e-5, dtype=np.float32)
    r = np.arange(128)[:, None]
    col = np.arange(512)[None, :]
    mA = np.stack([(col <= 128 * m + r) for m in range(4)])
    c["maskA"] = np.where(mA, 0.0, NEG).astype(np.float32)
    mB = np.stack([(col >= 128 * m + r) for m in range(4)])
    c["maskB"] = np.where(mB, 0.0, NEG).astype(np.float32)
    base = 10000.0
    inv = 1.0 / base ** (np.arange(0, 64, 2, dtype=np.float64) / 64)
    fr = np.outer(inv, np.arange(T, dtype=np.float64))
    c["ropecos"] = np.tile(np.cos(fr), (4, 1)).astype(ml_dtypes.bfloat16)
    sin = np.tile(np.sin(fr), (4, 1))
    sign = np.repeat(np.array([-1.0, 1.0, -1.0, 1.0]), 32)[:, None]
    c["ropesin"] = (sin * sign).astype(ml_dtypes.bfloat16)
    mswap = np.zeros((128, 128), dtype=np.float32)
    for i in range(128):
        mswap[i ^ 32, i] = 1.0
    c["mswap"] = mswap.astype(ml_dtypes.bfloat16)
    mrot = np.zeros((128, 128), dtype=np.float32)
    for i in range(128):
        mrot[(i + 64) % 128, i] = 1.0
    c["mrot"] = mrot.astype(ml_dtypes.bfloat16)
    h = np.array([[1.0]])
    while h.shape[0] < 128:
        h = np.block([[h, h], [h, -h]])
    c["h128"] = h.astype(ml_dtypes.bfloat16)
    return c


_CONSTS = None


def prep_inputs(inp, core):
    global _CONSTS
    d = {}
    h0 = HPC * core
    cc = np.ascontiguousarray
    asnp = np.asarray

    def tile4(w):
        O, I = w.shape
        return cc(w.reshape(O // 128, 128, I // 128, 128).transpose(0, 3, 2, 1))

    def settile(w):
        I, DO = w.shape
        return w.reshape(I // 128, 128, DO).transpose(1, 0, 2)

    wq = asnp(inp["L0_attn_Wq"])[h0:h0 + HPC]
    wk = asnp(inp["L0_attn_Wk"])[h0:h0 + HPC]
    wv = asnp(inp["L0_attn_Wv"])[h0:h0 + HPC]
    d["l0_qkvw"] = cc(np.stack([settile(wq[0]), settile(wq[1]),
                                settile(wk[0]), settile(wk[1]),
                                settile(wv[0]), settile(wv[1])]))
    d["l0_bq"] = cc(asnp(inp["L0_attn_bq"])[h0:h0 + HPC].T)
    d["l0_bk"] = cc(asnp(inp["L0_attn_bk"])[h0:h0 + HPC].T)
    d["l0_bv"] = cc(asnp(inp["L0_attn_bv"])[h0:h0 + HPC].T)
    wo0 = asnp(inp["L0_attn_Wo"])
    d["l0_wo"] = cc(wo0.reshape(CD, 128, CD, 128).transpose(2, 1, 0, 3))
    d["l0_bo"] = cc(asnp(inp["L0_attn_bo"]).reshape(CD, 128).T)
    d["l0_wg"] = tile4(asnp(inp["L0_ffn_W_gate"]))
    d["l0_wu"] = tile4(asnp(inp["L0_ffn_W_up"]))
    d["l0_wd"] = tile4(asnp(inp["L0_ffn_W_down"]))
    d["l0_su_ug"] = cc(asnp(inp["L0_ffn_SU_ug"]).reshape(CD, 128).T)
    d["l0_su_dp"] = cc(asnp(inp["L0_ffn_SU_dp"]).reshape(HC, 128).T)
    d["l0_ws_up"] = cc(asnp(inp["L0_ffn_Wscale_ug"])[:INTER].reshape(HC, 128).T)
    d["l0_ws_g"] = cc(asnp(inp["L0_ffn_Wscale_ug"])[INTER:].reshape(HC, 128).T)
    d["l0_ws_dp"] = cc(asnp(inp["L0_ffn_Wscale_dp"]).reshape(CD, 128).T)

    sl = slice(h0 * 64, (h0 + HPC) * 64)
    slv = slice(h0 * HD, (h0 + HPC) * HD)
    wqg = settile(asnp(inp["L1_attn_Wq_geo"])[sl].T)
    wqs = settile(asnp(inp["L1_attn_Wq_sem"])[sl].T)
    wkg = settile(asnp(inp["L1_attn_Wk_geo"])[sl].T)
    wks = settile(asnp(inp["L1_attn_Wk_sem"])[sl].T)
    wv1 = settile(asnp(inp["L1_attn_Wv"])[slv].T)
    d["l1_qkvw"] = cc(np.stack([wqg, wqs, wkg, wks,
                                wv1[:, :, 0:128], wv1[:, :, 128:256]]))
    wo1 = asnp(inp["L1_attn_Wo"])
    d["l1_wo"] = cc(wo1.reshape(CD, 128, CD, 128).transpose(0, 3, 2, 1))
    d["l1_wg"] = tile4(asnp(inp["L1_ffn_w_gate"]))
    d["l1_wu"] = tile4(asnp(inp["L1_ffn_w_up"]))
    d["l1_wd"] = tile4(asnp(inp["L1_ffn_w_down"]))

    x = asnp(inp["L0_attn_x"]).reshape(T, D)
    d["x0"] = cc(x[core * TPC:(core + 1) * TPC]).astype(np.float32)
    if _CONSTS is None:
        _CONSTS = host_constants()
    d.update(_CONSTS)
    return d



class Prog:
    def __init__(self, layers=(0, 1), dbg=(), phases=("p0", "p1", "p2", "p3", "p4")):
        self.nc = bass.Bass("TRN2", target_bir_lowering=False)
        self.ctx = contextlib.ExitStack()
        self.dbg = set(dbg)
        self.phases = set(phases)
        self.layers = layers
        self.dram = {}
        self.g = {}
        self.semv = {}

    def sem(self, name):
        if name not in self.semv:
            h = self.ctx.enter_context(self.nc.semaphore(name))
            self.semv[name] = [h, 0]
        return self.semv[name][0]

    def bump(self, name, k=1):
        self.sem(name)
        self.semv[name][1] += k
        return self.semv[name][1]

    def val(self, name):
        self.sem(name)
        return self.semv[name][1]

    def sbuf(self, name, shape, dtype):
        return self.ctx.enter_context(self.nc.sbuf_tensor(name, shape, dtype))

    def psum(self, name, shape, dtype):
        return self.ctx.enter_context(self.nc.psum_tensor(name, shape, dtype))

    def din(self, name, shape, dtype=F32):
        t = self.nc.dram_tensor(name, shape, dtype, kind="ExternalInput")
        self.dram[name] = t
        return t

    def dout(self, name, shape, dtype=F32):
        t = self.nc.dram_tensor(name, shape, dtype, kind="ExternalOutput")
        self.dram[name] = t
        return t

    def dint(self, name, shape, dtype=F32):
        t = self.nc.dram_tensor(name, shape, dtype)
        self.dram[name] = t
        return t

    def barrier(self):
        nc = self.nc
        s = self.sem("bar")
        base = self.semv["bar"][1]
        with nc.Block() as blk:
            @blk.sync
            def _(e):
                e.nop().then_inc(s, 1)
                e.wait_ge(s, base + 5)

            @blk.tensor
            def _(e):
                e.nop().then_inc(s, 1)
                e.wait_ge(s, base + 5)

            @blk.vector
            def _(e):
                e.nop().then_inc(s, 1)
                e.wait_ge(s, base + 5)

            @blk.scalar
            def _(e):
                e.nop().then_inc(s, 1)
                e.wait_ge(s, base + 5)

            @blk.gpsimd
            def _(e):
                e.nop().then_inc(s, 1)
                e.wait_ge(s, base + 5)
        self.semv["bar"][1] = base + 5


def build(layers=(0, 1), dbg=(), phases=("p0", "p1", "p2", "p3", "p4")):
    p = Prog(layers, dbg, phases)
    nc = p.nc

    p.din("l0_qkvw", [6, 128, CD, 128])
    p.din("l0_bq", [128, HPC]); p.din("l0_bk", [128, HPC]); p.din("l0_bv", [128, HPC])
    p.din("l0_wo", [CD, 128, CD, 128]); p.din("l0_bo", [128, CD])
    p.din("l0_wg", [HC, 128, CD, 128]); p.din("l0_wu", [HC, 128, CD, 128])
    p.din("l0_wd", [CD, 128, HC, 128])
    p.din("l0_su_ug", [128, CD]); p.din("l0_su_dp", [128, HC])
    p.din("l0_ws_up", [128, HC]); p.din("l0_ws_g", [128, HC]); p.din("l0_ws_dp", [128, CD])
    p.din("l1_qkvw", [6, 128, CD, 128])
    p.din("l1_wo", [CD, 128, CD, 128])
    p.din("l1_wg", [HC, 128, CD, 128]); p.din("l1_wu", [HC, 128, CD, 128])
    p.din("l1_wd", [CD, 128, HC, 128])
    p.din("x0", [TPC, D])
    p.din("id_bf16", [128, 128], BF16); p.din("id_f32", [128, 128], F32)
    p.din("ones_1x128", [1, 128], BF16); p.din("ones_128x1", [128, 1], BF16)
    p.din("ones_1x128_f32", [1, 128], F32)
    p.din("maskA", [4, 128, 512]); p.din("maskB", [4, 128, 512])
    p.din("ropecos", [128, T], BF16); p.din("ropesin", [128, T], BF16)
    p.din("h128", [128, 128], BF16); p.din("epsb", [128, 1])
    p.din("mswap", [128, 128], BF16); p.din("mrot", [128, 128], BF16)

    p.dout("y", [TPC, D])
    p.dint("x1", [TPC, D])
    p.dint("ag_in", [D, TPC], BF16)
    p.dint("ag_out", [NCORE, D, TPC], BF16)
    p.dint("a2a_in", [NCORE, HPC * HD, TPC], BF16)
    p.dint("a2a_out", [NCORE, HPC * HD, TPC], BF16)
    p.dint("ffn_out", [D, TPC], F32)
    p.dint("wbf", [3 * D * INTER], BF16)

    if "xn_full" in p.dbg:
        p.dout("dbg_xn_full", [NCORE, D, TPC], BF16)
    if "z" in p.dbg:
        p.dout("dbg_z", [128, CD * TPC], BF16)
    if "ffn_out" in p.dbg:
        p.dout("dbg_ffn_out", [D, TPC], F32)
    if "qkv" in p.dbg:
        p.dout("dbg_q", [128, HPC * T], BF16)
        p.dout("dbg_k", [128, HPC * T], BF16)
        p.dout("dbg_v", [128, (T // 128) * HPC * HD], BF16)
    if "yattn" in p.dbg:
        p.dout("dbg_yattn", [NCORE, HPC * HD, TPC], BF16)
    if "x1" in p.dbg:
        p.dout("dbg_x1", [TPC, D])

    g = p.g
    g["id_bf16"] = p.sbuf("g_id_bf16", [128, 128], BF16)
    g["id_f32"] = p.sbuf("g_id_f32", [128, 128], F32)
    g["ones_1x128"] = p.sbuf("g_ones_1x128", [1, 128], BF16)
    g["ones_128x1"] = p.sbuf("g_ones_128x1", [128, 1], BF16)
    g["ones_1x128_f32"] = p.sbuf("g_ones_1x128_f32", [1, 128], F32)
    g["h128"] = p.sbuf("g_h128", [128, 128], BF16)
    g["epsb"] = p.sbuf("g_epsb", [128, 1], F32)
    g["mswap"] = p.sbuf("g_mswap", [128, 128], BF16)
    g["mrot"] = p.sbuf("g_mrot", [128, 128], BF16)
    g["su_ug"] = p.sbuf("g_su_ug", [128, CD], F32)
    g["su_sup"] = p.sbuf("g_su_sup", [128, HC], F32)
    g["ws_g"] = p.sbuf("g_ws_g", [128, HC], F32)
    g["ws_dp"] = p.sbuf("g_ws_dp", [128, CD], F32)
    g["bq"] = p.sbuf("g_bq", [128, HPC], F32)
    g["bk"] = p.sbuf("g_bk", [128, HPC], F32)
    g["bv"] = p.sbuf("g_bv", [128, HPC], F32)
    g["bo"] = p.sbuf("g_bo", [128, CD], F32)

    with nc.Block() as blk:
        @blk.sync
        def _(e):
            for name in ("id_bf16", "id_f32", "ones_1x128", "ones_128x1",
                         "ones_1x128_f32", "h128", "epsb", "mswap", "mrot"):
                e.dma_start(out=g[name][:], in_=p.dram[name].ap()).then_inc(p.sem("ldc"), 16)
                p.bump("ldc", 16)
            for src, dst in (("l0_su_ug", "su_ug"), ("l0_ws_g", "ws_g"),
                             ("l0_ws_dp", "ws_dp"), ("l0_bq", "bq"),
                             ("l0_bk", "bk"), ("l0_bv", "bv"), ("l0_bo", "bo")):
                e.dma_start(out=g[dst][:], in_=p.dram[src].ap()).then_inc(p.sem("ldc"), 16)
                p.bump("ldc", 16)
            e.dma_start(out=g["su_sup"][:], in_=p.dram["l0_su_dp"].ap()).then_inc(p.sem("ldc"), 16)
            p.bump("ldc", 16)

        @blk.vector
        def _(e):
            e.wait_ge(p.sem("ldc"), p.val("ldc"))
            e.tensor_scalar(g["ws_g"][:], g["ws_g"][:], 1.0 / math.sqrt(D), None, ALU.mult)
            e.tensor_scalar(g["ws_dp"][:], g["ws_dp"][:], 1.0 / math.sqrt(INTER), None, ALU.mult)
            e.nop().then_inc(p.sem("vconst"), 1)
            p.bump("vconst", 1)

    wsu_tmp = p.sbuf("g_wsu_tmp", [128, HC], F32)
    with nc.Block() as blk:
        @blk.sync
        def _(e):
            e.dma_start(out=wsu_tmp[:], in_=p.dram["l0_ws_up"].ap()).then_inc(p.sem("ldc"), 16)
            p.bump("ldc", 16)

        @blk.vector
        def _(e):
            e.wait_ge(p.sem("ldc"), p.val("ldc"))
            e.tensor_tensor(g["su_sup"][:], g["su_sup"][:], wsu_tmp[:], ALU.mult
                            ).then_inc(p.sem("vconst"), 1)
            p.bump("vconst", 1)
            e.wait_ge(p.sem("vconst"), p.val("vconst"))
            e.tensor_scalar(g["su_sup"][:], g["su_sup"][:], 1.0 / math.sqrt(D), None, ALU.mult
                            ).then_inc(p.sem("vconst"), 1)
            p.bump("vconst", 1)

    for li in layers:
        emit_layer(p, li)

    p.ctx.close()
    return p


def emit_layer(p, li):
    last = (li == max(p.layers))
    if li == 0:
        x_in = p.dram["x0"]
    elif 0 not in p.layers:
        x_in = p.din("x1_in", [TPC, D])
    else:
        x_in = p.dram["x1"]
    x_out = p.dram["y"] if last else p.dram["x1"]
    with p.nc.sbuf_tensor(f"z_sb_{li}", [128, CD * TPC], BF16) as z_sb:
        p.g["z_sb"] = z_sb
        if "p0" in p.phases:
            emit_p0(p, li, x_in)
        if "p1" in p.phases:
            emit_p1(p, li)
    if "p2" in p.phases:
        emit_p2p3(p, li)
    if "p4" in p.phases:
        emit_p4(p, li, x_in, x_out)
    emit_final_touch(p, li, x_out)
    p.barrier()



def emit_p0(p, li, x_in):
    nc = p.nc
    g = p.g
    z_sb = p.g["z_sb"]

    with contextlib.ExitStack() as ph:
        xt = [ph.enter_context(nc.sbuf_tensor(f"p0x{i}_{li}", [128, D], F32)) for i in range(2)]
        sq = ph.enter_context(nc.sbuf_tensor(f"p0sq_{li}", [128, D], F32))
        xn = [ph.enter_context(nc.sbuf_tensor(f"p0xn{i}_{li}", [128, D], BF16)) for i in range(2)]
        st = ph.enter_context(nc.sbuf_tensor(f"p0st_{li}", [128, 16], F32))
        if li == 0:
            stage = [ph.enter_context(nc.sbuf_tensor(f"p0stage{i}_{li}", [128, CD * 512], BF16))
                     for i in range(2)]
            zb = [ph.enter_context(nc.sbuf_tensor(f"p0zb{i}_{li}", [128, CD * 512], BF16))
                  for i in range(2)]
            psz = [ph.enter_context(nc.psum_tensor(f"p0pz{i}_{li}", [128, 512], F32))
                   for i in range(2)]
        pst = [ph.enter_context(nc.psum_tensor(f"p0pt{i}_{li}", [128, 128], BF16))
               for i in range(4)]

        sld = f"p0ld_{li}"
        sdve = f"p0dve_{li}"
        sact = f"p0act_{li}"
        spe = f"p0pe_{li}"
        sst = f"p0sst_{li}"
        szdve = f"p0zdve_{li}"
        szpe = f"p0zpe_{li}"
        scc = f"cc_ag_{li}"


        def act_n(i, k):
            return 2 * i + k

        def dve_stats_n(i, k):
            return 23 * i + k

        def dve_n(i):
            return 23 * i + 23

        def pe_n(i, j):
            return 16 * i + j

        base_ld = p.val(sld)
        base_act = p.val(sact)
        base_dve = p.val(sdve)
        base_pe = p.val(spe)
        base_st = p.val(sst)
        base_zdve = p.val(szdve)
        base_zpe = p.val(szpe)
        vconst_target = p.val("vconst")

        nbt = NTT // 4

        with nc.Block() as blk:
            @blk.sync
            def _(e):
                e.wait_ge(p.sem("ldc"), p.val("ldc"))
                for i in range(NTT):
                    if i >= 2:
                        e.wait_ge(p.sem(sdve), base_dve + dve_stats_n(i - 2, 7))
                    e.dma_start(out=xt[i % 2][:],
                                in_=x_in.ap()[i * 128:(i + 1) * 128, :]
                                ).then_inc(p.sem(sld), 16)
                    p.bump(sld, 16)

            @blk.scalar
            def _(e):
                for i in range(NTT):
                    e.wait_ge(p.sem(sld), base_ld + 16 * (i + 1))
                    e.activation(sq[:], xt[i % 2][:], AF.Square,
                                 accum_out=st[:, 8 * (i % 2) + 1:8 * (i % 2) + 2]
                                 ).then_inc(p.sem(sact), 1)
                    p.bump(sact, 1)
                    e.wait_ge(p.sem(sdve), base_dve + dve_stats_n(i, 4))
                    e.activation(st[:, 8 * (i % 2) + 5:8 * (i % 2) + 6],
                                 st[:, 8 * (i % 2) + 4:8 * (i % 2) + 5],
                                 AF.Sqrt, bias=g["epsb"][:]).then_inc(p.sem(sact), 1)
                    p.bump(sact, 1)

            @blk.vector
            def _(e):
                e.wait_ge(p.sem("vconst"), vconst_target)
                for i in range(NTT):
                    o = 8 * (i % 2)
                    e.wait_ge(p.sem(sld), base_ld + 16 * (i + 1))
                    e.tensor_reduce(st[:, o:o + 1], xt[i % 2][:], AX.X, ALU.add
                                    ).then_inc(p.sem(sdve), 1)
                    p.bump(sdve, 1)
                    e.wait_ge(p.sem(sdve), base_dve + dve_stats_n(i, 1))
                    e.tensor_scalar(st[:, o + 2:o + 3], st[:, o:o + 1],
                                    1.0 / D, None, ALU.mult).then_inc(p.sem(sdve), 1)
                    p.bump(sdve, 1)
                    e.wait_ge(p.sem(sdve), base_dve + dve_stats_n(i, 2))
                    e.tensor_tensor(st[:, o + 3:o + 4], st[:, o + 2:o + 3],
                                    st[:, o + 2:o + 3], ALU.mult).then_inc(p.sem(sdve), 1)
                    p.bump(sdve, 1)
                    e.wait_ge(p.sem(sact), base_act + act_n(i, 1))
                    e.wait_ge(p.sem(sdve), base_dve + dve_stats_n(i, 3))
                    e.scalar_tensor_tensor(st[:, o + 4:o + 5], st[:, o + 1:o + 2],
                                           1.0 / D, st[:, o + 3:o + 4],
                                           ALU.mult, ALU.subtract).then_inc(p.sem(sdve), 1)
                    p.bump(sdve, 1)
                    e.wait_ge(p.sem(sact), base_act + act_n(i, 2))
                    e.reciprocal(st[:, o + 6:o + 7], st[:, o + 5:o + 6]
                                 ).then_inc(p.sem(sdve), 1)
                    p.bump(sdve, 1)
                    e.wait_ge(p.sem(sdve), base_dve + dve_stats_n(i, 5))
                    e.tensor_scalar(st[:, o + 7:o + 8], st[:, o + 2:o + 3],
                                    st[:, o + 6:o + 7], -1.0, ALU.mult, ALU.mult
                                    ).then_inc(p.sem(sdve), 1)
                    p.bump(sdve, 1)
                    e.wait_ge(p.sem(sdve), base_dve + dve_stats_n(i, 6))
                    e.tensor_scalar(xn[i % 2][:], xt[i % 2][:],
                                    st[:, o + 6:o + 7], st[:, o + 7:o + 8],
                                    ALU.mult, ALU.add).then_inc(p.sem(sdve), 1)
                    p.bump(sdve, 1)
                    for j in range(CD):
                        e.wait_ge(p.sem(spe), base_pe + pe_n(i, j + 1))
                        if li == 0:
                            dst = stage[i // 4][:, j * 512 + (i % 4) * 128:
                                               j * 512 + (i % 4) * 128 + 128]
                        else:
                            dst = z_sb[:, j * TPC + i * 128: j * TPC + i * 128 + 128]
                        e.tensor_copy(dst, pst[j % 4][:]).then_inc(p.sem(sdve), 1)
                        p.bump(sdve, 1)

            @blk.tensor
            def _(e):
                e.wait_ge(p.sem("ldc"), p.val("ldc"))
                for i in range(NTT):
                    e.wait_ge(p.sem(sdve), base_dve + dve_stats_n(i, 7))
                    for j in range(CD):
                        n = 16 * i + j
                        if n >= 4:
                            ip, jp = (n - 4) // 16, (n - 4) % 16
                            e.wait_ge(p.sem(sdve), base_dve + 23 * ip + 7 + jp + 1)
                        e.transpose(pst[j % 4][:], xn[i % 2][:, j * 128:(j + 1) * 128],
                                    g["id_bf16"][:]).then_inc(p.sem(spe), 1)
                        p.bump(spe, 1)

            @blk.gpsimd
            def _(e):
                for bidx in range(nbt):
                    e.wait_ge(p.sem(sdve), base_dve + dve_n(4 * bidx + 3))
                    for j in range(CD):
                        if li == 0:
                            sap = stage[bidx][:, j * 512:(j + 1) * 512]
                        else:
                            sap = z_sb[:, j * TPC + bidx * 512: j * TPC + bidx * 512 + 512]
                        dst = p.dram["ag_in"].ap().rearrange(
                            "(c p) t -> c p t", p=128)[j, :, bidx * 512:(bidx + 1) * 512]
                        e.dma_start(out=dst, in_=sap).then_inc(p.sem(sst), 16)
                        p.bump(sst, 16)
                e.wait_ge(p.sem(sst), base_st + 16 * CD * nbt)
                e.collective_compute(
                    "AllGather", ALU.bypass,
                    replica_groups=[list(range(NCORE))],
                    ins=[p.dram["ag_in"].ap().opt()],
                    outs=[p.dram["ag_out"].ap().rearrange("r c t -> (r c) t").opt()],
                ).then_inc(p.sem(scc), 1)
                p.bump(scc, 1)
                if "xn_full" in p.dbg and li == 0:
                    e.wait_ge(p.sem(scc), 1)
                    e.dma_start(out=p.dram["dbg_xn_full"].ap().rearrange("r c t -> (r c) t"),
                                in_=p.dram["ag_out"].ap().rearrange("r c t -> (r c) t")
                                ).then_inc(p.sem(sst), 16)
                    p.bump(sst, 16)

        if li == 0:
            with nc.Block() as blk:
                @blk.vector
                def _(e):
                    for bidx in range(nbt):
                        e.wait_ge(p.sem(sdve), base_dve + dve_n(4 * bidx + 3))
                        if bidx >= 1:
                            e.wait_ge(p.sem(szpe), base_zpe + bidx * CD)
                        for c in range(CD):
                            ins = e.tensor_scalar(zb[0][:, c * 512:(c + 1) * 512],
                                                  stage[bidx][:, c * 512:(c + 1) * 512],
                                                  g["su_ug"][:, c:c + 1], None, ALU.mult)
                            if c == CD - 1:
                                ins.then_inc(p.sem(szdve), 1)
                                p.bump(szdve, 1)
                        e.wait_ge(p.sem(szdve), base_zdve + bidx * 21 + 1)
                        cur, nxt = 0, 1
                        for si, hstep in enumerate((1, 2, 4, 8)):
                            for c in range(CD):
                                pr = c ^ hstep
                                ca, cb = (c, pr) if (c & hstep) == 0 else (pr, c)
                                op = ALU.add if (c & hstep) == 0 else ALU.subtract
                                ins = e.tensor_tensor(
                                    zb[nxt][:, c * 512:(c + 1) * 512],
                                    zb[cur][:, ca * 512:(ca + 1) * 512],
                                    zb[cur][:, cb * 512:(cb + 1) * 512],
                                    op)
                                if c == CD - 1:
                                    ins.then_inc(p.sem(szdve), 1)
                                    p.bump(szdve, 1)
                            e.wait_ge(p.sem(szdve), base_zdve + bidx * 21 + 2 + si)
                            cur, nxt = nxt, cur
                        for c in range(CD):
                            e.wait_ge(p.sem(szpe), base_zpe + bidx * CD + c + 1)
                            e.tensor_copy(
                                z_sb[:, c * TPC + bidx * 512: c * TPC + bidx * 512 + 512],
                                psz[c % 2][:]).then_inc(p.sem(szdve), 1)
                            p.bump(szdve, 1)

                @blk.tensor
                def _(e):
                    for bidx in range(nbt):
                        e.wait_ge(p.sem(szdve), base_zdve + bidx * 21 + 5)
                        for c in range(CD):
                            if c >= 2:
                                e.wait_ge(p.sem(szdve), base_zdve + bidx * 21 + 5 + (c - 2) + 1)
                            e.matmul(psz[c % 2][:], g["h128"][:],
                                     zb[0][:, c * 512:(c + 1) * 512],
                                     start=True, stop=True, skip_group_check=True
                                     ).then_inc(p.sem(szpe), 1)
                            p.bump(szpe, 1)

        if "z" in p.dbg and li == 0:
            with nc.Block() as blk:
                @blk.gpsimd
                def _(e):
                    e.wait_ge(p.sem(szdve), p.val(szdve))
                    e.dma_start(out=p.dram["dbg_z"].ap(), in_=z_sb[:]).then_inc(
                        p.sem(sst), 16)
                    p.bump(sst, 16)
                    e.wait_ge(p.sem(sst), p.val(sst))

    p.barrier()



OFF_U = D * INTER
OFF_D = 2 * D * INTER


def emit_p1(p, li):
    nc = p.nc
    g = p.g
    z_sb = g["z_sb"]
    pre = f"l{li}_"
    wg_d, wu_d, wd_d = p.dram[pre + "wg"], p.dram[pre + "wu"], p.dram[pre + "wd"]
    wbf = p.dram["wbf"]

    for tb in (0, 1):
        pass0 = (tb == 0)
        with nc.sbuf_tensor(f"h_sb_{li}_{tb}", [128, HC * 512], BF16) as h_sb:
            p.g["h_sb"] = h_sb
            emit_p1_gateup(p, li, tb, pass0)
            p.barrier()
            if li == 0:
                emit_p1_fwht2(p, li, tb)
                p.barrier()
            emit_p1_down(p, li, tb, pass0)
            p.barrier()

    if "ffn_out" in p.dbg:
        with nc.Block() as blk:
            @blk.gpsimd
            def _(e):
                s = p.sem(f"p1dbg_{li}")
                e.dma_start(out=p.dram["dbg_ffn_out"].ap(),
                            in_=p.dram["ffn_out"].ap()).then_inc(s, 16)
                p.bump(f"p1dbg_{li}", 16)
                e.wait_ge(s, p.val(f"p1dbg_{li}"))


def emit_p1_gateup(p, li, tb, pass0):
    nc = p.nc
    g = p.g
    z_sb = g["z_sb"]
    pre = f"l{li}_"
    wg_d, wu_d = p.dram[pre + "wg"], p.dram[pre + "wu"]
    wbf = p.dram["wbf"]

    sld = f"p1ld_{li}_{tb}"
    sldu = f"p1ldu_{li}_{tb}"
    sdve = f"p1dve_{li}_{tb}"
    sact = f"p1act_{li}_{tb}"
    spe = f"p1pe_{li}_{tb}"
    swb = f"p1wb_{li}_{tb}"
    b_ld, b_ldu, b_dve, b_act, b_pe, b_wb = (p.val(sld), p.val(sldu), p.val(sdve),
                                             p.val(sact), p.val(spe), p.val(swb))
    vconst_t = p.val("vconst")
    vo = 2 if pass0 else 1

    h_sb = p.g["h_sb"]
    with contextlib.ExitStack() as ph:
        wstage = [ph.enter_context(nc.sbuf_tensor(f"p1ws{i}_{li}_{tb}", [128, 4096], F32))
                  for i in range(2)] if pass0 else None
        wb = [ph.enter_context(nc.sbuf_tensor(f"p1wb{i}_{li}_{tb}", [128, 4096], BF16))
              for i in range(2)]
        sg = [ph.enter_context(nc.sbuf_tensor(f"p1sg{i}_{li}_{tb}", [128, 512], BF16))
              for i in range(2)]
        psG = [ph.enter_context(nc.psum_tensor(f"p1pg{i}_{li}_{tb}", [128, 512], F32))
               for i in range(2)]
        psU = [ph.enter_context(nc.psum_tensor(f"p1pu{i}_{li}_{tb}", [128, 512], F32))
               for i in range(2)]

        with nc.Block() as blk:
            @blk.sync
            def _(e):
                for hc in range(HC):
                    if pass0:
                        if hc >= 2:
                            e.wait_ge(p.sem(sdve), b_dve + vo * (hc - 2) + 1)
                        e.dma_start(
                            out=wstage[hc % 2][:, 0:2048],
                            in_=wg_d.ap()[hc]
                        ).then_inc(p.sem(sld), 16)
                        p.bump(sld, 16)
                    else:
                        if hc >= 2:
                            e.wait_ge(p.sem(spe), b_pe + 32 * (hc - 1))
                        for half, off in ((0, hc * 262144), (1, OFF_U + hc * 262144)):
                            e.dma_start(
                                out=wb[hc % 2][:, half * 2048:(half + 1) * 2048],
                                in_=wbf.ap()[off:off + 262144].rearrange(
                                    "(q w) -> q w", q=128)
                            ).then_inc(p.sem(sld), 16)
                            p.bump(sld, 16)

            @blk.vector
            def _(e):
                e.wait_ge(p.sem("vconst"), vconst_t)
                for hc in range(HC):
                    if pass0:
                        e.wait_ge(p.sem(sld), b_ld + 16 * (hc + 1))
                        e.wait_ge(p.sem(sldu), b_ldu + 16 * (hc + 1))
                        if hc >= 2:
                            e.wait_ge(p.sem(spe), b_pe + 32 * (hc - 1))
                            e.wait_ge(p.sem(swb), b_wb + 32 * (hc - 1))
                        e.tensor_copy(wb[hc % 2][:], wstage[hc % 2][:]
                                      ).then_inc(p.sem(sdve), 1)
                        p.bump(sdve, 1)
                    e.wait_ge(p.sem(sact), b_act + hc + 1)
                    e.wait_ge(p.sem(spe), b_pe + 32 * (hc + 1))
                    dst = h_sb[:, hc * 512:(hc + 1) * 512]
                    if li == 0:
                        e.scalar_tensor_tensor(dst, sg[hc % 2][:],
                                               g["su_sup"][:, hc:hc + 1],
                                               psU[hc % 2][:],
                                               ALU.mult, ALU.mult
                                               ).then_inc(p.sem(sdve), 1)
                    else:
                        e.tensor_tensor(dst, sg[hc % 2][:], psU[hc % 2][:],
                                        ALU.mult).then_inc(p.sem(sdve), 1)
                    p.bump(sdve, 1)

            @blk.scalar
            def _(e):
                e.wait_ge(p.sem("vconst"), vconst_t)
                for hc in range(HC):
                    if pass0:
                        if hc >= 2:
                            e.wait_ge(p.sem(sdve), b_dve + vo * (hc - 2) + 1)
                        e.dma_start(
                            out=wstage[hc % 2][:, 2048:4096],
                            in_=wu_d.ap()[hc]
                        ).then_inc(p.sem(sldu), 16)
                        p.bump(sldu, 16)
                    e.wait_ge(p.sem(spe), b_pe + 32 * hc + 16)
                    if hc >= 2:
                        e.wait_ge(p.sem(sdve), b_dve + vo * (hc - 2) + vo)
                    if li == 0:
                        e.activation(sg[hc % 2][:], psG[hc % 2][:], AF.Silu,
                                     scale=g["ws_g"][:, hc:hc + 1]
                                     ).then_inc(p.sem(sact), 1)
                    else:
                        e.activation(sg[hc % 2][:], psG[hc % 2][:], AF.Silu
                                     ).then_inc(p.sem(sact), 1)
                    p.bump(sact, 1)

            @blk.tensor
            def _(e):
                for hc in range(HC):
                    if pass0:
                        e.wait_ge(p.sem(sdve), b_dve + vo * hc + 1)
                    else:
                        e.wait_ge(p.sem(sld), b_ld + 32 * (hc + 1))
                    if hc >= 2:
                        e.wait_ge(p.sem(sact), b_act + hc - 1)
                        e.wait_ge(p.sem(sdve), b_dve + vo * (hc - 2) + vo)
                    for cc in range(CD):
                        e.matmul(psG[hc % 2][:], wb[hc % 2][:, cc * 128:(cc + 1) * 128],
                                 z_sb[:, cc * TPC + tb * 512: cc * TPC + tb * 512 + 512],
                                 start=(cc == 0), stop=(cc == CD - 1),
                                 skip_group_check=True).then_inc(p.sem(spe), 1)
                        p.bump(spe, 1)
                    for cc in range(CD):
                        e.matmul(psU[hc % 2][:],
                                 wb[hc % 2][:, 2048 + cc * 128: 2048 + (cc + 1) * 128],
                                 z_sb[:, cc * TPC + tb * 512: cc * TPC + tb * 512 + 512],
                                 start=(cc == 0), stop=(cc == CD - 1),
                                 skip_group_check=True).then_inc(p.sem(spe), 1)
                        p.bump(spe, 1)

            @blk.gpsimd
            def _(e):
                if pass0:
                    for hc in range(HC):
                        e.wait_ge(p.sem(sdve), b_dve + vo * hc + 1)
                        for half, off in ((0, hc * 262144), (1, OFF_U + hc * 262144)):
                            e.dma_start(
                                out=wbf.ap()[off:off + 262144].rearrange(
                                    "(q w) -> q w", q=128),
                                in_=wb[hc % 2][:, half * 2048:(half + 1) * 2048]
                            ).then_inc(p.sem(swb), 16)
                            p.bump(swb, 16)


def emit_p1_fwht2(p, li, tb):
    nc = p.nc
    g = p.g
    h_sb = p.g["h_sb"]
    sdve = f"p1fdve_{li}_{tb}"
    spe = f"p1fpe_{li}_{tb}"
    b_dve, b_pe = p.val(sdve), p.val(spe)

    with contextlib.ExitStack() as ph:
        h2 = ph.enter_context(nc.sbuf_tensor(f"p1h2_{li}_{tb}", [128, HC * 256], BF16))
        psZ2 = [ph.enter_context(nc.psum_tensor(f"p1pz{i}_{li}_{tb}", [128, 512], F32))
                for i in range(2)]

        nst = 0
        with nc.Block() as blk:
            @blk.vector
            def _(e):
                cnt = 0
                for sub in range(2):
                    for si, hstep in enumerate((1, 2, 4, 8, 16, 32)):
                        src_is_h = (si % 2 == 0)
                        for c in range(HC):
                            pr = c ^ hstep
                            ca, cb = (c, pr) if (c & hstep) == 0 else (pr, c)
                            op = ALU.add if (c & hstep) == 0 else ALU.subtract
                            if src_is_h:
                                sa = h_sb[:, ca * 512 + sub * 256: ca * 512 + sub * 256 + 256]
                                sb_ = h_sb[:, cb * 512 + sub * 256: cb * 512 + sub * 256 + 256]
                                dst = h2[:, c * 256:(c + 1) * 256]
                            else:
                                sa = h2[:, ca * 256:(ca + 1) * 256]
                                sb_ = h2[:, cb * 256:(cb + 1) * 256]
                                dst = h_sb[:, c * 512 + sub * 256: c * 512 + sub * 256 + 256]
                            ins = e.tensor_tensor(dst, sa, sb_, op)
                            if c == HC - 1:
                                ins.then_inc(p.sem(sdve), 1)
                                p.bump(sdve, 1)
                                cnt += 1
                                e.wait_ge(p.sem(sdve), b_dve + cnt)
                nevac = 0
                for hc in range(HC):
                    e.wait_ge(p.sem(spe), b_pe + hc + 1)
                    e.tensor_copy(h_sb[:, hc * 512:(hc + 1) * 512],
                                  psZ2[hc % 2][:]).then_inc(p.sem(sdve), 1)
                    p.bump(sdve, 1)

            @blk.tensor
            def _(e):
                e.wait_ge(p.sem(sdve), b_dve + 12)
                for hc in range(HC):
                    if hc >= 2:
                        e.wait_ge(p.sem(sdve), b_dve + 12 + (hc - 2) + 1)
                    e.matmul(psZ2[hc % 2][:], g["h128"][:],
                             h_sb[:, hc * 512:(hc + 1) * 512],
                             start=True, stop=True, skip_group_check=True
                             ).then_inc(p.sem(spe), 1)
                    p.bump(spe, 1)


def emit_p1_down(p, li, tb, pass0):
    nc = p.nc
    g = p.g
    h_sb = p.g["h_sb"]
    pre = f"l{li}_"
    wd_d = p.dram[pre + "wd"]
    wbf = p.dram["wbf"]

    sld = f"p1dld_{li}_{tb}"
    sdve = f"p1ddve_{li}_{tb}"
    spe = f"p1dpe_{li}_{tb}"
    swb = f"p1dwb_{li}_{tb}"
    sst = f"p1dst_{li}_{tb}"
    b_ld, b_dve, b_pe, b_wb, b_st = (p.val(sld), p.val(sdve), p.val(spe),
                                     p.val(swb), p.val(sst))
    vconst_t = p.val("vconst")
    vo = 5 if pass0 else 1

    with contextlib.ExitStack() as ph:
        wdb = [ph.enter_context(nc.sbuf_tensor(f"p1wd{i}_{li}_{tb}", [128, 8192], BF16))
               for i in range(2)]
        wstage = ph.enter_context(nc.sbuf_tensor(f"p1dws_{li}_{tb}", [128, 2048], F32)) \
            if pass0 else None
        dn = [ph.enter_context(nc.sbuf_tensor(f"p1dn{i}_{li}_{tb}", [128, 512], F32))
              for i in range(2)]
        psD = [ph.enter_context(nc.psum_tensor(f"p1pd{i}_{li}_{tb}", [128, 512], F32))
               for i in range(2)]

        with nc.Block() as blk:
            @blk.sync
            def _(e):
                for oc in range(CD):
                    if pass0:
                        for grp in range(4):
                            n = 4 * oc + grp
                            if n >= 1:
                                e.wait_ge(p.sem(sdve), b_dve + vo * ((n - 1) // 4)
                                          + ((n - 1) % 4) + 1)
                            e.dma_start(
                                out=wstage[:],
                                in_=wd_d.ap()[oc, :, 16 * grp:16 * (grp + 1), :]
                            ).then_inc(p.sem(sld), 16)
                            p.bump(sld, 16)
                    else:
                        if oc >= 2:
                            e.wait_ge(p.sem(spe), b_pe + 64 * (oc - 1))
                        off = OFF_D + oc * 1048576
                        e.dma_start(
                            out=wdb[oc % 2][:],
                            in_=wbf.ap()[off:off + 1048576].rearrange(
                                "(q w) -> q w", q=128)
                        ).then_inc(p.sem(sld), 16)
                        p.bump(sld, 16)

            @blk.vector
            def _(e):
                e.wait_ge(p.sem("vconst"), vconst_t)
                for oc in range(CD):
                    if pass0:
                        for grp in range(4):
                            n = 4 * oc + grp
                            e.wait_ge(p.sem(sld), b_ld + 16 * (n + 1))
                            if oc >= 2 and grp == 0:
                                e.wait_ge(p.sem(spe), b_pe + 64 * (oc - 1))
                                e.wait_ge(p.sem(swb), b_wb + 16 * (oc - 1))
                            e.tensor_copy(wdb[oc % 2][:, grp * 2048:(grp + 1) * 2048],
                                          wstage[:]).then_inc(p.sem(sdve), 1)
                            p.bump(sdve, 1)
                    e.wait_ge(p.sem(spe), b_pe + 64 * (oc + 1))
                    if oc >= 2:
                        e.wait_ge(p.sem(sst), b_st + 16 * (oc - 1))
                    if li == 0:
                        e.tensor_scalar(dn[oc % 2][:], psD[oc % 2][:],
                                        g["ws_dp"][:, oc:oc + 1], None, ALU.mult
                                        ).then_inc(p.sem(sdve), 1)
                    else:
                        e.tensor_copy(dn[oc % 2][:], psD[oc % 2][:]
                                      ).then_inc(p.sem(sdve), 1)
                    p.bump(sdve, 1)

            @blk.tensor
            def _(e):
                for oc in range(CD):
                    if oc >= 2:
                        e.wait_ge(p.sem(sdve), b_dve + vo * (oc - 2) + vo)
                    for sc in range(HC):
                        if pass0:
                            grp = sc // 16
                            e.wait_ge(p.sem(sdve), b_dve + vo * oc + grp + 1)
                        elif sc == 0:
                            e.wait_ge(p.sem(sld), b_ld + 16 * (oc + 1))
                        e.matmul(psD[oc % 2][:], wdb[oc % 2][:, sc * 128:(sc + 1) * 128],
                                 h_sb[:, sc * 512:(sc + 1) * 512],
                                 start=(sc == 0), stop=(sc == HC - 1),
                                 skip_group_check=True).then_inc(p.sem(spe), 1)
                        p.bump(spe, 1)

            @blk.gpsimd
            def _(e):
                for oc in range(CD):
                    if pass0:
                        e.wait_ge(p.sem(sdve), b_dve + vo * oc + 4)
                        off = OFF_D + oc * 1048576
                        e.dma_start(
                            out=wbf.ap()[off:off + 1048576].rearrange(
                                "(q w) -> q w", q=128),
                            in_=wdb[oc % 2][:]).then_inc(p.sem(swb), 16)
                        p.bump(swb, 16)
                    e.wait_ge(p.sem(sdve), b_dve + vo * oc + vo)
                    e.dma_start(
                        out=p.dram["ffn_out"].ap().rearrange("(c q) t -> c q t", q=128)
                        [oc, :, tb * 512:(tb + 1) * 512],
                        in_=dn[oc % 2][:]).then_inc(p.sem(sst), 16)
                    p.bump(sst, 16)
                e.wait_ge(p.sem(sst), b_st + 16 * CD)



def emit_final_touch(p, li, x_out):
    if "p4" in p.phases:
        return
    nc = p.nc
    with nc.Block() as blk:
        @blk.gpsimd
        def _(e):
            s = p.sem(f"ft_{li}")
            e.dma_start(out=x_out.ap(), in_=p.dram["x0"].ap()).then_inc(s, 16)
            p.bump(f"ft_{li}", 16)
            e.wait_ge(s, p.val(f"ft_{li}"))


def emit_p2p3(p, li):
    nc = p.nc
    g = p.g
    pre = f"l{li}_"

    with contextlib.ExitStack() as ph:
        Q_sb = ph.enter_context(nc.sbuf_tensor(f"q_sb_{li}", [128, HPC * T], BF16))
        K_sb = ph.enter_context(nc.sbuf_tensor(f"k_sb_{li}", [128, HPC * T], BF16))
        V_tm = ph.enter_context(nc.sbuf_tensor(f"v_tm_{li}", [128, (T // 128) * HPC * HD], BF16))
        maskA = ph.enter_context(nc.sbuf_tensor(f"mA_{li}", [128, 4 * 512], F32))
        maskB = ph.enter_context(nc.sbuf_tensor(f"mB_{li}", [128, 4 * 512], F32))
        emit_p2(p, li, Q_sb, K_sb, V_tm, maskA, maskB)
        p.barrier()
        if "p3" in p.phases:
            emit_p3(p, li, Q_sb, K_sb, V_tm, maskA, maskB)
            p.barrier()


def emit_p2(p, li, Q_sb, K_sb, V_tm, maskA, maskB):
    nc = p.nc
    g = p.g
    pre = f"l{li}_"

    qkvw_d = p.dram[pre + "qkvw"]

    sld = f"p2ld_{li}"
    swl = f"p2wl_{li}"
    sdve = f"p2dve_{li}"
    sact = f"p2act_{li}"
    spe = f"p2pe_{li}"
    b_ld, b_wl, b_dve, b_act, b_pe = (p.val(sld), p.val(swl), p.val(sdve),
                                      p.val(sact), p.val(spe))
    scc = f"cc_ag_{li}"
    cc_t = p.val(scc)
    vconst_t = p.val("vconst")

    if li == 0:
        DVE_TB = 14
        PE_TB = 96 + 8
    else:
        DVE_TB = 2 * 7 + 2 * 3 + 2 * 5
        PE_TB = 96 + 8 + 6

    with contextlib.ExitStack() as ph:
        qkvw = ph.enter_context(nc.sbuf_tensor(f"p2w_{li}", [128, 6 * CD * 128], BF16))
        wstage = ph.enter_context(nc.sbuf_tensor(f"p2ws_{li}", [128, CD * 128], F32))
        xnb = [ph.enter_context(nc.sbuf_tensor(f"p2xn{i}_{li}", [128, CD * 512], BF16))
               for i in range(2)]
        vtmp = ph.enter_context(nc.sbuf_tensor(f"p2vt_{li}", [128, 512], BF16))
        gtmp = ph.enter_context(nc.sbuf_tensor(f"p2gt_{li}", [128, 512], BF16))
        swtmp = ph.enter_context(nc.sbuf_tensor(f"p2sw_{li}", [128, 512], BF16))
        t1tmp = ph.enter_context(nc.sbuf_tensor(f"p2t1_{li}", [128, 512], BF16))
        t2tmp = ph.enter_context(nc.sbuf_tensor(f"p2t2_{li}", [128, 512], BF16))
        roptmp = ph.enter_context(nc.sbuf_tensor(f"p2rp_{li}", [128, 512], BF16))
        ctab = [ph.enter_context(nc.sbuf_tensor(f"p2ct{i}_{li}", [128, 512], BF16))
                for i in range(2)]
        stab = [ph.enter_context(nc.sbuf_tensor(f"p2st{i}_{li}", [128, 512], BF16))
                for i in range(2)]
        psQ = [ph.enter_context(nc.psum_tensor(f"p2pq{i}_{li}", [128, 512], F32))
               for i in range(2)]
        psT = [ph.enter_context(nc.psum_tensor(f"p2pt{i}_{li}", [128, 128], BF16))
               for i in range(2)]
        psR = ph.enter_context(nc.psum_tensor(f"p2pr_{li}", [128, 512], F32)) \
            if li == 1 else None

        with nc.Block() as blk:
            @blk.sync
            def _(e):
                for m in range(4):
                    e.dma_start(out=maskA[:, m * 512:(m + 1) * 512],
                                in_=p.dram["maskA"].ap()[m]).then_inc(p.sem(swl), 16)
                    p.bump(swl, 16)
                    e.dma_start(out=maskB[:, m * 512:(m + 1) * 512],
                                in_=p.dram["maskB"].ap()[m]).then_inc(p.sem(swl), 16)
                    p.bump(swl, 16)
                for s in range(6):
                    if s >= 1:
                        e.wait_ge(p.sem(sdve), b_dve + s)
                    e.dma_start(out=wstage[:], in_=qkvw_d.ap()[s]
                                ).then_inc(p.sem(swl), 16)
                    p.bump(swl, 16)
                e.wait_ge(p.sem(scc), cc_t)
                for tb in range(NTB):
                    if tb >= 2:
                        e.wait_ge(p.sem(spe), b_pe + PE_TB * (tb - 1))
                    r, half = tb // 2, tb % 2
                    e.dma_start(
                        out=xnb[tb % 2][:],
                        in_=p.dram["ag_out"].ap().rearrange(
                            "r (c q) t -> q (r c) t", q=128)
                        [:, r * CD:(r + 1) * CD, half * 512:(half + 1) * 512]
                    ).then_inc(p.sem(sld), 16)
                    p.bump(sld, 16)
                    if li == 1:
                        e.dma_start(out=ctab[tb % 2][:],
                                    in_=p.dram["ropecos"].ap()[:, tb * 512:(tb + 1) * 512]
                                    ).then_inc(p.sem(sld), 16)
                        p.bump(sld, 16)
                        e.dma_start(out=stab[tb % 2][:],
                                    in_=p.dram["ropesin"].ap()[:, tb * 512:(tb + 1) * 512]
                                    ).then_inc(p.sem(sld), 16)
                        p.bump(sld, 16)

            @blk.vector
            def _(e):
                e.wait_ge(p.sem("vconst"), vconst_t)
                for s in range(6):
                    e.wait_ge(p.sem(swl), b_wl + 128 + 16 * (s + 1))
                    e.tensor_copy(qkvw[:, s * 2048:(s + 1) * 2048], wstage[:]
                                  ).then_inc(p.sem(sdve), 1)
                    p.bump(sdve, 1)
                ld_per_tb = 3 if li == 1 else 1
                for tb in range(NTB):
                    dvetb = b_dve + 6 + DVE_TB * tb
                    n = 0

                    def bump1(ins):
                        nonlocal n
                        n += 1
                        ins.then_inc(p.sem(sdve), 1)
                        p.bump(sdve, 1)

                    def wself():
                        e.wait_ge(p.sem(sdve), dvetb + n)

                    col = lambda h: h * T + tb * 512
                    if li == 0:
                        for s in range(6):
                            e.wait_ge(p.sem(spe), b_pe + PE_TB * tb + 16 * (s + 1))
                            ps = psQ[s % 2][:]
                            if s < 2:
                                dst = Q_sb[:, col(s):col(s) + 512]
                                bias = g["bq"][:, s:s + 1]
                            elif s < 4:
                                dst = K_sb[:, col(s - 2):col(s - 2) + 512]
                                bias = g["bk"][:, s - 2:s - 2 + 1]
                            else:
                                dst = vtmp[:]
                                bias = g["bv"][:, s - 4:s - 4 + 1]
                            bump1(e.tensor_scalar(dst, ps, bias, None, ALU.add))
                            if s >= 4:
                                wself()
                                h = s - 4
                                for sub in range(4):
                                    kc = tb * 4 + sub
                                    e.wait_ge(p.sem(spe),
                                              b_pe + PE_TB * tb + 96 + 4 * h + sub + 1)
                                    bump1(e.tensor_copy(
                                        V_tm[:, kc * 256 + h * 128: kc * 256 + h * 128 + 128],
                                        psT[sub % 2][:]))
                    else:
                        pe_off = {0: 16, 1: 34, 2: 51, 3: 69, 4: 86, 5: 102}
                        pe_extra = {0: (17, 18), 2: (52, 53)}
                        pe_rot = {1: 35, 3: 70}
                        for s in range(6):
                            e.wait_ge(p.sem(spe), b_pe + PE_TB * tb + pe_off[s])
                            ps = psQ[s % 2]
                            dstT = Q_sb if s < 2 else (K_sb if s < 4 else None)
                            hcolA = col(0)
                            hcolB = col(1)
                            if s in (0, 2):
                                e.wait_ge(p.sem(sld), b_ld + 16 * ld_per_tb * tb + 48)
                                bump1(e.tensor_copy(gtmp[:], ps[:]))
                                wself()
                                e.wait_ge(p.sem(spe), b_pe + PE_TB * tb + pe_extra[s][0])
                                bump1(e.tensor_copy(swtmp[:], psR[:]))
                                bump1(e.tensor_tensor(t1tmp[:], gtmp[:], ctab[tb % 2][:],
                                                      ALU.mult))
                                wself()
                                bump1(e.tensor_tensor(t2tmp[:], swtmp[:], stab[tb % 2][:],
                                                      ALU.mult))
                                wself()
                                bump1(e.tensor_tensor(roptmp[:], t1tmp[:], t2tmp[:],
                                                      ALU.add))
                                wself()
                                bump1(e.tensor_copy(dstT[0:64, hcolA:hcolA + 512],
                                                    roptmp[0:64, :]))
                                e.wait_ge(p.sem(spe), b_pe + PE_TB * tb + pe_extra[s][1])
                                bump1(e.tensor_copy(dstT[0:64, hcolB:hcolB + 512],
                                                    psR[0:64, :]))
                            elif s in (1, 3):
                                bump1(e.tensor_copy(gtmp[:], ps[:]))
                                wself()
                                e.wait_ge(p.sem(spe), b_pe + PE_TB * tb + pe_rot[s])
                                bump1(e.tensor_copy(dstT[64:128, hcolA:hcolA + 512],
                                                    psR[64:128, :]))
                                bump1(e.tensor_copy(dstT[64:128, hcolB:hcolB + 512],
                                                    gtmp[64:128, :]))
                            else:
                                h = s - 4
                                bump1(e.tensor_copy(vtmp[:], ps[:]))
                                wself()
                                for sub in range(4):
                                    kc = tb * 4 + sub
                                    e.wait_ge(p.sem(spe),
                                              b_pe + PE_TB * tb + 102 + 4 * h + sub + 1)
                                    bump1(e.tensor_copy(
                                        V_tm[:, kc * 256 + h * 128: kc * 256 + h * 128 + 128],
                                        psT[sub % 2][:]))
                    assert n == DVE_TB, (li, tb, n)

            @blk.tensor
            def _(e):
                e.wait_ge(p.sem(sdve), b_dve + 6)
                ld_per_tb = 3 if li == 1 else 1
                for tb in range(NTB):
                    e.wait_ge(p.sem(sld), b_ld + 16 * ld_per_tb * tb + 16)
                    dvetb = b_dve + 6 + DVE_TB * tb
                    for s in range(6):
                        if tb > 0 or s >= 2:
                            if s >= 2:
                                pn = dve_evac_done(li, tb, s - 2)
                            else:
                                pn = dve_evac_done(li, tb - 1, s + 4)
                            e.wait_ge(p.sem(sdve), b_dve + 6 + pn)
                        for cc in range(CD):
                            e.matmul(psQ[s % 2][:],
                                     qkvw[:, s * 2048 + cc * 128: s * 2048 + (cc + 1) * 128],
                                     xnb[tb % 2][:, cc * 512:(cc + 1) * 512],
                                     start=(cc == 0), stop=(cc == CD - 1),
                                     skip_group_check=True).then_inc(p.sem(spe), 1)
                            p.bump(spe, 1)
                        if li == 1 and s in (0, 2):
                            e.wait_ge(p.sem(sdve),
                                      b_dve + 6 + dve_set_start(li, tb, s) + 1)
                            e.matmul(psR[:], g["mswap"][:], gtmp[:],
                                     start=True, stop=True, skip_group_check=True
                                     ).then_inc(p.sem(spe), 1)
                            p.bump(spe, 1)
                            e.wait_ge(p.sem(sdve),
                                      b_dve + 6 + dve_set_start(li, tb, s) + 5)
                            e.matmul(psR[:], g["mrot"][:], roptmp[:],
                                     start=True, stop=True, skip_group_check=True
                                     ).then_inc(p.sem(spe), 1)
                            p.bump(spe, 1)
                        if li == 1 and s in (1, 3):
                            e.wait_ge(p.sem(sdve),
                                      b_dve + 6 + dve_set_start(li, tb, s) + 1)
                            e.matmul(psR[:], g["mrot"][:], gtmp[:],
                                     start=True, stop=True, skip_group_check=True
                                     ).then_inc(p.sem(spe), 1)
                            p.bump(spe, 1)
                    for h in range(2):
                        s = 4 + h
                        e.wait_ge(p.sem(sdve), b_dve + 6 + dve_vtmp_done(li, tb, h))
                        for sub in range(4):
                            n96 = 4 * h + sub
                            if n96 >= 2:
                                e.wait_ge(p.sem(sdve),
                                          b_dve + 6 + dve_pst_done(li, tb, n96 - 2))
                            e.transpose(psT[sub % 2][:],
                                        vtmp[:, sub * 128:(sub + 1) * 128],
                                        g["id_bf16"][:]).then_inc(p.sem(spe), 1)
                            p.bump(spe, 1)

    if "qkv" in p.dbg:
        with nc.Block() as blk:
            @blk.gpsimd
            def _(e):
                s = p.sem(f"p2dbg_{li}")
                e.wait_ge(p.sem(spe), p.val(spe))
                e.wait_ge(p.sem(sdve), p.val(sdve))
                e.dma_start(out=p.dram["dbg_q"].ap(), in_=Q_sb[:]).then_inc(p.sem(f"p2dbg_{li}"), 16)
                p.bump(f"p2dbg_{li}", 16)
                e.dma_start(out=p.dram["dbg_k"].ap(), in_=K_sb[:]).then_inc(p.sem(f"p2dbg_{li}"), 16)
                p.bump(f"p2dbg_{li}", 16)
                e.dma_start(out=p.dram["dbg_v"].ap(), in_=V_tm[:]).then_inc(p.sem(f"p2dbg_{li}"), 16)
                p.bump(f"p2dbg_{li}", 16)
                e.wait_ge(p.sem(f"p2dbg_{li}"), p.val(f"p2dbg_{li}"))


_L1_ENDS = [7, 10, 17, 20, 25, 30]
_L1_STARTS = [0, 7, 10, 17, 20, 25]
_L0_ENDS = [1, 2, 3, 4, 9, 14]
_L0_STARTS = [0, 1, 2, 3, 4, 9]


def dve_evac_done(li, tb, s):
    if tb < 0:
        return 0
    DVE_TB = 14 if li == 0 else 30
    base = DVE_TB * tb
    return base + (_L0_ENDS if li == 0 else _L1_ENDS)[s]


def dve_set_start(li, tb, s):
    DVE_TB = 14 if li == 0 else 30
    return DVE_TB * tb + (_L0_STARTS if li == 0 else _L1_STARTS)[s]


def dve_vtmp_done(li, tb, h):
    return dve_set_start(li, tb, 4 + h) + 1


def dve_pst_done(li, tb, n96):
    h, sub = n96 // 4, n96 % 4
    return dve_set_start(li, tb, 4 + h) + 1 + sub + 1



def emit_p3(p, li, Q_sb, K_sb, V_tm, maskA, maskB):
    nc = p.nc
    g = p.g
    sdve = f"p3dve_{li}"
    sact = f"p3act_{li}"
    spe = f"p3pe_{li}"
    sst = f"p3st_{li}"
    scc2 = f"cc_a2a_{li}"

    with contextlib.ExitStack() as ph:
        probsT = [ph.enter_context(nc.sbuf_tensor(f"p3pr{i}_{li}", [128, 512], BF16))
                  for i in range(2)]
        zr = ph.enter_context(nc.sbuf_tensor(f"p3zr_{li}", [1, 512], F32))
        bc = ph.enter_context(nc.sbuf_tensor(f"p3bc_{li}", [128, 512], F32))
        ybuf = [ph.enter_context(nc.sbuf_tensor(f"p3y{i}_{li}", [128, 512], BF16))
                for i in range(2)]
        psB = [ph.enter_context(nc.psum_tensor(f"p3pb{i}_{li}", [128, 512], F32))
               for i in range(3)]
        psZ = ph.enter_context(nc.psum_tensor(f"p3pz_{li}", [1, 512], F32))
        psPV = ph.enter_context(nc.psum_tensor(f"p3pv_{li}", [128, 512], F32))
        psBC = ph.enter_context(nc.psum_tensor(f"p3pbc_{li}", [128, 512], F32))

        import os
        blocks = [(h, B) for h in range(HPC) for B in range(NQB)]
        nblk = int(os.environ.get("P3_BLOCKS", "0"))
        if nblk:
            blocks = blocks[:nblk]

        def b_cnt(B):
            return 4 * B + 4

        pe_c = [0]
        dve_c = [0]
        act_c = [0]
        st_c = [0]
        for (h, B) in blocks:
            pe_c.append(pe_c[-1] + 3 * b_cnt(B) + 2)
            dve_c.append(dve_c[-1] + 7)
            act_c.append(act_c[-1] + b_cnt(B))
            st_c.append(st_c[-1] + 16)
        b_pe, b_dve, b_act, b_st = p.val(spe), p.val(sdve), p.val(sact), p.val(sst)

        def mm1_at(j):
            return 1 if j == 0 else 3 * j - 1

        def pv_at(B, j):
            nb = b_cnt(B)
            return 3 * (j + 1) + 1 if j < nb - 1 else 3 * nb

        def ztail_at(B):
            return 3 * b_cnt(B) - 1

        def qcol(h, B):
            return h * T + B * 512

        with nc.Block() as blk:
            @blk.tensor
            def _(e):
                for bi, (h, B) in enumerate(blocks):
                    npe = b_pe + pe_c[bi]
                    ndve = b_dve + dve_c[bi]
                    nact = b_act + act_c[bi]
                    nb = b_cnt(B)
                    if bi > 0:
                        e.wait_ge(p.sem(sdve), b_dve + dve_c[bi])
                    for j in range(nb):
                        if j >= 3:
                            e.wait_ge(p.sem(sact), nact + j - 2)
                        e.matmul(psB[j % 3][:],
                                 K_sb[:, h * T + 128 * j: h * T + 128 * j + 128],
                                 Q_sb[:, qcol(h, B): qcol(h, B) + 512],
                                 start=True, stop=True, skip_group_check=True
                                 ).then_inc(p.sem(spe), 1)
                        p.bump(spe, 1)
                        if j >= 1:
                            jj = j - 1
                            e.wait_ge(p.sem(sact), nact + jj + 1)
                            e.matmul(psZ[:], g["ones_128x1"][:], probsT[jj % 2][:],
                                     start=(jj == 0), stop=False,
                                     skip_group_check=True).then_inc(p.sem(spe), 1)
                            p.bump(spe, 1)
                            e.matmul(psPV[:],
                                     V_tm[:, jj * 256 + h * 128: jj * 256 + h * 128 + 128],
                                     probsT[jj % 2][:],
                                     start=(jj == 0), stop=False,
                                     skip_group_check=True).then_inc(p.sem(spe), 1)
                            p.bump(spe, 1)
                    jj = nb - 1
                    e.wait_ge(p.sem(sact), nact + jj + 1)
                    e.matmul(psZ[:], g["ones_128x1"][:], probsT[jj % 2][:],
                             start=(jj == 0), stop=True,
                             skip_group_check=True).then_inc(p.sem(spe), 1)
                    p.bump(spe, 1)
                    e.matmul(psPV[:],
                             V_tm[:, jj * 256 + h * 128: jj * 256 + h * 128 + 128],
                             probsT[jj % 2][:],
                             start=(jj == 0), stop=True,
                             skip_group_check=True).then_inc(p.sem(spe), 1)
                    p.bump(spe, 1)
                    e.wait_ge(p.sem(sdve), ndve + 5)
                    e.matmul(psBC[:], g["ones_1x128_f32"][:], zr[:],
                             start=True, stop=True, skip_group_check=True
                             ).then_inc(p.sem(spe), 1)
                    p.bump(spe, 1)
                    e.matmul(psBC[:], g["ones_1x128_f32"][:], zr[:],
                             start=True, stop=True, skip_group_check=True
                             ).then_inc(p.sem(spe), 1)
                    p.bump(spe, 1)
                    assert p.val(spe) == b_pe + pe_c[bi + 1], (bi, p.val(spe))

            @blk.scalar
            def _(e):
                for bi, (h, B) in enumerate(blocks):
                    npe = b_pe + pe_c[bi]
                    ndve = b_dve + dve_c[bi]
                    nb = b_cnt(B)
                    for j in range(nb):
                        if j >= 4 * B:
                            e.wait_ge(p.sem(sdve), ndve + (j - 4 * B) + 1)
                        else:
                            e.wait_ge(p.sem(spe), npe + mm1_at(j))
                        if j >= 2:
                            e.wait_ge(p.sem(spe), npe + pv_at(B, j - 2))
                        elif bi > 0:
                            e.wait_ge(p.sem(spe), b_pe + pe_c[bi])
                        e.activation(probsT[j % 2][:], psB[j % 3][:], AF.Exp,
                                     scale=SC).then_inc(p.sem(sact), 1)
                        p.bump(sact, 1)

            @blk.vector
            def _(e):
                for bi, (h, B) in enumerate(blocks):
                    npe = b_pe + pe_c[bi]
                    ndve = b_dve + dve_c[bi]
                    nact = b_act + act_c[bi]
                    nb = b_cnt(B)
                    for m in range(4):
                        j = 4 * B + m
                        e.wait_ge(p.sem(spe), npe + mm1_at(j))
                        e.tensor_tensor(psB[j % 3][:], psB[j % 3][:],
                                        maskB[:, 512 * m: 512 * m + 512], ALU.add
                                        ).then_inc(p.sem(sdve), 1)
                        p.bump(sdve, 1)
                    e.wait_ge(p.sem(spe), npe + ztail_at(B) + 1)
                    e.reciprocal(zr[:], psZ[:]).then_inc(p.sem(sdve), 1)
                    p.bump(sdve, 1)
                    e.wait_ge(p.sem(spe), b_pe + pe_c[bi + 1])
                    e.tensor_copy(bc[:], psBC[:]).then_inc(p.sem(sdve), 1)
                    p.bump(sdve, 1)
                    e.wait_ge(p.sem(sdve), ndve + 6)
                    if bi >= 2:
                        e.wait_ge(p.sem(sst), b_st + st_c[bi - 1])
                    e.tensor_tensor(ybuf[bi % 2][:], psPV[:], bc[:], ALU.mult
                                    ).then_inc(p.sem(sdve), 1)
                    p.bump(sdve, 1)

            @blk.gpsimd
            def _(e):
                for bi, (h, B) in enumerate(blocks):
                    e.wait_ge(p.sem(sdve), b_dve + dve_c[bi + 1])
                    e.dma_start(
                        out=p.dram["a2a_in"].ap()[B // 2,
                                                  h * 128:(h + 1) * 128,
                                                  (B % 2) * 512:(B % 2) * 512 + 512],
                        in_=ybuf[bi % 2][:]).then_inc(p.sem(sst), 16)
                    p.bump(sst, 16)
                e.wait_ge(p.sem(sst), b_st + st_c[-1])
                with allow_ici_all_to_all():
                    e.collective_compute(
                        "AllToAll",
                        ALU.bypass,
                        replica_groups=[list(range(NCORE))],
                        ins=[p.dram["a2a_in"].ap().rearrange("r x t -> (r x) t").opt()],
                        outs=[p.dram["a2a_out"].ap().rearrange("r x t -> (r x) t").opt()],
                    ).then_inc(p.sem(scc2), 1)
                p.bump(scc2, 1)
                if "yattn" in p.dbg:
                    e.wait_ge(p.sem(scc2), p.val(scc2))
                    e.dma_start(out=p.dram["dbg_yattn"].ap().rearrange("r x t -> (r x) t"),
                                in_=p.dram["a2a_out"].ap().rearrange("r x t -> (r x) t")
                                ).then_inc(p.sem(sst), 16)
                    p.bump(sst, 16)
                    e.wait_ge(p.sem(sst), p.val(sst))



def emit_p4(p, li, x_in, x_out):
    nc = p.nc
    g = p.g
    pre = f"l{li}_"
    wo_d = p.dram[pre + "wo"]

    sld = f"p4ld_{li}"
    swl = f"p4wl_{li}"
    sdve = f"p4dve_{li}"
    spe = f"p4pe_{li}"
    sst = f"p4st_{li}"
    b_ld, b_wl, b_dve, b_pe, b_st = (p.val(sld), p.val(swl), p.val(sdve),
                                     p.val(spe), p.val(sst))
    scc2 = f"cc_a2a_{li}"
    cc2_t = p.val(scc2)

    with contextlib.ExitStack() as ph:
        yab = ph.enter_context(nc.sbuf_tensor(f"p4y_{li}", [128, CD * 512], BF16))
        fm = ph.enter_context(nc.sbuf_tensor(f"p4fm_{li}", [128, CD * 512], F32))
        wostage = ph.enter_context(nc.sbuf_tensor(f"p4ws_{li}", [128, 2048], F32))
        wob = [ph.enter_context(nc.sbuf_tensor(f"p4wb{i}_{li}", [128, 2048], BF16))
               for i in range(2)]
        ffnb = [ph.enter_context(nc.sbuf_tensor(f"p4fb{i}_{li}", [128, 512], F32))
                for i in range(2)]
        xold = [ph.enter_context(nc.sbuf_tensor(f"p4xo{i}_{li}", [128, D], F32))
                for i in range(2)]
        xnew = [ph.enter_context(nc.sbuf_tensor(f"p4xn{i}_{li}", [128, D], F32))
                for i in range(2)]
        psO = [ph.enter_context(nc.psum_tensor(f"p4po{i}_{li}", [128, 512], F32))
               for i in range(2)]
        psTr = [ph.enter_context(nc.psum_tensor(f"p4pt{i}_{li}", [128, 128], F32))
                for i in range(2)]

        PE_TB = 256 + 64
        DVE_TB = 32 + 64

        with nc.Block() as blk:
            @blk.sync
            def _(e):
                e.wait_ge(p.sem(scc2), cc2_t)
                for tb in range(2):
                    if tb >= 1:
                        e.wait_ge(p.sem(spe), b_pe + PE_TB * tb)
                    e.dma_start(
                        out=yab[:],
                        in_=p.dram["a2a_out"].ap().rearrange(
                            "r (q x) t -> x (r q) t", x=128)
                        [:, :, tb * 512:(tb + 1) * 512]).then_inc(p.sem(sld), 16)
                    p.bump(sld, 16)
                    for oc in range(CD):
                        if tb > 0 or oc >= 2:
                            prev = (tb, oc - 2) if oc >= 2 else (tb - 1, CD - 2)
                            e.wait_ge(p.sem(sdve),
                                      b_dve + DVE_TB * prev[0] + 2 * prev[1] + 2)
                        e.dma_start(
                            out=ffnb[oc % 2][:],
                            in_=p.dram["ffn_out"].ap().rearrange(
                                "(c q) t -> c q t", q=128)
                            [oc, :, tb * 512:(tb + 1) * 512]).then_inc(p.sem(sld), 16)
                        p.bump(sld, 16)
                    for tt in range(4):
                        if tb > 0 or tt >= 2:
                            prev = (tb, tt - 2) if tt >= 2 else (tb - 1, 2)
                            e.wait_ge(p.sem(sdve),
                                      b_dve + DVE_TB * prev[0] + 32 + 16 * (prev[1] + 1))
                        i = tb * 4 + tt
                        e.dma_start(out=xold[tt % 2][:],
                                    in_=x_in.ap()[i * 128:(i + 1) * 128, :]
                                    ).then_inc(p.sem(sld), 16)
                        p.bump(sld, 16)

            @blk.gpsimd
            def _(e):
                for tb in range(2):
                    for oc in range(CD):
                        n = tb * CD + oc
                        if n >= 1:
                            pt, po = (n - 1) // CD, (n - 1) % CD
                            e.wait_ge(p.sem(sdve), b_dve + DVE_TB * pt + 2 * po + 1)
                        if n >= 2:
                            pt, po = (n - 2) // CD, (n - 2) % CD
                            e.wait_ge(p.sem(spe), b_pe + PE_TB * pt + 16 * (po + 1))
                        e.dma_start(
                            out=wostage[:],
                            in_=wo_d.ap()[oc]).then_inc(p.sem(swl), 16)
                        p.bump(swl, 16)
                e.wait_ge(p.sem(sst), b_st + 16 * 8)
                if li == 0 and "x1" in p.dbg:
                    s = p.sem(f"p4dbg_{li}")
                    e.dma_start(out=p.dram["dbg_x1"].ap(), in_=x_out.ap()
                                ).then_inc(s, 16)
                    p.bump(f"p4dbg_{li}", 16)
                    e.wait_ge(s, p.val(f"p4dbg_{li}"))

            @blk.vector
            def _(e):
                for tb in range(2):
                    for oc in range(CD):
                        n = tb * CD + oc
                        e.wait_ge(p.sem(swl), b_wl + 16 * (n + 1))
                        e.tensor_copy(wob[n % 2][:], wostage[:]).then_inc(p.sem(sdve), 1)
                        p.bump(sdve, 1)
                        e.wait_ge(p.sem(spe), b_pe + PE_TB * tb + 16 * (oc + 1))
                        e.wait_ge(p.sem(sld), b_ld + 16 * (21 * tb + 1 + oc + 1))
                        if li == 0:
                            e.scalar_tensor_tensor(
                                fm[:, oc * 512:(oc + 1) * 512], psO[oc % 2][:],
                                g["bo"][:, oc:oc + 1], ffnb[oc % 2][:],
                                ALU.add, ALU.add).then_inc(p.sem(sdve), 1)
                        else:
                            e.tensor_tensor(
                                fm[:, oc * 512:(oc + 1) * 512], psO[oc % 2][:],
                                ffnb[oc % 2][:], ALU.add).then_inc(p.sem(sdve), 1)
                        p.bump(sdve, 1)
                    for tt in range(4):
                        e.wait_ge(p.sem(sld), b_ld + 16 * (21 * tb + 17 + tt + 1))
                        i = tb * 4 + tt
                        if i >= 2:
                            e.wait_ge(p.sem(sst), b_st + 16 * (i - 1))
                        for oc in range(CD):
                            ntr = 16 * tt + oc
                            e.wait_ge(p.sem(spe),
                                      b_pe + PE_TB * tb + 256 + ntr + 1)
                            e.tensor_tensor(
                                xnew[tt % 2][:, oc * 128:(oc + 1) * 128],
                                psTr[ntr % 2][:],
                                xold[tt % 2][:, oc * 128:(oc + 1) * 128],
                                ALU.add).then_inc(p.sem(sdve), 1)
                            p.bump(sdve, 1)


            @blk.scalar
            def _(e):
                for tb in range(2):
                    for tt in range(4):
                        i = tb * 4 + tt
                        e.wait_ge(p.sem(sdve),
                                  b_dve + DVE_TB * tb + 32 + 16 * (tt + 1))
                        if i >= 2:
                            e.wait_ge(p.sem(sst), b_st + 16 * (i - 1))
                        e.dma_start(out=x_out.ap()[i * 128:(i + 1) * 128, :],
                                    in_=xnew[tt % 2][:]).then_inc(p.sem(sst), 16)
                        p.bump(sst, 16)

            @blk.tensor
            def _(e):
                for tb in range(2):
                    e.wait_ge(p.sem(sld), b_ld + 16 * (21 * tb + 1))
                    for oc in range(CD):
                        n = tb * CD + oc
                        e.wait_ge(p.sem(sdve), b_dve + DVE_TB * tb + 2 * oc + 1)
                        if n >= 2:
                            pt, po = (n - 2) // CD, (n - 2) % CD
                            e.wait_ge(p.sem(sdve), b_dve + DVE_TB * pt + 2 * po + 2)
                        for c in range(CD):
                            e.matmul(psO[oc % 2][:],
                                     wob[n % 2][:, c * 128:(c + 1) * 128],
                                     yab[:, c * 512:(c + 1) * 512],
                                     start=(c == 0), stop=(c == CD - 1),
                                     skip_group_check=True).then_inc(p.sem(spe), 1)
                            p.bump(spe, 1)
                    for tt in range(4):
                        for oc in range(CD):
                            ntr = 16 * tt + oc
                            e.wait_ge(p.sem(sdve), b_dve + DVE_TB * tb + 2 * oc + 2)
                            if ntr >= 2:
                                e.wait_ge(p.sem(sdve),
                                          b_dve + DVE_TB * tb + 32 + ntr - 1)
                            e.transpose(psTr[ntr % 2][:],
                                        fm[:, oc * 512 + tt * 128: oc * 512 + tt * 128 + 128],
                                        g["id_f32"][:]).then_inc(p.sem(spe), 1)
                            p.bump(spe, 1)


def kernel(**inp):
    prog = build()
    in_maps = [prep_inputs(inp, c) for c in range(NCORE)]
    res = run_bass_kernel_spmd(prog.nc, in_maps, core_ids=list(range(NCORE)))
    out = np.concatenate([res.results[c]["y"] for c in range(NCORE)], axis=0)
    return out.reshape(1, T, D).astype(np.float32)


# version 06 (final)
# speedup vs baseline: 1.1912x; 1.0038x over previous
import contextlib
import math

import numpy as np
import ml_dtypes

import concourse.bass as bass
import concourse.mybir as mybir
from concourse.bass_utils import run_bass_kernel_spmd
from concourse.aot_env import allow_ici_all_to_all

F32 = mybir.dt.float32
BF16 = mybir.dt.bfloat16
AF = mybir.ActivationFunctionType
ALU = mybir.AluOpType
AX = mybir.AxisListType

NCORE = 8
T = 8192
TPC = T // NCORE
D = 2048
NH = 16
HPC = NH // NCORE
HD = 128
INTER = 8192
CD = D // 128
HC = INTER // 128
NEG = -1.0e30
SC = 1.0 / math.sqrt(128.0)
EPS = 1e-5
NTT = TPC // 128
NTB = T // 512
NQB = NTB



def host_constants():
    c = {}
    c["id_bf16"] = np.eye(128, dtype=ml_dtypes.bfloat16)
    c["id_f32"] = np.eye(128, dtype=np.float32)
    c["ones_1x128"] = np.ones((1, 128), dtype=ml_dtypes.bfloat16)
    c["ones_128x1"] = np.ones((128, 1), dtype=ml_dtypes.bfloat16)
    c["ones_1x128_f32"] = np.ones((1, 128), dtype=np.float32)
    c["epsb"] = np.full((128, 1), 1e-5, dtype=np.float32)
    r = np.arange(128)[:, None]
    col = np.arange(512)[None, :]
    mA = np.stack([(col <= 128 * m + r) for m in range(4)])
    c["maskA"] = np.where(mA, 0.0, NEG).astype(np.float32)
    mB = np.stack([(col >= 128 * m + r) for m in range(4)])
    c["maskB"] = np.where(mB, 0.0, NEG).astype(np.float32)
    base = 10000.0
    inv = 1.0 / base ** (np.arange(0, 64, 2, dtype=np.float64) / 64)
    fr = np.outer(inv, np.arange(T, dtype=np.float64))
    c["ropecos"] = np.tile(np.cos(fr), (4, 1)).astype(ml_dtypes.bfloat16)
    sin = np.tile(np.sin(fr), (4, 1))
    sign = np.repeat(np.array([-1.0, 1.0, -1.0, 1.0]), 32)[:, None]
    c["ropesin"] = (sin * sign).astype(ml_dtypes.bfloat16)
    mswap = np.zeros((128, 128), dtype=np.float32)
    for i in range(128):
        mswap[i ^ 32, i] = 1.0
    c["mswap"] = mswap.astype(ml_dtypes.bfloat16)
    mrot = np.zeros((128, 128), dtype=np.float32)
    for i in range(128):
        mrot[(i + 64) % 128, i] = 1.0
    c["mrot"] = mrot.astype(ml_dtypes.bfloat16)
    h = np.array([[1.0]])
    while h.shape[0] < 128:
        h = np.block([[h, h], [h, -h]])
    c["h128"] = h.astype(ml_dtypes.bfloat16)
    return c


_CONSTS = None


def prep_inputs(inp, core):
    global _CONSTS
    d = {}
    h0 = HPC * core
    cc = np.ascontiguousarray
    asnp = np.asarray

    def tile4(w):
        O, I = w.shape
        return cc(w.reshape(O // 128, 128, I // 128, 128).transpose(0, 3, 2, 1))

    def settile(w):
        I, DO = w.shape
        return w.reshape(I // 128, 128, DO).transpose(1, 0, 2)

    wq = asnp(inp["L0_attn_Wq"])[h0:h0 + HPC]
    wk = asnp(inp["L0_attn_Wk"])[h0:h0 + HPC]
    wv = asnp(inp["L0_attn_Wv"])[h0:h0 + HPC]
    d["l0_qkvw"] = cc(np.stack([settile(wq[0]), settile(wq[1]),
                                settile(wk[0]), settile(wk[1]),
                                settile(wv[0]), settile(wv[1])]))
    d["l0_bq"] = cc(asnp(inp["L0_attn_bq"])[h0:h0 + HPC].T)
    d["l0_bk"] = cc(asnp(inp["L0_attn_bk"])[h0:h0 + HPC].T)
    d["l0_bv"] = cc(asnp(inp["L0_attn_bv"])[h0:h0 + HPC].T)
    wo0 = asnp(inp["L0_attn_Wo"])
    d["l0_wo"] = cc(wo0.reshape(CD, 128, CD, 128).transpose(2, 1, 0, 3))
    d["l0_bo"] = cc(asnp(inp["L0_attn_bo"]).reshape(CD, 128).T)
    d["l0_wg"] = tile4(asnp(inp["L0_ffn_W_gate"]))
    d["l0_wu"] = tile4(asnp(inp["L0_ffn_W_up"]))
    d["l0_wd"] = tile4(asnp(inp["L0_ffn_W_down"]))
    d["l0_su_ug"] = cc(asnp(inp["L0_ffn_SU_ug"]).reshape(CD, 128).T)
    d["l0_su_dp"] = cc(asnp(inp["L0_ffn_SU_dp"]).reshape(HC, 128).T)
    d["l0_ws_up"] = cc(asnp(inp["L0_ffn_Wscale_ug"])[:INTER].reshape(HC, 128).T)
    d["l0_ws_g"] = cc(asnp(inp["L0_ffn_Wscale_ug"])[INTER:].reshape(HC, 128).T)
    d["l0_ws_dp"] = cc(asnp(inp["L0_ffn_Wscale_dp"]).reshape(CD, 128).T)

    sl = slice(h0 * 64, (h0 + HPC) * 64)
    slv = slice(h0 * HD, (h0 + HPC) * HD)
    wqg = settile(asnp(inp["L1_attn_Wq_geo"])[sl].T)
    wqs = settile(asnp(inp["L1_attn_Wq_sem"])[sl].T)
    wkg = settile(asnp(inp["L1_attn_Wk_geo"])[sl].T)
    wks = settile(asnp(inp["L1_attn_Wk_sem"])[sl].T)
    wv1 = settile(asnp(inp["L1_attn_Wv"])[slv].T)
    d["l1_qkvw"] = cc(np.stack([wqg, wqs, wkg, wks,
                                wv1[:, :, 0:128], wv1[:, :, 128:256]]))
    wo1 = asnp(inp["L1_attn_Wo"])
    d["l1_wo"] = cc(wo1.reshape(CD, 128, CD, 128).transpose(0, 3, 2, 1))
    d["l1_wg"] = tile4(asnp(inp["L1_ffn_w_gate"]))
    d["l1_wu"] = tile4(asnp(inp["L1_ffn_w_up"]))
    d["l1_wd"] = tile4(asnp(inp["L1_ffn_w_down"]))

    x = asnp(inp["L0_attn_x"]).reshape(T, D)
    d["x0"] = cc(x[core * TPC:(core + 1) * TPC]).astype(np.float32)
    if _CONSTS is None:
        _CONSTS = host_constants()
    d.update(_CONSTS)
    return d



class Prog:
    def __init__(self, layers=(0, 1), dbg=(), phases=("p0", "p1", "p2", "p3", "p4")):
        self.nc = bass.Bass("TRN2", target_bir_lowering=False)
        self.ctx = contextlib.ExitStack()
        self.dbg = set(dbg)
        self.phases = set(phases)
        self.layers = layers
        self.dram = {}
        self.g = {}
        self.semv = {}

    def sem(self, name):
        if name not in self.semv:
            h = self.ctx.enter_context(self.nc.semaphore(name))
            self.semv[name] = [h, 0]
        return self.semv[name][0]

    def bump(self, name, k=1):
        self.sem(name)
        self.semv[name][1] += k
        return self.semv[name][1]

    def val(self, name):
        self.sem(name)
        return self.semv[name][1]

    def sbuf(self, name, shape, dtype):
        return self.ctx.enter_context(self.nc.sbuf_tensor(name, shape, dtype))

    def psum(self, name, shape, dtype):
        return self.ctx.enter_context(self.nc.psum_tensor(name, shape, dtype))

    def din(self, name, shape, dtype=F32):
        t = self.nc.dram_tensor(name, shape, dtype, kind="ExternalInput")
        self.dram[name] = t
        return t

    def dout(self, name, shape, dtype=F32):
        t = self.nc.dram_tensor(name, shape, dtype, kind="ExternalOutput")
        self.dram[name] = t
        return t

    def dint(self, name, shape, dtype=F32):
        t = self.nc.dram_tensor(name, shape, dtype)
        self.dram[name] = t
        return t

    def barrier(self):
        nc = self.nc
        s = self.sem("bar")
        base = self.semv["bar"][1]
        with nc.Block() as blk:
            @blk.sync
            def _(e):
                e.nop().then_inc(s, 1)
                e.wait_ge(s, base + 5)

            @blk.tensor
            def _(e):
                e.nop().then_inc(s, 1)
                e.wait_ge(s, base + 5)

            @blk.vector
            def _(e):
                e.nop().then_inc(s, 1)
                e.wait_ge(s, base + 5)

            @blk.scalar
            def _(e):
                e.nop().then_inc(s, 1)
                e.wait_ge(s, base + 5)

            @blk.gpsimd
            def _(e):
                e.nop().then_inc(s, 1)
                e.wait_ge(s, base + 5)
        self.semv["bar"][1] = base + 5


def build(layers=(0, 1), dbg=(), phases=("p0", "p1", "p2", "p3", "p4")):
    p = Prog(layers, dbg, phases)
    nc = p.nc

    p.din("l0_qkvw", [6, 128, CD, 128])
    p.din("l0_bq", [128, HPC]); p.din("l0_bk", [128, HPC]); p.din("l0_bv", [128, HPC])
    p.din("l0_wo", [CD, 128, CD, 128]); p.din("l0_bo", [128, CD])
    p.din("l0_wg", [HC, 128, CD, 128]); p.din("l0_wu", [HC, 128, CD, 128])
    p.din("l0_wd", [CD, 128, HC, 128])
    p.din("l0_su_ug", [128, CD]); p.din("l0_su_dp", [128, HC])
    p.din("l0_ws_up", [128, HC]); p.din("l0_ws_g", [128, HC]); p.din("l0_ws_dp", [128, CD])
    p.din("l1_qkvw", [6, 128, CD, 128])
    p.din("l1_wo", [CD, 128, CD, 128])
    p.din("l1_wg", [HC, 128, CD, 128]); p.din("l1_wu", [HC, 128, CD, 128])
    p.din("l1_wd", [CD, 128, HC, 128])
    p.din("x0", [TPC, D])
    p.din("id_bf16", [128, 128], BF16); p.din("id_f32", [128, 128], F32)
    p.din("ones_1x128", [1, 128], BF16); p.din("ones_128x1", [128, 1], BF16)
    p.din("ones_1x128_f32", [1, 128], F32)
    p.din("maskA", [4, 128, 512]); p.din("maskB", [4, 128, 512])
    p.din("ropecos", [128, T], BF16); p.din("ropesin", [128, T], BF16)
    p.din("h128", [128, 128], BF16); p.din("epsb", [128, 1])
    p.din("mswap", [128, 128], BF16); p.din("mrot", [128, 128], BF16)

    p.dout("y", [TPC, D])
    p.dint("x1", [TPC, D])
    p.dint("ag_in", [D, TPC], BF16)
    p.dint("ag_out", [NCORE, D, TPC], BF16)
    p.dint("a2a_in", [NCORE, HPC * HD, TPC], BF16)
    p.dint("a2a_out", [NCORE, HPC * HD, TPC], BF16)
    p.dint("ffn_out", [D, TPC], F32)
    p.dint("wbf", [3 * D * INTER], BF16)

    if "xn_full" in p.dbg:
        p.dout("dbg_xn_full", [NCORE, D, TPC], BF16)
    if "z" in p.dbg:
        p.dout("dbg_z", [128, CD * TPC], BF16)
    if "ffn_out" in p.dbg:
        p.dout("dbg_ffn_out", [D, TPC], F32)
    if "qkv" in p.dbg:
        p.dout("dbg_q", [128, HPC * T], BF16)
        p.dout("dbg_k", [128, HPC * T], BF16)
        p.dout("dbg_v", [128, (T // 128) * HPC * HD], BF16)
    if "yattn" in p.dbg:
        p.dout("dbg_yattn", [NCORE, HPC * HD, TPC], BF16)
    if "x1" in p.dbg:
        p.dout("dbg_x1", [TPC, D])

    g = p.g
    g["id_bf16"] = p.sbuf("g_id_bf16", [128, 128], BF16)
    g["id_f32"] = p.sbuf("g_id_f32", [128, 128], F32)
    g["ones_1x128"] = p.sbuf("g_ones_1x128", [1, 128], BF16)
    g["ones_128x1"] = p.sbuf("g_ones_128x1", [128, 1], BF16)
    g["ones_1x128_f32"] = p.sbuf("g_ones_1x128_f32", [1, 128], F32)
    g["h128"] = p.sbuf("g_h128", [128, 128], BF16)
    g["epsb"] = p.sbuf("g_epsb", [128, 1], F32)
    g["mswap"] = p.sbuf("g_mswap", [128, 128], BF16)
    g["mrot"] = p.sbuf("g_mrot", [128, 128], BF16)
    g["su_ug"] = p.sbuf("g_su_ug", [128, CD], F32)
    g["su_sup"] = p.sbuf("g_su_sup", [128, HC], F32)
    g["ws_g"] = p.sbuf("g_ws_g", [128, HC], F32)
    g["ws_dp"] = p.sbuf("g_ws_dp", [128, CD], F32)
    g["bq"] = p.sbuf("g_bq", [128, HPC], F32)
    g["bk"] = p.sbuf("g_bk", [128, HPC], F32)
    g["bv"] = p.sbuf("g_bv", [128, HPC], F32)
    g["bo"] = p.sbuf("g_bo", [128, CD], F32)

    with nc.Block() as blk:
        @blk.sync
        def _(e):
            for name in ("id_bf16", "id_f32", "ones_1x128", "ones_128x1",
                         "ones_1x128_f32", "h128", "epsb", "mswap", "mrot"):
                e.dma_start(out=g[name][:], in_=p.dram[name].ap()).then_inc(p.sem("ldc"), 16)
                p.bump("ldc", 16)
            for src, dst in (("l0_su_ug", "su_ug"), ("l0_ws_g", "ws_g"),
                             ("l0_ws_dp", "ws_dp"), ("l0_bq", "bq"),
                             ("l0_bk", "bk"), ("l0_bv", "bv"), ("l0_bo", "bo")):
                e.dma_start(out=g[dst][:], in_=p.dram[src].ap()).then_inc(p.sem("ldc"), 16)
                p.bump("ldc", 16)
            e.dma_start(out=g["su_sup"][:], in_=p.dram["l0_su_dp"].ap()).then_inc(p.sem("ldc"), 16)
            p.bump("ldc", 16)

        @blk.vector
        def _(e):
            e.wait_ge(p.sem("ldc"), p.val("ldc"))
            e.tensor_scalar(g["ws_g"][:], g["ws_g"][:], 1.0 / math.sqrt(D), None, ALU.mult)
            e.tensor_scalar(g["ws_dp"][:], g["ws_dp"][:], 1.0 / math.sqrt(INTER), None, ALU.mult)
            e.nop().then_inc(p.sem("vconst"), 1)
            p.bump("vconst", 1)

    wsu_tmp = p.sbuf("g_wsu_tmp", [128, HC], F32)
    with nc.Block() as blk:
        @blk.sync
        def _(e):
            e.dma_start(out=wsu_tmp[:], in_=p.dram["l0_ws_up"].ap()).then_inc(p.sem("ldc"), 16)
            p.bump("ldc", 16)

        @blk.vector
        def _(e):
            e.wait_ge(p.sem("ldc"), p.val("ldc"))
            e.tensor_tensor(g["su_sup"][:], g["su_sup"][:], wsu_tmp[:], ALU.mult
                            ).then_inc(p.sem("vconst"), 1)
            p.bump("vconst", 1)
            e.wait_ge(p.sem("vconst"), p.val("vconst"))
            e.tensor_scalar(g["su_sup"][:], g["su_sup"][:], 1.0 / math.sqrt(D), None, ALU.mult
                            ).then_inc(p.sem("vconst"), 1)
            p.bump("vconst", 1)

    for li in layers:
        emit_layer(p, li)

    p.ctx.close()
    return p


def emit_layer(p, li):
    last = (li == max(p.layers))
    if li == 0:
        x_in = p.dram["x0"]
    elif 0 not in p.layers:
        x_in = p.din("x1_in", [TPC, D])
    else:
        x_in = p.dram["x1"]
    x_out = p.dram["y"] if last else p.dram["x1"]
    with p.nc.sbuf_tensor(f"z_sb_{li}", [128, CD * TPC], BF16) as z_sb:
        p.g["z_sb"] = z_sb
        if "p0" in p.phases:
            emit_p0(p, li, x_in)
        if "p1" in p.phases:
            emit_p1(p, li)
    if "p2" in p.phases:
        emit_p2p3(p, li)
    if "p4" in p.phases:
        emit_p4(p, li, x_in, x_out)
    emit_final_touch(p, li, x_out)
    p.barrier()



def emit_p0(p, li, x_in):
    nc = p.nc
    g = p.g
    z_sb = p.g["z_sb"]

    with contextlib.ExitStack() as ph:
        xt = [ph.enter_context(nc.sbuf_tensor(f"p0x{i}_{li}", [128, D], F32)) for i in range(2)]
        sq = ph.enter_context(nc.sbuf_tensor(f"p0sq_{li}", [128, D], F32))
        xn = [ph.enter_context(nc.sbuf_tensor(f"p0xn{i}_{li}", [128, D], BF16)) for i in range(2)]
        st = ph.enter_context(nc.sbuf_tensor(f"p0st_{li}", [128, 16], F32))
        if li == 0:
            stage = [ph.enter_context(nc.sbuf_tensor(f"p0stage{i}_{li}", [128, CD * 512], BF16))
                     for i in range(2)]
            zb = [ph.enter_context(nc.sbuf_tensor(f"p0zb{i}_{li}", [128, CD * 512], BF16))
                  for i in range(2)]
            psz = [ph.enter_context(nc.psum_tensor(f"p0pz{i}_{li}", [128, 512], F32))
                   for i in range(2)]
        pst = [ph.enter_context(nc.psum_tensor(f"p0pt{i}_{li}", [128, 128], BF16))
               for i in range(4)]

        sld = f"p0ld_{li}"
        sdve = f"p0dve_{li}"
        sact = f"p0act_{li}"
        spe = f"p0pe_{li}"
        sst = f"p0sst_{li}"
        szdve = f"p0zdve_{li}"
        szpe = f"p0zpe_{li}"
        scc = f"cc_ag_{li}"


        def act_n(i, k):
            return 2 * i + k

        def dve_stats_n(i, k):
            return 23 * i + k

        def dve_n(i):
            return 23 * i + 23

        def pe_n(i, j):
            return 16 * i + j

        base_ld = p.val(sld)
        base_act = p.val(sact)
        base_dve = p.val(sdve)
        base_pe = p.val(spe)
        base_st = p.val(sst)
        base_zdve = p.val(szdve)
        base_zpe = p.val(szpe)
        vconst_target = p.val("vconst")

        nbt = NTT // 4

        with nc.Block() as blk:
            @blk.sync
            def _(e):
                e.wait_ge(p.sem("ldc"), p.val("ldc"))
                for i in range(NTT):
                    if i >= 2:
                        e.wait_ge(p.sem(sdve), base_dve + dve_stats_n(i - 2, 7))
                    e.dma_start(out=xt[i % 2][:],
                                in_=x_in.ap()[i * 128:(i + 1) * 128, :]
                                ).then_inc(p.sem(sld), 16)
                    p.bump(sld, 16)

            @blk.scalar
            def _(e):
                for i in range(NTT):
                    e.wait_ge(p.sem(sld), base_ld + 16 * (i + 1))
                    e.activation(sq[:], xt[i % 2][:], AF.Square,
                                 accum_out=st[:, 8 * (i % 2) + 1:8 * (i % 2) + 2]
                                 ).then_inc(p.sem(sact), 1)
                    p.bump(sact, 1)
                    e.wait_ge(p.sem(sdve), base_dve + dve_stats_n(i, 4))
                    e.activation(st[:, 8 * (i % 2) + 5:8 * (i % 2) + 6],
                                 st[:, 8 * (i % 2) + 4:8 * (i % 2) + 5],
                                 AF.Sqrt, bias=g["epsb"][:]).then_inc(p.sem(sact), 1)
                    p.bump(sact, 1)

            @blk.vector
            def _(e):
                e.wait_ge(p.sem("vconst"), vconst_target)
                for i in range(NTT):
                    o = 8 * (i % 2)
                    e.wait_ge(p.sem(sld), base_ld + 16 * (i + 1))
                    e.tensor_reduce(st[:, o:o + 1], xt[i % 2][:], AX.X, ALU.add
                                    ).then_inc(p.sem(sdve), 1)
                    p.bump(sdve, 1)
                    e.wait_ge(p.sem(sdve), base_dve + dve_stats_n(i, 1))
                    e.tensor_scalar(st[:, o + 2:o + 3], st[:, o:o + 1],
                                    1.0 / D, None, ALU.mult).then_inc(p.sem(sdve), 1)
                    p.bump(sdve, 1)
                    e.wait_ge(p.sem(sdve), base_dve + dve_stats_n(i, 2))
                    e.tensor_tensor(st[:, o + 3:o + 4], st[:, o + 2:o + 3],
                                    st[:, o + 2:o + 3], ALU.mult).then_inc(p.sem(sdve), 1)
                    p.bump(sdve, 1)
                    e.wait_ge(p.sem(sact), base_act + act_n(i, 1))
                    e.wait_ge(p.sem(sdve), base_dve + dve_stats_n(i, 3))
                    e.scalar_tensor_tensor(st[:, o + 4:o + 5], st[:, o + 1:o + 2],
                                           1.0 / D, st[:, o + 3:o + 4],
                                           ALU.mult, ALU.subtract).then_inc(p.sem(sdve), 1)
                    p.bump(sdve, 1)
                    e.wait_ge(p.sem(sact), base_act + act_n(i, 2))
                    e.reciprocal(st[:, o + 6:o + 7], st[:, o + 5:o + 6]
                                 ).then_inc(p.sem(sdve), 1)
                    p.bump(sdve, 1)
                    e.wait_ge(p.sem(sdve), base_dve + dve_stats_n(i, 5))
                    e.tensor_scalar(st[:, o + 7:o + 8], st[:, o + 2:o + 3],
                                    st[:, o + 6:o + 7], -1.0, ALU.mult, ALU.mult
                                    ).then_inc(p.sem(sdve), 1)
                    p.bump(sdve, 1)
                    e.wait_ge(p.sem(sdve), base_dve + dve_stats_n(i, 6))
                    e.tensor_scalar(xn[i % 2][:], xt[i % 2][:],
                                    st[:, o + 6:o + 7], st[:, o + 7:o + 8],
                                    ALU.mult, ALU.add).then_inc(p.sem(sdve), 1)
                    p.bump(sdve, 1)
                    for j in range(CD):
                        e.wait_ge(p.sem(spe), base_pe + pe_n(i, j + 1))
                        if li == 0:
                            dst = stage[i // 4][:, j * 512 + (i % 4) * 128:
                                               j * 512 + (i % 4) * 128 + 128]
                        else:
                            dst = z_sb[:, j * TPC + i * 128: j * TPC + i * 128 + 128]
                        e.tensor_copy(dst, pst[j % 4][:]).then_inc(p.sem(sdve), 1)
                        p.bump(sdve, 1)

            @blk.tensor
            def _(e):
                e.wait_ge(p.sem("ldc"), p.val("ldc"))
                for i in range(NTT):
                    e.wait_ge(p.sem(sdve), base_dve + dve_stats_n(i, 7))
                    for j in range(CD):
                        n = 16 * i + j
                        if n >= 4:
                            ip, jp = (n - 4) // 16, (n - 4) % 16
                            e.wait_ge(p.sem(sdve), base_dve + 23 * ip + 7 + jp + 1)
                        e.transpose(pst[j % 4][:], xn[i % 2][:, j * 128:(j + 1) * 128],
                                    g["id_bf16"][:]).then_inc(p.sem(spe), 1)
                        p.bump(spe, 1)

            @blk.gpsimd
            def _(e):
                for bidx in range(nbt):
                    e.wait_ge(p.sem(sdve), base_dve + dve_n(4 * bidx + 3))
                    for j in range(CD):
                        if li == 0:
                            sap = stage[bidx][:, j * 512:(j + 1) * 512]
                        else:
                            sap = z_sb[:, j * TPC + bidx * 512: j * TPC + bidx * 512 + 512]
                        dst = p.dram["ag_in"].ap().rearrange(
                            "(c p) t -> c p t", p=128)[j, :, bidx * 512:(bidx + 1) * 512]
                        e.dma_start(out=dst, in_=sap).then_inc(p.sem(sst), 16)
                        p.bump(sst, 16)
                e.wait_ge(p.sem(sst), base_st + 16 * CD * nbt)
                e.collective_compute(
                    "AllGather", ALU.bypass,
                    replica_groups=[list(range(NCORE))],
                    ins=[p.dram["ag_in"].ap().opt()],
                    outs=[p.dram["ag_out"].ap().rearrange("r c t -> (r c) t").opt()],
                ).then_inc(p.sem(scc), 1)
                p.bump(scc, 1)
                if "xn_full" in p.dbg and li == 0:
                    e.wait_ge(p.sem(scc), 1)
                    e.dma_start(out=p.dram["dbg_xn_full"].ap().rearrange("r c t -> (r c) t"),
                                in_=p.dram["ag_out"].ap().rearrange("r c t -> (r c) t")
                                ).then_inc(p.sem(sst), 16)
                    p.bump(sst, 16)

        if li == 0:
            with nc.Block() as blk:
                @blk.vector
                def _(e):
                    for bidx in range(nbt):
                        e.wait_ge(p.sem(sdve), base_dve + dve_n(4 * bidx + 3))
                        if bidx >= 1:
                            e.wait_ge(p.sem(szpe), base_zpe + bidx * CD)
                        for c in range(CD):
                            ins = e.tensor_scalar(zb[0][:, c * 512:(c + 1) * 512],
                                                  stage[bidx][:, c * 512:(c + 1) * 512],
                                                  g["su_ug"][:, c:c + 1], None, ALU.mult)
                            if c == CD - 1:
                                ins.then_inc(p.sem(szdve), 1)
                                p.bump(szdve, 1)
                        e.wait_ge(p.sem(szdve), base_zdve + bidx * 21 + 1)
                        cur, nxt = 0, 1
                        for si, hstep in enumerate((1, 2, 4, 8)):
                            for c in range(CD):
                                pr = c ^ hstep
                                ca, cb = (c, pr) if (c & hstep) == 0 else (pr, c)
                                op = ALU.add if (c & hstep) == 0 else ALU.subtract
                                ins = e.tensor_tensor(
                                    zb[nxt][:, c * 512:(c + 1) * 512],
                                    zb[cur][:, ca * 512:(ca + 1) * 512],
                                    zb[cur][:, cb * 512:(cb + 1) * 512],
                                    op)
                                if c == CD - 1:
                                    ins.then_inc(p.sem(szdve), 1)
                                    p.bump(szdve, 1)
                            e.wait_ge(p.sem(szdve), base_zdve + bidx * 21 + 2 + si)
                            cur, nxt = nxt, cur
                        for c in range(CD):
                            e.wait_ge(p.sem(szpe), base_zpe + bidx * CD + c + 1)
                            e.tensor_copy(
                                z_sb[:, c * TPC + bidx * 512: c * TPC + bidx * 512 + 512],
                                psz[c % 2][:]).then_inc(p.sem(szdve), 1)
                            p.bump(szdve, 1)

                @blk.tensor
                def _(e):
                    for bidx in range(nbt):
                        e.wait_ge(p.sem(szdve), base_zdve + bidx * 21 + 5)
                        for c in range(CD):
                            if c >= 2:
                                e.wait_ge(p.sem(szdve), base_zdve + bidx * 21 + 5 + (c - 2) + 1)
                            e.matmul(psz[c % 2][:], g["h128"][:],
                                     zb[0][:, c * 512:(c + 1) * 512],
                                     start=True, stop=True, skip_group_check=True
                                     ).then_inc(p.sem(szpe), 1)
                            p.bump(szpe, 1)

        if "z" in p.dbg and li == 0:
            with nc.Block() as blk:
                @blk.gpsimd
                def _(e):
                    e.wait_ge(p.sem(szdve), p.val(szdve))
                    e.dma_start(out=p.dram["dbg_z"].ap(), in_=z_sb[:]).then_inc(
                        p.sem(sst), 16)
                    p.bump(sst, 16)
                    e.wait_ge(p.sem(sst), p.val(sst))

    p.barrier()



OFF_U = D * INTER
OFF_D = 2 * D * INTER


def emit_p1(p, li):
    nc = p.nc
    g = p.g
    z_sb = g["z_sb"]
    pre = f"l{li}_"
    wg_d, wu_d, wd_d = p.dram[pre + "wg"], p.dram[pre + "wu"], p.dram[pre + "wd"]
    wbf = p.dram["wbf"]

    for tb in (0, 1):
        pass0 = (tb == 0)
        with nc.sbuf_tensor(f"h_sb_{li}_{tb}", [128, HC * 512], BF16) as h_sb:
            p.g["h_sb"] = h_sb
            emit_p1_gateup(p, li, tb, pass0)
            p.barrier()
            if li == 0:
                emit_p1_fwht2(p, li, tb)
                p.barrier()
            emit_p1_down(p, li, tb, pass0)
            p.barrier()

    if "ffn_out" in p.dbg:
        with nc.Block() as blk:
            @blk.gpsimd
            def _(e):
                s = p.sem(f"p1dbg_{li}")
                e.dma_start(out=p.dram["dbg_ffn_out"].ap(),
                            in_=p.dram["ffn_out"].ap()).then_inc(s, 16)
                p.bump(f"p1dbg_{li}", 16)
                e.wait_ge(s, p.val(f"p1dbg_{li}"))


def emit_p1_gateup(p, li, tb, pass0):
    nc = p.nc
    g = p.g
    z_sb = g["z_sb"]
    pre = f"l{li}_"
    wg_d, wu_d = p.dram[pre + "wg"], p.dram[pre + "wu"]
    wbf = p.dram["wbf"]

    sld = f"p1ld_{li}_{tb}"
    sldu = f"p1ldu_{li}_{tb}"
    sdve = f"p1dve_{li}_{tb}"
    sact = f"p1act_{li}_{tb}"
    spe = f"p1pe_{li}_{tb}"
    swb = f"p1wb_{li}_{tb}"
    b_ld, b_ldu, b_dve, b_act, b_pe, b_wb = (p.val(sld), p.val(sldu), p.val(sdve),
                                             p.val(sact), p.val(spe), p.val(swb))
    vconst_t = p.val("vconst")
    vo = 2 if pass0 else 1

    h_sb = p.g["h_sb"]
    with contextlib.ExitStack() as ph:
        wstage = [ph.enter_context(nc.sbuf_tensor(f"p1ws{i}_{li}_{tb}", [128, 4096], F32))
                  for i in range(2)] if pass0 else None
        wb = [ph.enter_context(nc.sbuf_tensor(f"p1wb{i}_{li}_{tb}", [128, 4096], BF16))
              for i in range(2)]
        sg = [ph.enter_context(nc.sbuf_tensor(f"p1sg{i}_{li}_{tb}", [128, 512], BF16))
              for i in range(2)]
        psG = [ph.enter_context(nc.psum_tensor(f"p1pg{i}_{li}_{tb}", [128, 512], F32))
               for i in range(2)]
        psU = [ph.enter_context(nc.psum_tensor(f"p1pu{i}_{li}_{tb}", [128, 512], F32))
               for i in range(2)]

        with nc.Block() as blk:
            @blk.sync
            def _(e):
                for hc in range(HC):
                    if pass0:
                        if hc >= 2:
                            e.wait_ge(p.sem(sdve), b_dve + vo * (hc - 2) + 1)
                        e.dma_start(
                            out=wstage[hc % 2][:, 0:2048],
                            in_=wg_d.ap()[hc]
                        ).then_inc(p.sem(sld), 16)
                        p.bump(sld, 16)
                    else:
                        if hc >= 2:
                            e.wait_ge(p.sem(spe), b_pe + 32 * (hc - 1))
                        for half, off in ((0, hc * 262144), (1, OFF_U + hc * 262144)):
                            e.dma_start(
                                out=wb[hc % 2][:, half * 2048:(half + 1) * 2048],
                                in_=wbf.ap()[off:off + 262144].rearrange(
                                    "(q w) -> q w", q=128)
                            ).then_inc(p.sem(sld), 16)
                            p.bump(sld, 16)

            @blk.vector
            def _(e):
                e.wait_ge(p.sem("vconst"), vconst_t)
                for hc in range(HC):
                    if pass0:
                        e.wait_ge(p.sem(sld), b_ld + 16 * (hc + 1))
                        e.wait_ge(p.sem(sldu), b_ldu + 16 * (hc + 1))
                        if hc >= 2:
                            e.wait_ge(p.sem(spe), b_pe + 32 * (hc - 1))
                            e.wait_ge(p.sem(swb), b_wb + 32 * (hc - 1))
                        e.tensor_copy(wb[hc % 2][:], wstage[hc % 2][:]
                                      ).then_inc(p.sem(sdve), 1)
                        p.bump(sdve, 1)
                    e.wait_ge(p.sem(sact), b_act + hc + 1)
                    e.wait_ge(p.sem(spe), b_pe + 32 * (hc + 1))
                    dst = h_sb[:, hc * 512:(hc + 1) * 512]
                    if li == 0:
                        e.scalar_tensor_tensor(dst, sg[hc % 2][:],
                                               g["su_sup"][:, hc:hc + 1],
                                               psU[hc % 2][:],
                                               ALU.mult, ALU.mult
                                               ).then_inc(p.sem(sdve), 1)
                    else:
                        e.tensor_tensor(dst, sg[hc % 2][:], psU[hc % 2][:],
                                        ALU.mult).then_inc(p.sem(sdve), 1)
                    p.bump(sdve, 1)

            @blk.scalar
            def _(e):
                e.wait_ge(p.sem("vconst"), vconst_t)
                for hc in range(HC):
                    if pass0:
                        if hc >= 2:
                            e.wait_ge(p.sem(sdve), b_dve + vo * (hc - 2) + 1)
                        e.dma_start(
                            out=wstage[hc % 2][:, 2048:4096],
                            in_=wu_d.ap()[hc]
                        ).then_inc(p.sem(sldu), 16)
                        p.bump(sldu, 16)
                    e.wait_ge(p.sem(spe), b_pe + 32 * hc + 16)
                    if hc >= 2:
                        e.wait_ge(p.sem(sdve), b_dve + vo * (hc - 2) + vo)
                    if li == 0:
                        e.activation(sg[hc % 2][:], psG[hc % 2][:], AF.Silu,
                                     scale=g["ws_g"][:, hc:hc + 1]
                                     ).then_inc(p.sem(sact), 1)
                    else:
                        e.activation(sg[hc % 2][:], psG[hc % 2][:], AF.Silu
                                     ).then_inc(p.sem(sact), 1)
                    p.bump(sact, 1)

            @blk.tensor
            def _(e):
                for hc in range(HC):
                    if pass0:
                        e.wait_ge(p.sem(sdve), b_dve + vo * hc + 1)
                    else:
                        e.wait_ge(p.sem(sld), b_ld + 32 * (hc + 1))
                    if hc >= 2:
                        e.wait_ge(p.sem(sact), b_act + hc - 1)
                        e.wait_ge(p.sem(sdve), b_dve + vo * (hc - 2) + vo)
                    for cc in range(CD):
                        ins = e.matmul(psG[hc % 2][:], wb[hc % 2][:, cc * 128:(cc + 1) * 128],
                                       z_sb[:, cc * TPC + tb * 512: cc * TPC + tb * 512 + 512],
                                       start=(cc == 0), stop=(cc == CD - 1),
                                       skip_group_check=True)
                        if cc == CD - 1:
                            ins.then_inc(p.sem(spe), CD)
                        p.bump(spe, 1)
                    for cc in range(CD):
                        ins = e.matmul(psU[hc % 2][:],
                                       wb[hc % 2][:, 2048 + cc * 128: 2048 + (cc + 1) * 128],
                                       z_sb[:, cc * TPC + tb * 512: cc * TPC + tb * 512 + 512],
                                       start=(cc == 0), stop=(cc == CD - 1),
                                       skip_group_check=True)
                        if cc == CD - 1:
                            ins.then_inc(p.sem(spe), CD)
                        p.bump(spe, 1)

            @blk.gpsimd
            def _(e):
                if pass0:
                    for hc in range(HC):
                        e.wait_ge(p.sem(sdve), b_dve + vo * hc + 1)
                        for half, off in ((0, hc * 262144), (1, OFF_U + hc * 262144)):
                            e.dma_start(
                                out=wbf.ap()[off:off + 262144].rearrange(
                                    "(q w) -> q w", q=128),
                                in_=wb[hc % 2][:, half * 2048:(half + 1) * 2048]
                            ).then_inc(p.sem(swb), 16)
                            p.bump(swb, 16)


def emit_p1_fwht2(p, li, tb):
    nc = p.nc
    g = p.g
    h_sb = p.g["h_sb"]
    sdve = f"p1fdve_{li}_{tb}"
    spe = f"p1fpe_{li}_{tb}"
    b_dve, b_pe = p.val(sdve), p.val(spe)

    with contextlib.ExitStack() as ph:
        h2 = ph.enter_context(nc.sbuf_tensor(f"p1h2_{li}_{tb}", [128, HC * 256], BF16))
        psZ2 = [ph.enter_context(nc.psum_tensor(f"p1pz{i}_{li}_{tb}", [128, 512], F32))
                for i in range(2)]

        nst = 0
        with nc.Block() as blk:
            @blk.vector
            def _(e):
                cnt = 0
                for sub in range(2):
                    for si, hstep in enumerate((1, 2, 4, 8, 16, 32)):
                        src_is_h = (si % 2 == 0)
                        for c in range(HC):
                            pr = c ^ hstep
                            ca, cb = (c, pr) if (c & hstep) == 0 else (pr, c)
                            op = ALU.add if (c & hstep) == 0 else ALU.subtract
                            if src_is_h:
                                sa = h_sb[:, ca * 512 + sub * 256: ca * 512 + sub * 256 + 256]
                                sb_ = h_sb[:, cb * 512 + sub * 256: cb * 512 + sub * 256 + 256]
                                dst = h2[:, c * 256:(c + 1) * 256]
                            else:
                                sa = h2[:, ca * 256:(ca + 1) * 256]
                                sb_ = h2[:, cb * 256:(cb + 1) * 256]
                                dst = h_sb[:, c * 512 + sub * 256: c * 512 + sub * 256 + 256]
                            ins = e.tensor_tensor(dst, sa, sb_, op)
                            if c == HC - 1:
                                ins.then_inc(p.sem(sdve), 1)
                                p.bump(sdve, 1)
                                cnt += 1
                                e.wait_ge(p.sem(sdve), b_dve + cnt)
                nevac = 0
                for hc in range(HC):
                    e.wait_ge(p.sem(spe), b_pe + hc + 1)
                    e.tensor_copy(h_sb[:, hc * 512:(hc + 1) * 512],
                                  psZ2[hc % 2][:]).then_inc(p.sem(sdve), 1)
                    p.bump(sdve, 1)

            @blk.tensor
            def _(e):
                e.wait_ge(p.sem(sdve), b_dve + 12)
                for hc in range(HC):
                    if hc >= 2:
                        e.wait_ge(p.sem(sdve), b_dve + 12 + (hc - 2) + 1)
                    e.matmul(psZ2[hc % 2][:], g["h128"][:],
                             h_sb[:, hc * 512:(hc + 1) * 512],
                             start=True, stop=True, skip_group_check=True
                             ).then_inc(p.sem(spe), 1)
                    p.bump(spe, 1)


def emit_p1_down(p, li, tb, pass0):
    nc = p.nc
    g = p.g
    h_sb = p.g["h_sb"]
    pre = f"l{li}_"
    wd_d = p.dram[pre + "wd"]
    wbf = p.dram["wbf"]

    sld = f"p1dld_{li}_{tb}"
    sdve = f"p1ddve_{li}_{tb}"
    spe = f"p1dpe_{li}_{tb}"
    swb = f"p1dwb_{li}_{tb}"
    sst = f"p1dst_{li}_{tb}"
    b_ld, b_dve, b_pe, b_wb, b_st = (p.val(sld), p.val(sdve), p.val(spe),
                                     p.val(swb), p.val(sst))
    vconst_t = p.val("vconst")
    vo = 5 if pass0 else 1

    with contextlib.ExitStack() as ph:
        wdb = [ph.enter_context(nc.sbuf_tensor(f"p1wd{i}_{li}_{tb}", [128, 8192], BF16))
               for i in range(2)]
        wstage = ph.enter_context(nc.sbuf_tensor(f"p1dws_{li}_{tb}", [128, 2048], F32)) \
            if pass0 else None
        dn = [ph.enter_context(nc.sbuf_tensor(f"p1dn{i}_{li}_{tb}", [128, 512], F32))
              for i in range(2)]
        psD = [ph.enter_context(nc.psum_tensor(f"p1pd{i}_{li}_{tb}", [128, 512], F32))
               for i in range(2)]

        with nc.Block() as blk:
            @blk.sync
            def _(e):
                for oc in range(CD):
                    if pass0:
                        for grp in range(4):
                            n = 4 * oc + grp
                            if n >= 1:
                                e.wait_ge(p.sem(sdve), b_dve + vo * ((n - 1) // 4)
                                          + ((n - 1) % 4) + 1)
                            e.dma_start(
                                out=wstage[:],
                                in_=wd_d.ap()[oc, :, 16 * grp:16 * (grp + 1), :]
                            ).then_inc(p.sem(sld), 16)
                            p.bump(sld, 16)
                    else:
                        if oc >= 2:
                            e.wait_ge(p.sem(spe), b_pe + 64 * (oc - 1))
                        off = OFF_D + oc * 1048576
                        e.dma_start(
                            out=wdb[oc % 2][:],
                            in_=wbf.ap()[off:off + 1048576].rearrange(
                                "(q w) -> q w", q=128)
                        ).then_inc(p.sem(sld), 16)
                        p.bump(sld, 16)

            @blk.vector
            def _(e):
                e.wait_ge(p.sem("vconst"), vconst_t)
                for oc in range(CD):
                    if pass0:
                        for grp in range(4):
                            n = 4 * oc + grp
                            e.wait_ge(p.sem(sld), b_ld + 16 * (n + 1))
                            if oc >= 2 and grp == 0:
                                e.wait_ge(p.sem(spe), b_pe + 64 * (oc - 1))
                                e.wait_ge(p.sem(swb), b_wb + 16 * (oc - 1))
                            e.tensor_copy(wdb[oc % 2][:, grp * 2048:(grp + 1) * 2048],
                                          wstage[:]).then_inc(p.sem(sdve), 1)
                            p.bump(sdve, 1)
                    e.wait_ge(p.sem(spe), b_pe + 64 * (oc + 1))
                    if oc >= 2:
                        e.wait_ge(p.sem(sst), b_st + 16 * (oc - 1))
                    if li == 0:
                        e.tensor_scalar(dn[oc % 2][:], psD[oc % 2][:],
                                        g["ws_dp"][:, oc:oc + 1], None, ALU.mult
                                        ).then_inc(p.sem(sdve), 1)
                    else:
                        e.tensor_copy(dn[oc % 2][:], psD[oc % 2][:]
                                      ).then_inc(p.sem(sdve), 1)
                    p.bump(sdve, 1)

            @blk.tensor
            def _(e):
                for oc in range(CD):
                    if oc >= 2:
                        e.wait_ge(p.sem(sdve), b_dve + vo * (oc - 2) + vo)
                    for sc in range(HC):
                        if pass0:
                            grp = sc // 16
                            e.wait_ge(p.sem(sdve), b_dve + vo * oc + grp + 1)
                        elif sc == 0:
                            e.wait_ge(p.sem(sld), b_ld + 16 * (oc + 1))
                        ins = e.matmul(psD[oc % 2][:], wdb[oc % 2][:, sc * 128:(sc + 1) * 128],
                                       h_sb[:, sc * 512:(sc + 1) * 512],
                                       start=(sc == 0), stop=(sc == HC - 1),
                                       skip_group_check=True)
                        if sc == HC - 1:
                            ins.then_inc(p.sem(spe), HC)
                        p.bump(spe, 1)

            @blk.gpsimd
            def _(e):
                for oc in range(CD):
                    if pass0:
                        e.wait_ge(p.sem(sdve), b_dve + vo * oc + 4)
                        off = OFF_D + oc * 1048576
                        e.dma_start(
                            out=wbf.ap()[off:off + 1048576].rearrange(
                                "(q w) -> q w", q=128),
                            in_=wdb[oc % 2][:]).then_inc(p.sem(swb), 16)
                        p.bump(swb, 16)
                    e.wait_ge(p.sem(sdve), b_dve + vo * oc + vo)
                    e.dma_start(
                        out=p.dram["ffn_out"].ap().rearrange("(c q) t -> c q t", q=128)
                        [oc, :, tb * 512:(tb + 1) * 512],
                        in_=dn[oc % 2][:]).then_inc(p.sem(sst), 16)
                    p.bump(sst, 16)
                e.wait_ge(p.sem(sst), b_st + 16 * CD)



def emit_final_touch(p, li, x_out):
    if "p4" in p.phases:
        return
    nc = p.nc
    with nc.Block() as blk:
        @blk.gpsimd
        def _(e):
            s = p.sem(f"ft_{li}")
            e.dma_start(out=x_out.ap(), in_=p.dram["x0"].ap()).then_inc(s, 16)
            p.bump(f"ft_{li}", 16)
            e.wait_ge(s, p.val(f"ft_{li}"))


def emit_p2p3(p, li):
    nc = p.nc
    g = p.g
    pre = f"l{li}_"

    with contextlib.ExitStack() as ph:
        Q_sb = ph.enter_context(nc.sbuf_tensor(f"q_sb_{li}", [128, HPC * T], BF16))
        K_sb = ph.enter_context(nc.sbuf_tensor(f"k_sb_{li}", [128, HPC * T], BF16))
        V_tm = ph.enter_context(nc.sbuf_tensor(f"v_tm_{li}", [128, (T // 128) * HPC * HD], BF16))
        maskA = ph.enter_context(nc.sbuf_tensor(f"mA_{li}", [128, 4 * 512], F32))
        maskB = ph.enter_context(nc.sbuf_tensor(f"mB_{li}", [128, 4 * 512], F32))
        emit_p2(p, li, Q_sb, K_sb, V_tm, maskA, maskB)
        p.barrier()
        if "p3" in p.phases:
            emit_p3(p, li, Q_sb, K_sb, V_tm, maskA, maskB)
            p.barrier()


def emit_p2(p, li, Q_sb, K_sb, V_tm, maskA, maskB):
    nc = p.nc
    g = p.g
    pre = f"l{li}_"

    qkvw_d = p.dram[pre + "qkvw"]

    sld = f"p2ld_{li}"
    swl = f"p2wl_{li}"
    sdve = f"p2dve_{li}"
    sact = f"p2act_{li}"
    spe = f"p2pe_{li}"
    b_ld, b_wl, b_dve, b_act, b_pe = (p.val(sld), p.val(swl), p.val(sdve),
                                      p.val(sact), p.val(spe))
    scc = f"cc_ag_{li}"
    cc_t = p.val(scc)
    vconst_t = p.val("vconst")

    if li == 0:
        DVE_TB = 14
        PE_TB = 96 + 8
    else:
        DVE_TB = 2 * 7 + 2 * 3 + 2 * 5
        PE_TB = 96 + 8 + 6

    with contextlib.ExitStack() as ph:
        qkvw = ph.enter_context(nc.sbuf_tensor(f"p2w_{li}", [128, 6 * CD * 128], BF16))
        wstage = ph.enter_context(nc.sbuf_tensor(f"p2ws_{li}", [128, CD * 128], F32))
        xnb = [ph.enter_context(nc.sbuf_tensor(f"p2xn{i}_{li}", [128, CD * 512], BF16))
               for i in range(2)]
        vtmp = ph.enter_context(nc.sbuf_tensor(f"p2vt_{li}", [128, 512], BF16))
        gtmp = ph.enter_context(nc.sbuf_tensor(f"p2gt_{li}", [128, 512], BF16))
        swtmp = ph.enter_context(nc.sbuf_tensor(f"p2sw_{li}", [128, 512], BF16))
        t1tmp = ph.enter_context(nc.sbuf_tensor(f"p2t1_{li}", [128, 512], BF16))
        t2tmp = ph.enter_context(nc.sbuf_tensor(f"p2t2_{li}", [128, 512], BF16))
        roptmp = ph.enter_context(nc.sbuf_tensor(f"p2rp_{li}", [128, 512], BF16))
        ctab = [ph.enter_context(nc.sbuf_tensor(f"p2ct{i}_{li}", [128, 512], BF16))
                for i in range(2)]
        stab = [ph.enter_context(nc.sbuf_tensor(f"p2st{i}_{li}", [128, 512], BF16))
                for i in range(2)]
        psQ = [ph.enter_context(nc.psum_tensor(f"p2pq{i}_{li}", [128, 512], F32))
               for i in range(2)]
        psT = [ph.enter_context(nc.psum_tensor(f"p2pt{i}_{li}", [128, 128], BF16))
               for i in range(2)]
        psR = ph.enter_context(nc.psum_tensor(f"p2pr_{li}", [128, 512], F32)) \
            if li == 1 else None

        with nc.Block() as blk:
            @blk.sync
            def _(e):
                for m in range(4):
                    e.dma_start(out=maskA[:, m * 512:(m + 1) * 512],
                                in_=p.dram["maskA"].ap()[m]).then_inc(p.sem(swl), 16)
                    p.bump(swl, 16)
                    e.dma_start(out=maskB[:, m * 512:(m + 1) * 512],
                                in_=p.dram["maskB"].ap()[m]).then_inc(p.sem(swl), 16)
                    p.bump(swl, 16)
                for s in range(6):
                    if s >= 1:
                        e.wait_ge(p.sem(sdve), b_dve + s)
                    e.dma_start(out=wstage[:], in_=qkvw_d.ap()[s]
                                ).then_inc(p.sem(swl), 16)
                    p.bump(swl, 16)
                e.wait_ge(p.sem(scc), cc_t)
                for tb in range(NTB):
                    if tb >= 2:
                        e.wait_ge(p.sem(spe), b_pe + PE_TB * (tb - 1))
                    r, half = tb // 2, tb % 2
                    e.dma_start(
                        out=xnb[tb % 2][:],
                        in_=p.dram["ag_out"].ap().rearrange(
                            "r (c q) t -> q (r c) t", q=128)
                        [:, r * CD:(r + 1) * CD, half * 512:(half + 1) * 512]
                    ).then_inc(p.sem(sld), 16)
                    p.bump(sld, 16)
                    if li == 1:
                        e.dma_start(out=ctab[tb % 2][:],
                                    in_=p.dram["ropecos"].ap()[:, tb * 512:(tb + 1) * 512]
                                    ).then_inc(p.sem(sld), 16)
                        p.bump(sld, 16)
                        e.dma_start(out=stab[tb % 2][:],
                                    in_=p.dram["ropesin"].ap()[:, tb * 512:(tb + 1) * 512]
                                    ).then_inc(p.sem(sld), 16)
                        p.bump(sld, 16)

            @blk.vector
            def _(e):
                e.wait_ge(p.sem("vconst"), vconst_t)
                for s in range(6):
                    e.wait_ge(p.sem(swl), b_wl + 128 + 16 * (s + 1))
                    e.tensor_copy(qkvw[:, s * 2048:(s + 1) * 2048], wstage[:]
                                  ).then_inc(p.sem(sdve), 1)
                    p.bump(sdve, 1)
                ld_per_tb = 3 if li == 1 else 1
                for tb in range(NTB):
                    dvetb = b_dve + 6 + DVE_TB * tb
                    n = 0

                    def bump1(ins):
                        nonlocal n
                        n += 1
                        ins.then_inc(p.sem(sdve), 1)
                        p.bump(sdve, 1)

                    def wself():
                        e.wait_ge(p.sem(sdve), dvetb + n)

                    col = lambda h: h * T + tb * 512
                    if li == 0:
                        for s in range(6):
                            e.wait_ge(p.sem(spe), b_pe + PE_TB * tb + 16 * (s + 1))
                            ps = psQ[s % 2][:]
                            if s < 2:
                                dst = Q_sb[:, col(s):col(s) + 512]
                                bias = g["bq"][:, s:s + 1]
                            elif s < 4:
                                dst = K_sb[:, col(s - 2):col(s - 2) + 512]
                                bias = g["bk"][:, s - 2:s - 2 + 1]
                            else:
                                dst = vtmp[:]
                                bias = g["bv"][:, s - 4:s - 4 + 1]
                            bump1(e.tensor_scalar(dst, ps, bias, None, ALU.add))
                            if s >= 4:
                                wself()
                                h = s - 4
                                for sub in range(4):
                                    kc = tb * 4 + sub
                                    e.wait_ge(p.sem(spe),
                                              b_pe + PE_TB * tb + 96 + 4 * h + sub + 1)
                                    bump1(e.tensor_copy(
                                        V_tm[:, kc * 256 + h * 128: kc * 256 + h * 128 + 128],
                                        psT[sub % 2][:]))
                    else:
                        pe_off = {0: 16, 1: 34, 2: 51, 3: 69, 4: 86, 5: 102}
                        pe_extra = {0: (17, 18), 2: (52, 53)}
                        pe_rot = {1: 35, 3: 70}
                        for s in range(6):
                            e.wait_ge(p.sem(spe), b_pe + PE_TB * tb + pe_off[s])
                            ps = psQ[s % 2]
                            dstT = Q_sb if s < 2 else (K_sb if s < 4 else None)
                            hcolA = col(0)
                            hcolB = col(1)
                            if s in (0, 2):
                                e.wait_ge(p.sem(sld), b_ld + 16 * ld_per_tb * tb + 48)
                                bump1(e.tensor_copy(gtmp[:], ps[:]))
                                wself()
                                e.wait_ge(p.sem(spe), b_pe + PE_TB * tb + pe_extra[s][0])
                                bump1(e.tensor_copy(swtmp[:], psR[:]))
                                bump1(e.tensor_tensor(t1tmp[:], gtmp[:], ctab[tb % 2][:],
                                                      ALU.mult))
                                wself()
                                bump1(e.tensor_tensor(t2tmp[:], swtmp[:], stab[tb % 2][:],
                                                      ALU.mult))
                                wself()
                                bump1(e.tensor_tensor(roptmp[:], t1tmp[:], t2tmp[:],
                                                      ALU.add))
                                wself()
                                bump1(e.tensor_copy(dstT[0:64, hcolA:hcolA + 512],
                                                    roptmp[0:64, :]))
                                e.wait_ge(p.sem(spe), b_pe + PE_TB * tb + pe_extra[s][1])
                                bump1(e.tensor_copy(dstT[0:64, hcolB:hcolB + 512],
                                                    psR[0:64, :]))
                            elif s in (1, 3):
                                bump1(e.tensor_copy(gtmp[:], ps[:]))
                                wself()
                                e.wait_ge(p.sem(spe), b_pe + PE_TB * tb + pe_rot[s])
                                bump1(e.tensor_copy(dstT[64:128, hcolA:hcolA + 512],
                                                    psR[64:128, :]))
                                bump1(e.tensor_copy(dstT[64:128, hcolB:hcolB + 512],
                                                    gtmp[64:128, :]))
                            else:
                                h = s - 4
                                bump1(e.tensor_copy(vtmp[:], ps[:]))
                                wself()
                                for sub in range(4):
                                    kc = tb * 4 + sub
                                    e.wait_ge(p.sem(spe),
                                              b_pe + PE_TB * tb + 102 + 4 * h + sub + 1)
                                    bump1(e.tensor_copy(
                                        V_tm[:, kc * 256 + h * 128: kc * 256 + h * 128 + 128],
                                        psT[sub % 2][:]))
                    assert n == DVE_TB, (li, tb, n)

            @blk.tensor
            def _(e):
                e.wait_ge(p.sem(sdve), b_dve + 6)
                ld_per_tb = 3 if li == 1 else 1
                for tb in range(NTB):
                    e.wait_ge(p.sem(sld), b_ld + 16 * ld_per_tb * tb + 16)
                    dvetb = b_dve + 6 + DVE_TB * tb
                    for s in range(6):
                        if tb > 0 or s >= 2:
                            if s >= 2:
                                pn = dve_evac_done(li, tb, s - 2)
                            else:
                                pn = dve_evac_done(li, tb - 1, s + 4)
                            e.wait_ge(p.sem(sdve), b_dve + 6 + pn)
                        for cc in range(CD):
                            ins = e.matmul(psQ[s % 2][:],
                                           qkvw[:, s * 2048 + cc * 128: s * 2048 + (cc + 1) * 128],
                                           xnb[tb % 2][:, cc * 512:(cc + 1) * 512],
                                           start=(cc == 0), stop=(cc == CD - 1),
                                           skip_group_check=True)
                            if cc == CD - 1:
                                ins.then_inc(p.sem(spe), CD)
                            p.bump(spe, 1)
                        if li == 1 and s in (0, 2):
                            e.wait_ge(p.sem(sdve),
                                      b_dve + 6 + dve_set_start(li, tb, s) + 1)
                            e.matmul(psR[:], g["mswap"][:], gtmp[:],
                                     start=True, stop=True, skip_group_check=True
                                     ).then_inc(p.sem(spe), 1)
                            p.bump(spe, 1)
                            e.wait_ge(p.sem(sdve),
                                      b_dve + 6 + dve_set_start(li, tb, s) + 5)
                            e.matmul(psR[:], g["mrot"][:], roptmp[:],
                                     start=True, stop=True, skip_group_check=True
                                     ).then_inc(p.sem(spe), 1)
                            p.bump(spe, 1)
                        if li == 1 and s in (1, 3):
                            e.wait_ge(p.sem(sdve),
                                      b_dve + 6 + dve_set_start(li, tb, s) + 1)
                            e.matmul(psR[:], g["mrot"][:], gtmp[:],
                                     start=True, stop=True, skip_group_check=True
                                     ).then_inc(p.sem(spe), 1)
                            p.bump(spe, 1)
                    for h in range(2):
                        s = 4 + h
                        e.wait_ge(p.sem(sdve), b_dve + 6 + dve_vtmp_done(li, tb, h))
                        for sub in range(4):
                            n96 = 4 * h + sub
                            if n96 >= 2:
                                e.wait_ge(p.sem(sdve),
                                          b_dve + 6 + dve_pst_done(li, tb, n96 - 2))
                            e.transpose(psT[sub % 2][:],
                                        vtmp[:, sub * 128:(sub + 1) * 128],
                                        g["id_bf16"][:]).then_inc(p.sem(spe), 1)
                            p.bump(spe, 1)

    if "qkv" in p.dbg:
        with nc.Block() as blk:
            @blk.gpsimd
            def _(e):
                s = p.sem(f"p2dbg_{li}")
                e.wait_ge(p.sem(spe), p.val(spe))
                e.wait_ge(p.sem(sdve), p.val(sdve))
                e.dma_start(out=p.dram["dbg_q"].ap(), in_=Q_sb[:]).then_inc(p.sem(f"p2dbg_{li}"), 16)
                p.bump(f"p2dbg_{li}", 16)
                e.dma_start(out=p.dram["dbg_k"].ap(), in_=K_sb[:]).then_inc(p.sem(f"p2dbg_{li}"), 16)
                p.bump(f"p2dbg_{li}", 16)
                e.dma_start(out=p.dram["dbg_v"].ap(), in_=V_tm[:]).then_inc(p.sem(f"p2dbg_{li}"), 16)
                p.bump(f"p2dbg_{li}", 16)
                e.wait_ge(p.sem(f"p2dbg_{li}"), p.val(f"p2dbg_{li}"))


_L1_ENDS = [7, 10, 17, 20, 25, 30]
_L1_STARTS = [0, 7, 10, 17, 20, 25]
_L0_ENDS = [1, 2, 3, 4, 9, 14]
_L0_STARTS = [0, 1, 2, 3, 4, 9]


def dve_evac_done(li, tb, s):
    if tb < 0:
        return 0
    DVE_TB = 14 if li == 0 else 30
    base = DVE_TB * tb
    return base + (_L0_ENDS if li == 0 else _L1_ENDS)[s]


def dve_set_start(li, tb, s):
    DVE_TB = 14 if li == 0 else 30
    return DVE_TB * tb + (_L0_STARTS if li == 0 else _L1_STARTS)[s]


def dve_vtmp_done(li, tb, h):
    return dve_set_start(li, tb, 4 + h) + 1


def dve_pst_done(li, tb, n96):
    h, sub = n96 // 4, n96 % 4
    return dve_set_start(li, tb, 4 + h) + 1 + sub + 1



def emit_p3(p, li, Q_sb, K_sb, V_tm, maskA, maskB):
    nc = p.nc
    g = p.g
    sdve = f"p3dve_{li}"
    sact = f"p3act_{li}"
    spe = f"p3pe_{li}"
    sst = f"p3st_{li}"
    scc2 = f"cc_a2a_{li}"

    with contextlib.ExitStack() as ph:
        probsT = [ph.enter_context(nc.sbuf_tensor(f"p3pr{i}_{li}", [128, 512], BF16))
                  for i in range(2)]
        zr = ph.enter_context(nc.sbuf_tensor(f"p3zr_{li}", [1, 512], F32))
        bc = ph.enter_context(nc.sbuf_tensor(f"p3bc_{li}", [128, 512], F32))
        ybuf = [ph.enter_context(nc.sbuf_tensor(f"p3y{i}_{li}", [128, 512], BF16))
                for i in range(2)]
        psB = [ph.enter_context(nc.psum_tensor(f"p3pb{i}_{li}", [128, 512], F32))
               for i in range(3)]
        psZ = ph.enter_context(nc.psum_tensor(f"p3pz_{li}", [1, 512], F32))
        psPV = ph.enter_context(nc.psum_tensor(f"p3pv_{li}", [128, 512], F32))
        psBC = ph.enter_context(nc.psum_tensor(f"p3pbc_{li}", [128, 512], F32))

        import os
        blocks = [(h, B) for h in range(HPC) for B in range(NQB)]
        nblk = int(os.environ.get("P3_BLOCKS", "0"))
        if nblk:
            blocks = blocks[:nblk]

        def b_cnt(B):
            return 4 * B + 4

        pe_c = [0]
        dve_c = [0]
        act_c = [0]
        st_c = [0]
        for (h, B) in blocks:
            pe_c.append(pe_c[-1] + 3 * b_cnt(B) + 2)
            dve_c.append(dve_c[-1] + 7)
            act_c.append(act_c[-1] + b_cnt(B))
            st_c.append(st_c[-1] + 16)
        b_pe, b_dve, b_act, b_st = p.val(spe), p.val(sdve), p.val(sact), p.val(sst)

        def mm1_at(j):
            return 1 if j == 0 else 3 * j - 1

        def pv_at(B, j):
            nb = b_cnt(B)
            return 3 * (j + 1) + 1 if j < nb - 1 else 3 * nb

        def ztail_at(B):
            return 3 * b_cnt(B) - 1

        def qcol(h, B):
            return h * T + B * 512

        with nc.Block() as blk:
            @blk.tensor
            def _(e):
                for bi, (h, B) in enumerate(blocks):
                    npe = b_pe + pe_c[bi]
                    ndve = b_dve + dve_c[bi]
                    nact = b_act + act_c[bi]
                    nb = b_cnt(B)
                    if bi > 0:
                        e.wait_ge(p.sem(sdve), b_dve + dve_c[bi])
                    for j in range(nb):
                        if j >= 3:
                            e.wait_ge(p.sem(sact), nact + j - 2)
                        e.matmul(psB[j % 3][:],
                                 K_sb[:, h * T + 128 * j: h * T + 128 * j + 128],
                                 Q_sb[:, qcol(h, B): qcol(h, B) + 512],
                                 start=True, stop=True, skip_group_check=True
                                 ).then_inc(p.sem(spe), 1)
                        p.bump(spe, 1)
                        if j >= 1:
                            jj = j - 1
                            e.wait_ge(p.sem(sact), nact + jj + 1)
                            e.matmul(psZ[:], g["ones_128x1"][:], probsT[jj % 2][:],
                                     start=(jj == 0), stop=False,
                                     skip_group_check=True).then_inc(p.sem(spe), 1)
                            p.bump(spe, 1)
                            e.matmul(psPV[:],
                                     V_tm[:, jj * 256 + h * 128: jj * 256 + h * 128 + 128],
                                     probsT[jj % 2][:],
                                     start=(jj == 0), stop=False,
                                     skip_group_check=True).then_inc(p.sem(spe), 1)
                            p.bump(spe, 1)
                    jj = nb - 1
                    e.wait_ge(p.sem(sact), nact + jj + 1)
                    e.matmul(psZ[:], g["ones_128x1"][:], probsT[jj % 2][:],
                             start=(jj == 0), stop=True,
                             skip_group_check=True).then_inc(p.sem(spe), 1)
                    p.bump(spe, 1)
                    e.matmul(psPV[:],
                             V_tm[:, jj * 256 + h * 128: jj * 256 + h * 128 + 128],
                             probsT[jj % 2][:],
                             start=(jj == 0), stop=True,
                             skip_group_check=True).then_inc(p.sem(spe), 1)
                    p.bump(spe, 1)
                    e.wait_ge(p.sem(sdve), ndve + 5)
                    e.matmul(psBC[:], g["ones_1x128_f32"][:], zr[:],
                             start=True, stop=True, skip_group_check=True
                             ).then_inc(p.sem(spe), 1)
                    p.bump(spe, 1)
                    e.matmul(psBC[:], g["ones_1x128_f32"][:], zr[:],
                             start=True, stop=True, skip_group_check=True
                             ).then_inc(p.sem(spe), 1)
                    p.bump(spe, 1)
                    assert p.val(spe) == b_pe + pe_c[bi + 1], (bi, p.val(spe))

            @blk.scalar
            def _(e):
                for bi, (h, B) in enumerate(blocks):
                    npe = b_pe + pe_c[bi]
                    ndve = b_dve + dve_c[bi]
                    nb = b_cnt(B)
                    for j in range(nb):
                        if j >= 4 * B:
                            e.wait_ge(p.sem(sdve), ndve + (j - 4 * B) + 1)
                        else:
                            e.wait_ge(p.sem(spe), npe + mm1_at(j))
                        if j >= 2:
                            e.wait_ge(p.sem(spe), npe + pv_at(B, j - 2))
                        elif bi > 0:
                            e.wait_ge(p.sem(spe), b_pe + pe_c[bi])
                        e.activation(probsT[j % 2][:], psB[j % 3][:], AF.Exp,
                                     scale=SC).then_inc(p.sem(sact), 1)
                        p.bump(sact, 1)

            @blk.vector
            def _(e):
                for bi, (h, B) in enumerate(blocks):
                    npe = b_pe + pe_c[bi]
                    ndve = b_dve + dve_c[bi]
                    nact = b_act + act_c[bi]
                    nb = b_cnt(B)
                    for m in range(4):
                        j = 4 * B + m
                        e.wait_ge(p.sem(spe), npe + mm1_at(j))
                        e.tensor_tensor(psB[j % 3][:], psB[j % 3][:],
                                        maskB[:, 512 * m: 512 * m + 512], ALU.add
                                        ).then_inc(p.sem(sdve), 1)
                        p.bump(sdve, 1)
                    e.wait_ge(p.sem(spe), npe + ztail_at(B) + 1)
                    e.reciprocal(zr[:], psZ[:]).then_inc(p.sem(sdve), 1)
                    p.bump(sdve, 1)
                    e.wait_ge(p.sem(spe), b_pe + pe_c[bi + 1])
                    e.tensor_copy(bc[:], psBC[:]).then_inc(p.sem(sdve), 1)
                    p.bump(sdve, 1)
                    e.wait_ge(p.sem(sdve), ndve + 6)
                    if bi >= 2:
                        e.wait_ge(p.sem(sst), b_st + st_c[bi - 1])
                    e.tensor_tensor(ybuf[bi % 2][:], psPV[:], bc[:], ALU.mult
                                    ).then_inc(p.sem(sdve), 1)
                    p.bump(sdve, 1)

            @blk.gpsimd
            def _(e):
                for bi, (h, B) in enumerate(blocks):
                    e.wait_ge(p.sem(sdve), b_dve + dve_c[bi + 1])
                    e.dma_start(
                        out=p.dram["a2a_in"].ap()[B // 2,
                                                  h * 128:(h + 1) * 128,
                                                  (B % 2) * 512:(B % 2) * 512 + 512],
                        in_=ybuf[bi % 2][:]).then_inc(p.sem(sst), 16)
                    p.bump(sst, 16)
                e.wait_ge(p.sem(sst), b_st + st_c[-1])
                with allow_ici_all_to_all():
                    e.collective_compute(
                        "AllToAll",
                        ALU.bypass,
                        replica_groups=[list(range(NCORE))],
                        ins=[p.dram["a2a_in"].ap().rearrange("r x t -> (r x) t").opt()],
                        outs=[p.dram["a2a_out"].ap().rearrange("r x t -> (r x) t").opt()],
                    ).then_inc(p.sem(scc2), 1)
                p.bump(scc2, 1)
                if "yattn" in p.dbg:
                    e.wait_ge(p.sem(scc2), p.val(scc2))
                    e.dma_start(out=p.dram["dbg_yattn"].ap().rearrange("r x t -> (r x) t"),
                                in_=p.dram["a2a_out"].ap().rearrange("r x t -> (r x) t")
                                ).then_inc(p.sem(sst), 16)
                    p.bump(sst, 16)
                    e.wait_ge(p.sem(sst), p.val(sst))



def emit_p4(p, li, x_in, x_out):
    nc = p.nc
    g = p.g
    pre = f"l{li}_"
    wo_d = p.dram[pre + "wo"]

    sld = f"p4ld_{li}"
    swl = f"p4wl_{li}"
    sdve = f"p4dve_{li}"
    spe = f"p4pe_{li}"
    sst = f"p4st_{li}"
    b_ld, b_wl, b_dve, b_pe, b_st = (p.val(sld), p.val(swl), p.val(sdve),
                                     p.val(spe), p.val(sst))
    scc2 = f"cc_a2a_{li}"
    cc2_t = p.val(scc2)

    with contextlib.ExitStack() as ph:
        yab = ph.enter_context(nc.sbuf_tensor(f"p4y_{li}", [128, CD * 512], BF16))
        fm = ph.enter_context(nc.sbuf_tensor(f"p4fm_{li}", [128, CD * 512], F32))
        wostage = ph.enter_context(nc.sbuf_tensor(f"p4ws_{li}", [128, 2048], F32))
        wob = [ph.enter_context(nc.sbuf_tensor(f"p4wb{i}_{li}", [128, 2048], BF16))
               for i in range(2)]
        ffnb = [ph.enter_context(nc.sbuf_tensor(f"p4fb{i}_{li}", [128, 512], F32))
                for i in range(2)]
        xold = [ph.enter_context(nc.sbuf_tensor(f"p4xo{i}_{li}", [128, D], F32))
                for i in range(2)]
        xnew = [ph.enter_context(nc.sbuf_tensor(f"p4xn{i}_{li}", [128, D], F32))
                for i in range(2)]
        psO = [ph.enter_context(nc.psum_tensor(f"p4po{i}_{li}", [128, 512], F32))
               for i in range(2)]
        psTr = [ph.enter_context(nc.psum_tensor(f"p4pt{i}_{li}", [128, 128], F32))
                for i in range(2)]

        PE_TB = 256 + 64
        DVE_TB = 32 + 64

        with nc.Block() as blk:
            @blk.sync
            def _(e):
                e.wait_ge(p.sem(scc2), cc2_t)
                for tb in range(2):
                    if tb >= 1:
                        e.wait_ge(p.sem(spe), b_pe + PE_TB * tb)
                    e.dma_start(
                        out=yab[:],
                        in_=p.dram["a2a_out"].ap().rearrange(
                            "r (q x) t -> x (r q) t", x=128)
                        [:, :, tb * 512:(tb + 1) * 512]).then_inc(p.sem(sld), 16)
                    p.bump(sld, 16)
                    for oc in range(CD):
                        if tb > 0 or oc >= 2:
                            prev = (tb, oc - 2) if oc >= 2 else (tb - 1, CD - 2)
                            e.wait_ge(p.sem(sdve),
                                      b_dve + DVE_TB * prev[0] + 2 * prev[1] + 2)
                        e.dma_start(
                            out=ffnb[oc % 2][:],
                            in_=p.dram["ffn_out"].ap().rearrange(
                                "(c q) t -> c q t", q=128)
                            [oc, :, tb * 512:(tb + 1) * 512]).then_inc(p.sem(sld), 16)
                        p.bump(sld, 16)
                    for tt in range(4):
                        if tb > 0 or tt >= 2:
                            prev = (tb, tt - 2) if tt >= 2 else (tb - 1, 2)
                            e.wait_ge(p.sem(sdve),
                                      b_dve + DVE_TB * prev[0] + 32 + 16 * (prev[1] + 1))
                        i = tb * 4 + tt
                        e.dma_start(out=xold[tt % 2][:],
                                    in_=x_in.ap()[i * 128:(i + 1) * 128, :]
                                    ).then_inc(p.sem(sld), 16)
                        p.bump(sld, 16)

            @blk.gpsimd
            def _(e):
                for tb in range(2):
                    for oc in range(CD):
                        n = tb * CD + oc
                        if n >= 1:
                            pt, po = (n - 1) // CD, (n - 1) % CD
                            e.wait_ge(p.sem(sdve), b_dve + DVE_TB * pt + 2 * po + 1)
                        if n >= 2:
                            pt, po = (n - 2) // CD, (n - 2) % CD
                            e.wait_ge(p.sem(spe), b_pe + PE_TB * pt + 16 * (po + 1))
                        e.dma_start(
                            out=wostage[:],
                            in_=wo_d.ap()[oc]).then_inc(p.sem(swl), 16)
                        p.bump(swl, 16)
                e.wait_ge(p.sem(sst), b_st + 16 * 8)
                if li == 0 and "x1" in p.dbg:
                    s = p.sem(f"p4dbg_{li}")
                    e.dma_start(out=p.dram["dbg_x1"].ap(), in_=x_out.ap()
                                ).then_inc(s, 16)
                    p.bump(f"p4dbg_{li}", 16)
                    e.wait_ge(s, p.val(f"p4dbg_{li}"))

            @blk.vector
            def _(e):
                for tb in range(2):
                    for oc in range(CD):
                        n = tb * CD + oc
                        e.wait_ge(p.sem(swl), b_wl + 16 * (n + 1))
                        e.tensor_copy(wob[n % 2][:], wostage[:]).then_inc(p.sem(sdve), 1)
                        p.bump(sdve, 1)
                        e.wait_ge(p.sem(spe), b_pe + PE_TB * tb + 16 * (oc + 1))
                        e.wait_ge(p.sem(sld), b_ld + 16 * (21 * tb + 1 + oc + 1))
                        if li == 0:
                            e.scalar_tensor_tensor(
                                fm[:, oc * 512:(oc + 1) * 512], psO[oc % 2][:],
                                g["bo"][:, oc:oc + 1], ffnb[oc % 2][:],
                                ALU.add, ALU.add).then_inc(p.sem(sdve), 1)
                        else:
                            e.tensor_tensor(
                                fm[:, oc * 512:(oc + 1) * 512], psO[oc % 2][:],
                                ffnb[oc % 2][:], ALU.add).then_inc(p.sem(sdve), 1)
                        p.bump(sdve, 1)
                    for tt in range(4):
                        e.wait_ge(p.sem(sld), b_ld + 16 * (21 * tb + 17 + tt + 1))
                        i = tb * 4 + tt
                        if i >= 2:
                            e.wait_ge(p.sem(sst), b_st + 16 * (i - 1))
                        for oc in range(CD):
                            ntr = 16 * tt + oc
                            e.wait_ge(p.sem(spe),
                                      b_pe + PE_TB * tb + 256 + ntr + 1)
                            e.tensor_tensor(
                                xnew[tt % 2][:, oc * 128:(oc + 1) * 128],
                                psTr[ntr % 2][:],
                                xold[tt % 2][:, oc * 128:(oc + 1) * 128],
                                ALU.add).then_inc(p.sem(sdve), 1)
                            p.bump(sdve, 1)


            @blk.scalar
            def _(e):
                for tb in range(2):
                    for tt in range(4):
                        i = tb * 4 + tt
                        e.wait_ge(p.sem(sdve),
                                  b_dve + DVE_TB * tb + 32 + 16 * (tt + 1))
                        if i >= 2:
                            e.wait_ge(p.sem(sst), b_st + 16 * (i - 1))
                        e.dma_start(out=x_out.ap()[i * 128:(i + 1) * 128, :],
                                    in_=xnew[tt % 2][:]).then_inc(p.sem(sst), 16)
                        p.bump(sst, 16)

            @blk.tensor
            def _(e):
                for tb in range(2):
                    e.wait_ge(p.sem(sld), b_ld + 16 * (21 * tb + 1))
                    for oc in range(CD):
                        n = tb * CD + oc
                        e.wait_ge(p.sem(sdve), b_dve + DVE_TB * tb + 2 * oc + 1)
                        if n >= 2:
                            pt, po = (n - 2) // CD, (n - 2) % CD
                            e.wait_ge(p.sem(sdve), b_dve + DVE_TB * pt + 2 * po + 2)
                        for c in range(CD):
                            ins = e.matmul(psO[oc % 2][:],
                                           wob[n % 2][:, c * 128:(c + 1) * 128],
                                           yab[:, c * 512:(c + 1) * 512],
                                           start=(c == 0), stop=(c == CD - 1),
                                           skip_group_check=True)
                            if c == CD - 1:
                                ins.then_inc(p.sem(spe), CD)
                            p.bump(spe, 1)
                    for tt in range(4):
                        for oc in range(CD):
                            ntr = 16 * tt + oc
                            e.wait_ge(p.sem(sdve), b_dve + DVE_TB * tb + 2 * oc + 2)
                            if ntr >= 2:
                                e.wait_ge(p.sem(sdve),
                                          b_dve + DVE_TB * tb + 32 + ntr - 1)
                            e.transpose(psTr[ntr % 2][:],
                                        fm[:, oc * 512 + tt * 128: oc * 512 + tt * 128 + 128],
                                        g["id_f32"][:]).then_inc(p.sem(spe), 1)
                            p.bump(spe, 1)


def kernel(**inp):
    prog = build()
    in_maps = [prep_inputs(inp, c) for c in range(NCORE)]
    res = run_bass_kernel_spmd(prog.nc, in_maps, core_ids=list(range(NCORE)))
    out = np.concatenate([res.results[c]["y"] for c in range(NCORE)], axis=0)
    return out.reshape(1, T, D).astype(np.float32)


# version 07
# speedup vs baseline: 1.1932x; 1.0017x over previous
import contextlib
import math

import numpy as np
import ml_dtypes

import concourse.bass as bass
import concourse.mybir as mybir
from concourse.bass_utils import run_bass_kernel_spmd
from concourse.aot_env import allow_ici_all_to_all

F32 = mybir.dt.float32
BF16 = mybir.dt.bfloat16
AF = mybir.ActivationFunctionType
ALU = mybir.AluOpType
AX = mybir.AxisListType

NCORE = 8
T = 8192
TPC = T // NCORE
D = 2048
NH = 16
HPC = NH // NCORE
HD = 128
INTER = 8192
CD = D // 128
HC = INTER // 128
NEG = -1.0e30
SC = 1.0 / math.sqrt(128.0)
EPS = 1e-5
NTT = TPC // 128
NTB = T // 512
NQB = NTB



def host_constants():
    c = {}
    c["id_bf16"] = np.eye(128, dtype=ml_dtypes.bfloat16)
    c["id_f32"] = np.eye(128, dtype=np.float32)
    c["ones_1x128"] = np.ones((1, 128), dtype=ml_dtypes.bfloat16)
    c["ones_128x1"] = np.ones((128, 1), dtype=ml_dtypes.bfloat16)
    c["ones_1x128_f32"] = np.ones((1, 128), dtype=np.float32)
    c["epsb"] = np.full((128, 1), 1e-5, dtype=np.float32)
    r = np.arange(128)[:, None]
    col = np.arange(512)[None, :]
    mA = np.stack([(col <= 128 * m + r) for m in range(4)])
    c["maskA"] = np.where(mA, 0.0, NEG).astype(np.float32)
    mB = np.stack([(col >= 128 * m + r) for m in range(4)])
    c["maskB"] = np.where(mB, 0.0, NEG).astype(np.float32)
    base = 10000.0
    inv = 1.0 / base ** (np.arange(0, 64, 2, dtype=np.float64) / 64)
    fr = np.outer(inv, np.arange(T, dtype=np.float64))
    c["ropecos"] = np.tile(np.cos(fr), (4, 1)).astype(ml_dtypes.bfloat16)
    sin = np.tile(np.sin(fr), (4, 1))
    sign = np.repeat(np.array([-1.0, 1.0, -1.0, 1.0]), 32)[:, None]
    c["ropesin"] = (sin * sign).astype(ml_dtypes.bfloat16)
    mswap = np.zeros((128, 128), dtype=np.float32)
    for i in range(128):
        mswap[i ^ 32, i] = 1.0
    c["mswap"] = mswap.astype(ml_dtypes.bfloat16)
    mrot = np.zeros((128, 128), dtype=np.float32)
    for i in range(128):
        mrot[(i + 64) % 128, i] = 1.0
    c["mrot"] = mrot.astype(ml_dtypes.bfloat16)
    h = np.array([[1.0]])
    while h.shape[0] < 128:
        h = np.block([[h, h], [h, -h]])
    c["h128"] = h.astype(ml_dtypes.bfloat16)
    return c


_CONSTS = None


def prep_inputs(inp, core):
    global _CONSTS
    d = {}
    h0 = HPC * core
    cc = np.ascontiguousarray
    asnp = np.asarray

    def tile4(w):
        O, I = w.shape
        return cc(w.reshape(O // 128, 128, I // 128, 128).transpose(0, 3, 2, 1))

    def settile(w):
        I, DO = w.shape
        return w.reshape(I // 128, 128, DO).transpose(1, 0, 2)

    wq = asnp(inp["L0_attn_Wq"])[h0:h0 + HPC]
    wk = asnp(inp["L0_attn_Wk"])[h0:h0 + HPC]
    wv = asnp(inp["L0_attn_Wv"])[h0:h0 + HPC]
    d["l0_qkvw"] = cc(np.stack([settile(wq[0]), settile(wq[1]),
                                settile(wk[0]), settile(wk[1]),
                                settile(wv[0]), settile(wv[1])]))
    d["l0_bq"] = cc(asnp(inp["L0_attn_bq"])[h0:h0 + HPC].T)
    d["l0_bk"] = cc(asnp(inp["L0_attn_bk"])[h0:h0 + HPC].T)
    d["l0_bv"] = cc(asnp(inp["L0_attn_bv"])[h0:h0 + HPC].T)
    wo0 = asnp(inp["L0_attn_Wo"])
    d["l0_wo"] = cc(wo0.reshape(CD, 128, CD, 128).transpose(2, 1, 0, 3))
    d["l0_bo"] = cc(asnp(inp["L0_attn_bo"]).reshape(CD, 128).T)
    d["l0_wg"] = tile4(asnp(inp["L0_ffn_W_gate"]))
    d["l0_wu"] = tile4(asnp(inp["L0_ffn_W_up"]))
    d["l0_wd"] = tile4(asnp(inp["L0_ffn_W_down"]))
    d["l0_su_ug"] = cc(asnp(inp["L0_ffn_SU_ug"]).reshape(CD, 128).T)
    d["l0_su_dp"] = cc(asnp(inp["L0_ffn_SU_dp"]).reshape(HC, 128).T)
    d["l0_ws_up"] = cc(asnp(inp["L0_ffn_Wscale_ug"])[:INTER].reshape(HC, 128).T)
    d["l0_ws_g"] = cc(asnp(inp["L0_ffn_Wscale_ug"])[INTER:].reshape(HC, 128).T)
    d["l0_ws_dp"] = cc(asnp(inp["L0_ffn_Wscale_dp"]).reshape(CD, 128).T)

    sl = slice(h0 * 64, (h0 + HPC) * 64)
    slv = slice(h0 * HD, (h0 + HPC) * HD)
    wqg = settile(asnp(inp["L1_attn_Wq_geo"])[sl].T)
    wqs = settile(asnp(inp["L1_attn_Wq_sem"])[sl].T)
    wkg = settile(asnp(inp["L1_attn_Wk_geo"])[sl].T)
    wks = settile(asnp(inp["L1_attn_Wk_sem"])[sl].T)
    wv1 = settile(asnp(inp["L1_attn_Wv"])[slv].T)
    d["l1_qkvw"] = cc(np.stack([wqg, wqs, wkg, wks,
                                wv1[:, :, 0:128], wv1[:, :, 128:256]]))
    wo1 = asnp(inp["L1_attn_Wo"])
    d["l1_wo"] = cc(wo1.reshape(CD, 128, CD, 128).transpose(0, 3, 2, 1))
    d["l1_wg"] = tile4(asnp(inp["L1_ffn_w_gate"]))
    d["l1_wu"] = tile4(asnp(inp["L1_ffn_w_up"]))
    d["l1_wd"] = tile4(asnp(inp["L1_ffn_w_down"]))

    x = asnp(inp["L0_attn_x"]).reshape(T, D)
    d["x0"] = cc(x[core * TPC:(core + 1) * TPC]).astype(np.float32)
    if _CONSTS is None:
        _CONSTS = host_constants()
    d.update(_CONSTS)
    return d



class Prog:
    def __init__(self, layers=(0, 1), dbg=(), phases=("p0", "p1", "p2", "p3", "p4")):
        self.nc = bass.Bass("TRN2", target_bir_lowering=False)
        self.ctx = contextlib.ExitStack()
        self.dbg = set(dbg)
        self.phases = set(phases)
        self.layers = layers
        self.dram = {}
        self.g = {}
        self.semv = {}

    def sem(self, name):
        if name not in self.semv:
            h = self.ctx.enter_context(self.nc.semaphore(name))
            self.semv[name] = [h, 0]
        return self.semv[name][0]

    def bump(self, name, k=1):
        self.sem(name)
        self.semv[name][1] += k
        return self.semv[name][1]

    def val(self, name):
        self.sem(name)
        return self.semv[name][1]

    def sbuf(self, name, shape, dtype):
        return self.ctx.enter_context(self.nc.sbuf_tensor(name, shape, dtype))

    def psum(self, name, shape, dtype):
        return self.ctx.enter_context(self.nc.psum_tensor(name, shape, dtype))

    def din(self, name, shape, dtype=F32):
        t = self.nc.dram_tensor(name, shape, dtype, kind="ExternalInput")
        self.dram[name] = t
        return t

    def dout(self, name, shape, dtype=F32):
        t = self.nc.dram_tensor(name, shape, dtype, kind="ExternalOutput")
        self.dram[name] = t
        return t

    def dint(self, name, shape, dtype=F32):
        t = self.nc.dram_tensor(name, shape, dtype)
        self.dram[name] = t
        return t

    def barrier(self):
        nc = self.nc
        s = self.sem("bar")
        base = self.semv["bar"][1]
        with nc.Block() as blk:
            @blk.sync
            def _(e):
                e.nop().then_inc(s, 1)
                e.wait_ge(s, base + 5)

            @blk.tensor
            def _(e):
                e.nop().then_inc(s, 1)
                e.wait_ge(s, base + 5)

            @blk.vector
            def _(e):
                e.nop().then_inc(s, 1)
                e.wait_ge(s, base + 5)

            @blk.scalar
            def _(e):
                e.nop().then_inc(s, 1)
                e.wait_ge(s, base + 5)

            @blk.gpsimd
            def _(e):
                e.nop().then_inc(s, 1)
                e.wait_ge(s, base + 5)
        self.semv["bar"][1] = base + 5


def build(layers=(0, 1), dbg=(), phases=("p0", "p1", "p2", "p3", "p4")):
    p = Prog(layers, dbg, phases)
    nc = p.nc

    p.din("l0_qkvw", [6, 128, CD, 128])
    p.din("l0_bq", [128, HPC]); p.din("l0_bk", [128, HPC]); p.din("l0_bv", [128, HPC])
    p.din("l0_wo", [CD, 128, CD, 128]); p.din("l0_bo", [128, CD])
    p.din("l0_wg", [HC, 128, CD, 128]); p.din("l0_wu", [HC, 128, CD, 128])
    p.din("l0_wd", [CD, 128, HC, 128])
    p.din("l0_su_ug", [128, CD]); p.din("l0_su_dp", [128, HC])
    p.din("l0_ws_up", [128, HC]); p.din("l0_ws_g", [128, HC]); p.din("l0_ws_dp", [128, CD])
    p.din("l1_qkvw", [6, 128, CD, 128])
    p.din("l1_wo", [CD, 128, CD, 128])
    p.din("l1_wg", [HC, 128, CD, 128]); p.din("l1_wu", [HC, 128, CD, 128])
    p.din("l1_wd", [CD, 128, HC, 128])
    p.din("x0", [TPC, D])
    p.din("id_bf16", [128, 128], BF16); p.din("id_f32", [128, 128], F32)
    p.din("ones_1x128", [1, 128], BF16); p.din("ones_128x1", [128, 1], BF16)
    p.din("ones_1x128_f32", [1, 128], F32)
    p.din("maskA", [4, 128, 512]); p.din("maskB", [4, 128, 512])
    p.din("ropecos", [128, T], BF16); p.din("ropesin", [128, T], BF16)
    p.din("h128", [128, 128], BF16); p.din("epsb", [128, 1])
    p.din("mswap", [128, 128], BF16); p.din("mrot", [128, 128], BF16)

    p.dout("y", [TPC, D])
    p.dint("x1", [TPC, D])
    p.dint("ag_in", [D, TPC], BF16)
    p.dint("ag_out", [NCORE, D, TPC], BF16)
    p.dint("a2a_in", [NCORE, HPC * HD, TPC], BF16)
    p.dint("a2a_out", [NCORE, HPC * HD, TPC], BF16)
    p.dint("ffn_out", [D, TPC], F32)
    p.dint("wbf", [3 * D * INTER], BF16)

    if "xn_full" in p.dbg:
        p.dout("dbg_xn_full", [NCORE, D, TPC], BF16)
    if "z" in p.dbg:
        p.dout("dbg_z", [128, CD * TPC], BF16)
    if "ffn_out" in p.dbg:
        p.dout("dbg_ffn_out", [D, TPC], F32)
    if "qkv" in p.dbg:
        p.dout("dbg_q", [128, HPC * T], BF16)
        p.dout("dbg_k", [128, HPC * T], BF16)
        p.dout("dbg_v", [128, (T // 128) * HPC * HD], BF16)
    if "yattn" in p.dbg:
        p.dout("dbg_yattn", [NCORE, HPC * HD, TPC], BF16)
    if "x1" in p.dbg:
        p.dout("dbg_x1", [TPC, D])

    g = p.g
    g["id_bf16"] = p.sbuf("g_id_bf16", [128, 128], BF16)
    g["id_f32"] = p.sbuf("g_id_f32", [128, 128], F32)
    g["ones_1x128"] = p.sbuf("g_ones_1x128", [1, 128], BF16)
    g["ones_128x1"] = p.sbuf("g_ones_128x1", [128, 1], BF16)
    g["ones_1x128_f32"] = p.sbuf("g_ones_1x128_f32", [1, 128], F32)
    g["h128"] = p.sbuf("g_h128", [128, 128], BF16)
    g["epsb"] = p.sbuf("g_epsb", [128, 1], F32)
    g["mswap"] = p.sbuf("g_mswap", [128, 128], BF16)
    g["mrot"] = p.sbuf("g_mrot", [128, 128], BF16)
    g["su_ug"] = p.sbuf("g_su_ug", [128, CD], F32)
    g["su_sup"] = p.sbuf("g_su_sup", [128, HC], F32)
    g["ws_g"] = p.sbuf("g_ws_g", [128, HC], F32)
    g["ws_dp"] = p.sbuf("g_ws_dp", [128, CD], F32)
    g["bq"] = p.sbuf("g_bq", [128, HPC], F32)
    g["bk"] = p.sbuf("g_bk", [128, HPC], F32)
    g["bv"] = p.sbuf("g_bv", [128, HPC], F32)
    g["bo"] = p.sbuf("g_bo", [128, CD], F32)

    with nc.Block() as blk:
        @blk.sync
        def _(e):
            for name in ("id_bf16", "id_f32", "ones_1x128", "ones_128x1",
                         "ones_1x128_f32", "h128", "epsb", "mswap", "mrot"):
                e.dma_start(out=g[name][:], in_=p.dram[name].ap()).then_inc(p.sem("ldc"), 16)
                p.bump("ldc", 16)
            for src, dst in (("l0_su_ug", "su_ug"), ("l0_ws_g", "ws_g"),
                             ("l0_ws_dp", "ws_dp"), ("l0_bq", "bq"),
                             ("l0_bk", "bk"), ("l0_bv", "bv"), ("l0_bo", "bo")):
                e.dma_start(out=g[dst][:], in_=p.dram[src].ap()).then_inc(p.sem("ldc"), 16)
                p.bump("ldc", 16)
            e.dma_start(out=g["su_sup"][:], in_=p.dram["l0_su_dp"].ap()).then_inc(p.sem("ldc"), 16)
            p.bump("ldc", 16)

        @blk.vector
        def _(e):
            e.wait_ge(p.sem("ldc"), p.val("ldc"))
            e.tensor_scalar(g["ws_g"][:], g["ws_g"][:], 1.0 / math.sqrt(D), None, ALU.mult)
            e.tensor_scalar(g["ws_dp"][:], g["ws_dp"][:], 1.0 / math.sqrt(INTER), None, ALU.mult)
            e.nop().then_inc(p.sem("vconst"), 1)
            p.bump("vconst", 1)

    wsu_tmp = p.sbuf("g_wsu_tmp", [128, HC], F32)
    with nc.Block() as blk:
        @blk.sync
        def _(e):
            e.dma_start(out=wsu_tmp[:], in_=p.dram["l0_ws_up"].ap()).then_inc(p.sem("ldc"), 16)
            p.bump("ldc", 16)

        @blk.vector
        def _(e):
            e.wait_ge(p.sem("ldc"), p.val("ldc"))
            e.tensor_tensor(g["su_sup"][:], g["su_sup"][:], wsu_tmp[:], ALU.mult
                            ).then_inc(p.sem("vconst"), 1)
            p.bump("vconst", 1)
            e.wait_ge(p.sem("vconst"), p.val("vconst"))
            e.tensor_scalar(g["su_sup"][:], g["su_sup"][:], 1.0 / math.sqrt(D), None, ALU.mult
                            ).then_inc(p.sem("vconst"), 1)
            p.bump("vconst", 1)

    for li in layers:
        emit_layer(p, li)

    p.ctx.close()
    return p


def emit_layer(p, li):
    last = (li == max(p.layers))
    if li == 0:
        x_in = p.dram["x0"]
    elif 0 not in p.layers:
        x_in = p.din("x1_in", [TPC, D])
    else:
        x_in = p.dram["x1"]
    x_out = p.dram["y"] if last else p.dram["x1"]
    with p.nc.sbuf_tensor(f"z_sb_{li}", [128, CD * TPC], BF16) as z_sb:
        p.g["z_sb"] = z_sb
        if "p0" in p.phases:
            emit_p0(p, li, x_in)
        if "p1" in p.phases:
            emit_p1(p, li)
    if "p2" in p.phases:
        emit_p2p3(p, li)
    if "p4" in p.phases:
        emit_p4(p, li, x_in, x_out)
    emit_final_touch(p, li, x_out)
    p.barrier()



def emit_p0(p, li, x_in):
    nc = p.nc
    g = p.g
    z_sb = p.g["z_sb"]

    with contextlib.ExitStack() as ph:
        xt = [ph.enter_context(nc.sbuf_tensor(f"p0x{i}_{li}", [128, D], F32)) for i in range(2)]
        sq = ph.enter_context(nc.sbuf_tensor(f"p0sq_{li}", [128, D], F32))
        xn = [ph.enter_context(nc.sbuf_tensor(f"p0xn{i}_{li}", [128, D], BF16)) for i in range(2)]
        st = ph.enter_context(nc.sbuf_tensor(f"p0st_{li}", [128, 16], F32))
        if li == 0:
            stage = [ph.enter_context(nc.sbuf_tensor(f"p0stage{i}_{li}", [128, CD * 512], BF16))
                     for i in range(2)]
            zb = [ph.enter_context(nc.sbuf_tensor(f"p0zb{i}_{li}", [128, CD * 512], BF16))
                  for i in range(2)]
            psz = [ph.enter_context(nc.psum_tensor(f"p0pz{i}_{li}", [128, 512], F32))
                   for i in range(2)]
        pst = [ph.enter_context(nc.psum_tensor(f"p0pt{i}_{li}", [128, 128], BF16))
               for i in range(4)]

        sld = f"p0ld_{li}"
        sdve = f"p0dve_{li}"
        sact = f"p0act_{li}"
        spe = f"p0pe_{li}"
        sst = f"p0sst_{li}"
        szdve = f"p0zdve_{li}"
        szpe = f"p0zpe_{li}"
        scc = f"cc_ag_{li}"


        def act_n(i, k):
            return 2 * i + k

        def dve_stats_n(i, k):
            return 23 * i + k

        def dve_n(i):
            return 23 * i + 23

        def pe_n(i, j):
            return 16 * i + j

        base_ld = p.val(sld)
        base_act = p.val(sact)
        base_dve = p.val(sdve)
        base_pe = p.val(spe)
        base_st = p.val(sst)
        base_zdve = p.val(szdve)
        base_zpe = p.val(szpe)
        vconst_target = p.val("vconst")

        nbt = NTT // 4

        with nc.Block() as blk:
            @blk.sync
            def _(e):
                e.wait_ge(p.sem("ldc"), p.val("ldc"))
                for i in range(NTT):
                    if i >= 2:
                        e.wait_ge(p.sem(sdve), base_dve + dve_stats_n(i - 2, 7))
                    e.dma_start(out=xt[i % 2][:],
                                in_=x_in.ap()[i * 128:(i + 1) * 128, :]
                                ).then_inc(p.sem(sld), 16)
                    p.bump(sld, 16)

            @blk.scalar
            def _(e):
                for i in range(NTT):
                    e.wait_ge(p.sem(sld), base_ld + 16 * (i + 1))
                    e.activation(sq[:], xt[i % 2][:], AF.Square,
                                 accum_out=st[:, 8 * (i % 2) + 1:8 * (i % 2) + 2]
                                 ).then_inc(p.sem(sact), 1)
                    p.bump(sact, 1)
                    e.wait_ge(p.sem(sdve), base_dve + dve_stats_n(i, 4))
                    e.activation(st[:, 8 * (i % 2) + 5:8 * (i % 2) + 6],
                                 st[:, 8 * (i % 2) + 4:8 * (i % 2) + 5],
                                 AF.Sqrt, bias=g["epsb"][:]).then_inc(p.sem(sact), 1)
                    p.bump(sact, 1)

            @blk.vector
            def _(e):
                e.wait_ge(p.sem("vconst"), vconst_target)
                for i in range(NTT):
                    o = 8 * (i % 2)
                    e.wait_ge(p.sem(sld), base_ld + 16 * (i + 1))
                    e.tensor_reduce(st[:, o:o + 1], xt[i % 2][:], AX.X, ALU.add
                                    ).then_inc(p.sem(sdve), 1)
                    p.bump(sdve, 1)
                    e.wait_ge(p.sem(sdve), base_dve + dve_stats_n(i, 1))
                    e.tensor_scalar(st[:, o + 2:o + 3], st[:, o:o + 1],
                                    1.0 / D, None, ALU.mult).then_inc(p.sem(sdve), 1)
                    p.bump(sdve, 1)
                    e.wait_ge(p.sem(sdve), base_dve + dve_stats_n(i, 2))
                    e.tensor_tensor(st[:, o + 3:o + 4], st[:, o + 2:o + 3],
                                    st[:, o + 2:o + 3], ALU.mult).then_inc(p.sem(sdve), 1)
                    p.bump(sdve, 1)
                    e.wait_ge(p.sem(sact), base_act + act_n(i, 1))
                    e.wait_ge(p.sem(sdve), base_dve + dve_stats_n(i, 3))
                    e.scalar_tensor_tensor(st[:, o + 4:o + 5], st[:, o + 1:o + 2],
                                           1.0 / D, st[:, o + 3:o + 4],
                                           ALU.mult, ALU.subtract).then_inc(p.sem(sdve), 1)
                    p.bump(sdve, 1)
                    e.wait_ge(p.sem(sact), base_act + act_n(i, 2))
                    e.reciprocal(st[:, o + 6:o + 7], st[:, o + 5:o + 6]
                                 ).then_inc(p.sem(sdve), 1)
                    p.bump(sdve, 1)
                    e.wait_ge(p.sem(sdve), base_dve + dve_stats_n(i, 5))
                    e.tensor_scalar(st[:, o + 7:o + 8], st[:, o + 2:o + 3],
                                    st[:, o + 6:o + 7], -1.0, ALU.mult, ALU.mult
                                    ).then_inc(p.sem(sdve), 1)
                    p.bump(sdve, 1)
                    e.wait_ge(p.sem(sdve), base_dve + dve_stats_n(i, 6))
                    e.tensor_scalar(xn[i % 2][:], xt[i % 2][:],
                                    st[:, o + 6:o + 7], st[:, o + 7:o + 8],
                                    ALU.mult, ALU.add).then_inc(p.sem(sdve), 1)
                    p.bump(sdve, 1)
                    for j in range(CD):
                        e.wait_ge(p.sem(spe), base_pe + pe_n(i, j + 1))
                        if li == 0:
                            dst = stage[i // 4][:, j * 512 + (i % 4) * 128:
                                               j * 512 + (i % 4) * 128 + 128]
                        else:
                            dst = z_sb[:, j * TPC + i * 128: j * TPC + i * 128 + 128]
                        e.tensor_copy(dst, pst[j % 4][:]).then_inc(p.sem(sdve), 1)
                        p.bump(sdve, 1)

            @blk.tensor
            def _(e):
                e.wait_ge(p.sem("ldc"), p.val("ldc"))
                for i in range(NTT):
                    e.wait_ge(p.sem(sdve), base_dve + dve_stats_n(i, 7))
                    for j in range(CD):
                        n = 16 * i + j
                        if n >= 4:
                            ip, jp = (n - 4) // 16, (n - 4) % 16
                            e.wait_ge(p.sem(sdve), base_dve + 23 * ip + 7 + jp + 1)
                        e.transpose(pst[j % 4][:], xn[i % 2][:, j * 128:(j + 1) * 128],
                                    g["id_bf16"][:]).then_inc(p.sem(spe), 1)
                        p.bump(spe, 1)

            @blk.gpsimd
            def _(e):
                for bidx in range(nbt):
                    e.wait_ge(p.sem(sdve), base_dve + dve_n(4 * bidx + 3))
                    for j in range(CD):
                        if li == 0:
                            sap = stage[bidx][:, j * 512:(j + 1) * 512]
                        else:
                            sap = z_sb[:, j * TPC + bidx * 512: j * TPC + bidx * 512 + 512]
                        dst = p.dram["ag_in"].ap().rearrange(
                            "(c p) t -> c p t", p=128)[j, :, bidx * 512:(bidx + 1) * 512]
                        e.dma_start(out=dst, in_=sap).then_inc(p.sem(sst), 16)
                        p.bump(sst, 16)
                e.wait_ge(p.sem(sst), base_st + 16 * CD * nbt)
                e.collective_compute(
                    "AllGather", ALU.bypass,
                    replica_groups=[list(range(NCORE))],
                    ins=[p.dram["ag_in"].ap().opt()],
                    outs=[p.dram["ag_out"].ap().rearrange("r c t -> (r c) t").opt()],
                ).then_inc(p.sem(scc), 1)
                p.bump(scc, 1)
                if "xn_full" in p.dbg and li == 0:
                    e.wait_ge(p.sem(scc), 1)
                    e.dma_start(out=p.dram["dbg_xn_full"].ap().rearrange("r c t -> (r c) t"),
                                in_=p.dram["ag_out"].ap().rearrange("r c t -> (r c) t")
                                ).then_inc(p.sem(sst), 16)
                    p.bump(sst, 16)

        if li == 0:
            with nc.Block() as blk:
                @blk.vector
                def _(e):
                    for bidx in range(nbt):
                        e.wait_ge(p.sem(sdve), base_dve + dve_n(4 * bidx + 3))
                        if bidx >= 1:
                            e.wait_ge(p.sem(szpe), base_zpe + bidx * CD)
                        for c in range(CD):
                            ins = e.tensor_scalar(zb[0][:, c * 512:(c + 1) * 512],
                                                  stage[bidx][:, c * 512:(c + 1) * 512],
                                                  g["su_ug"][:, c:c + 1], None, ALU.mult)
                            if c == CD - 1:
                                ins.then_inc(p.sem(szdve), 1)
                                p.bump(szdve, 1)
                        e.wait_ge(p.sem(szdve), base_zdve + bidx * 21 + 1)
                        cur, nxt = 0, 1
                        for si, hstep in enumerate((1, 2, 4, 8)):
                            for c in range(CD):
                                pr = c ^ hstep
                                ca, cb = (c, pr) if (c & hstep) == 0 else (pr, c)
                                op = ALU.add if (c & hstep) == 0 else ALU.subtract
                                ins = e.tensor_tensor(
                                    zb[nxt][:, c * 512:(c + 1) * 512],
                                    zb[cur][:, ca * 512:(ca + 1) * 512],
                                    zb[cur][:, cb * 512:(cb + 1) * 512],
                                    op)
                                if c == CD - 1:
                                    ins.then_inc(p.sem(szdve), 1)
                                    p.bump(szdve, 1)
                            e.wait_ge(p.sem(szdve), base_zdve + bidx * 21 + 2 + si)
                            cur, nxt = nxt, cur
                        for c in range(CD):
                            e.wait_ge(p.sem(szpe), base_zpe + bidx * CD + c + 1)
                            e.tensor_copy(
                                z_sb[:, c * TPC + bidx * 512: c * TPC + bidx * 512 + 512],
                                psz[c % 2][:]).then_inc(p.sem(szdve), 1)
                            p.bump(szdve, 1)

                @blk.tensor
                def _(e):
                    for bidx in range(nbt):
                        e.wait_ge(p.sem(szdve), base_zdve + bidx * 21 + 5)
                        for c in range(CD):
                            if c >= 2:
                                e.wait_ge(p.sem(szdve), base_zdve + bidx * 21 + 5 + (c - 2) + 1)
                            e.matmul(psz[c % 2][:], g["h128"][:],
                                     zb[0][:, c * 512:(c + 1) * 512],
                                     start=True, stop=True, skip_group_check=True
                                     ).then_inc(p.sem(szpe), 1)
                            p.bump(szpe, 1)

        if "z" in p.dbg and li == 0:
            with nc.Block() as blk:
                @blk.gpsimd
                def _(e):
                    e.wait_ge(p.sem(szdve), p.val(szdve))
                    e.dma_start(out=p.dram["dbg_z"].ap(), in_=z_sb[:]).then_inc(
                        p.sem(sst), 16)
                    p.bump(sst, 16)
                    e.wait_ge(p.sem(sst), p.val(sst))

    p.barrier()



OFF_U = D * INTER
OFF_D = 2 * D * INTER


def emit_p1(p, li):
    nc = p.nc
    g = p.g
    z_sb = g["z_sb"]
    pre = f"l{li}_"
    wg_d, wu_d, wd_d = p.dram[pre + "wg"], p.dram[pre + "wu"], p.dram[pre + "wd"]
    wbf = p.dram["wbf"]

    for tb in (0, 1):
        pass0 = (tb == 0)
        with nc.sbuf_tensor(f"h_sb_{li}_{tb}", [128, HC * 512], BF16) as h_sb:
            p.g["h_sb"] = h_sb
            emit_p1_gateup(p, li, tb, pass0)
            p.barrier()
            if li == 0:
                emit_p1_fwht2(p, li, tb)
                p.barrier()
            emit_p1_down(p, li, tb, pass0)
            p.barrier()

    if "ffn_out" in p.dbg:
        with nc.Block() as blk:
            @blk.gpsimd
            def _(e):
                s = p.sem(f"p1dbg_{li}")
                e.dma_start(out=p.dram["dbg_ffn_out"].ap(),
                            in_=p.dram["ffn_out"].ap()).then_inc(s, 16)
                p.bump(f"p1dbg_{li}", 16)
                e.wait_ge(s, p.val(f"p1dbg_{li}"))


def emit_p1_gateup(p, li, tb, pass0):
    nc = p.nc
    g = p.g
    z_sb = g["z_sb"]
    pre = f"l{li}_"
    wg_d, wu_d = p.dram[pre + "wg"], p.dram[pre + "wu"]
    wbf = p.dram["wbf"]

    sld = f"p1ld_{li}_{tb}"
    sldu = f"p1ldu_{li}_{tb}"
    sdve = f"p1dve_{li}_{tb}"
    sact = f"p1act_{li}_{tb}"
    spe = f"p1pe_{li}_{tb}"
    swb = f"p1wb_{li}_{tb}"
    b_ld, b_ldu, b_dve, b_act, b_pe, b_wb = (p.val(sld), p.val(sldu), p.val(sdve),
                                             p.val(sact), p.val(spe), p.val(swb))
    vconst_t = p.val("vconst")
    vo = 2 if pass0 else 1

    h_sb = p.g["h_sb"]
    with contextlib.ExitStack() as ph:
        wstage = [ph.enter_context(nc.sbuf_tensor(f"p1ws{i}_{li}_{tb}", [128, 4096], F32))
                  for i in range(2)] if pass0 else None
        wb = [ph.enter_context(nc.sbuf_tensor(f"p1wb{i}_{li}_{tb}", [128, 4096], BF16))
              for i in range(2)]
        sg = [ph.enter_context(nc.sbuf_tensor(f"p1sg{i}_{li}_{tb}", [128, 512], BF16))
              for i in range(2)]
        psG = [ph.enter_context(nc.psum_tensor(f"p1pg{i}_{li}_{tb}", [128, 512], F32))
               for i in range(2)]
        psU = [ph.enter_context(nc.psum_tensor(f"p1pu{i}_{li}_{tb}", [128, 512], F32))
               for i in range(2)]

        with nc.Block() as blk:
            @blk.sync
            def _(e):
                for hc in range(HC):
                    if pass0:
                        if hc >= 2:
                            e.wait_ge(p.sem(sdve), b_dve + vo * (hc - 2) + 1)
                        e.dma_start(
                            out=wstage[hc % 2][:, 0:2048],
                            in_=wg_d.ap()[hc]
                        ).then_inc(p.sem(sld), 16)
                        p.bump(sld, 16)
                    else:
                        if hc >= 2:
                            e.wait_ge(p.sem(spe), b_pe + 32 * (hc - 1))
                        for half, off in ((0, hc * 262144), (1, OFF_U + hc * 262144)):
                            e.dma_start(
                                out=wb[hc % 2][:, half * 2048:(half + 1) * 2048],
                                in_=wbf.ap()[off:off + 262144].rearrange(
                                    "(q w) -> q w", q=128)
                            ).then_inc(p.sem(sld), 16)
                            p.bump(sld, 16)

            @blk.vector
            def _(e):
                e.wait_ge(p.sem("vconst"), vconst_t)
                for hc in range(HC):
                    if pass0:
                        e.wait_ge(p.sem(sld), b_ld + 16 * (hc + 1))
                        e.wait_ge(p.sem(sldu), b_ldu + 16 * (hc + 1))
                        if hc >= 2:
                            e.wait_ge(p.sem(spe), b_pe + 32 * (hc - 1))
                            e.wait_ge(p.sem(swb), b_wb + 32 * (hc - 1))
                        e.tensor_copy(wb[hc % 2][:], wstage[hc % 2][:]
                                      ).then_inc(p.sem(sdve), 1)
                        p.bump(sdve, 1)
                    e.wait_ge(p.sem(sact), b_act + hc + 1)
                    e.wait_ge(p.sem(spe), b_pe + 32 * (hc + 1))
                    dst = h_sb[:, hc * 512:(hc + 1) * 512]
                    if li == 0:
                        e.scalar_tensor_tensor(dst, sg[hc % 2][:],
                                               g["su_sup"][:, hc:hc + 1],
                                               psU[hc % 2][:],
                                               ALU.mult, ALU.mult
                                               ).then_inc(p.sem(sdve), 1)
                    else:
                        e.tensor_tensor(dst, sg[hc % 2][:], psU[hc % 2][:],
                                        ALU.mult).then_inc(p.sem(sdve), 1)
                    p.bump(sdve, 1)

            @blk.scalar
            def _(e):
                e.wait_ge(p.sem("vconst"), vconst_t)
                for hc in range(HC):
                    if pass0:
                        if hc >= 2:
                            e.wait_ge(p.sem(sdve), b_dve + vo * (hc - 2) + 1)
                        e.dma_start(
                            out=wstage[hc % 2][:, 2048:4096],
                            in_=wu_d.ap()[hc]
                        ).then_inc(p.sem(sldu), 16)
                        p.bump(sldu, 16)
                    e.wait_ge(p.sem(spe), b_pe + 32 * hc + 16)
                    if hc >= 2:
                        e.wait_ge(p.sem(sdve), b_dve + vo * (hc - 2) + vo)
                    if li == 0:
                        e.activation(sg[hc % 2][:], psG[hc % 2][:], AF.Silu,
                                     scale=g["ws_g"][:, hc:hc + 1]
                                     ).then_inc(p.sem(sact), 1)
                    else:
                        e.activation(sg[hc % 2][:], psG[hc % 2][:], AF.Silu
                                     ).then_inc(p.sem(sact), 1)
                    p.bump(sact, 1)

            @blk.tensor
            def _(e):
                for hc in range(HC):
                    if pass0:
                        e.wait_ge(p.sem(sdve), b_dve + vo * hc + 1)
                    else:
                        e.wait_ge(p.sem(sld), b_ld + 32 * (hc + 1))
                    if hc >= 2:
                        e.wait_ge(p.sem(sact), b_act + hc - 1)
                        e.wait_ge(p.sem(sdve), b_dve + vo * (hc - 2) + vo)
                    for cc in range(CD):
                        ins = e.matmul(psG[hc % 2][:], wb[hc % 2][:, cc * 128:(cc + 1) * 128],
                                       z_sb[:, cc * TPC + tb * 512: cc * TPC + tb * 512 + 512],
                                       start=(cc == 0), stop=(cc == CD - 1),
                                       skip_group_check=True)
                        if cc == CD - 1:
                            ins.then_inc(p.sem(spe), CD)
                        p.bump(spe, 1)
                    for cc in range(CD):
                        ins = e.matmul(psU[hc % 2][:],
                                       wb[hc % 2][:, 2048 + cc * 128: 2048 + (cc + 1) * 128],
                                       z_sb[:, cc * TPC + tb * 512: cc * TPC + tb * 512 + 512],
                                       start=(cc == 0), stop=(cc == CD - 1),
                                       skip_group_check=True)
                        if cc == CD - 1:
                            ins.then_inc(p.sem(spe), CD)
                        p.bump(spe, 1)

            @blk.gpsimd
            def _(e):
                if pass0:
                    for hc in range(HC):
                        e.wait_ge(p.sem(sdve), b_dve + vo * hc + 1)
                        for half, off in ((0, hc * 262144), (1, OFF_U + hc * 262144)):
                            e.dma_start(
                                out=wbf.ap()[off:off + 262144].rearrange(
                                    "(q w) -> q w", q=128),
                                in_=wb[hc % 2][:, half * 2048:(half + 1) * 2048]
                            ).then_inc(p.sem(swb), 16)
                            p.bump(swb, 16)


def emit_p1_fwht2(p, li, tb):
    nc = p.nc
    g = p.g
    h_sb = p.g["h_sb"]
    sdve = f"p1fdve_{li}_{tb}"
    spe = f"p1fpe_{li}_{tb}"
    b_dve, b_pe = p.val(sdve), p.val(spe)

    with contextlib.ExitStack() as ph:
        h2 = ph.enter_context(nc.sbuf_tensor(f"p1h2_{li}_{tb}", [128, HC * 256], BF16))
        psZ2 = [ph.enter_context(nc.psum_tensor(f"p1pz{i}_{li}_{tb}", [128, 512], F32))
                for i in range(2)]

        nst = 0
        with nc.Block() as blk:
            @blk.vector
            def _(e):
                cnt = 0
                for sub in range(2):
                    for si, hstep in enumerate((1, 2, 4, 8, 16, 32)):
                        src_is_h = (si % 2 == 0)
                        for c in range(HC):
                            pr = c ^ hstep
                            ca, cb = (c, pr) if (c & hstep) == 0 else (pr, c)
                            op = ALU.add if (c & hstep) == 0 else ALU.subtract
                            if src_is_h:
                                sa = h_sb[:, ca * 512 + sub * 256: ca * 512 + sub * 256 + 256]
                                sb_ = h_sb[:, cb * 512 + sub * 256: cb * 512 + sub * 256 + 256]
                                dst = h2[:, c * 256:(c + 1) * 256]
                            else:
                                sa = h2[:, ca * 256:(ca + 1) * 256]
                                sb_ = h2[:, cb * 256:(cb + 1) * 256]
                                dst = h_sb[:, c * 512 + sub * 256: c * 512 + sub * 256 + 256]
                            ins = e.tensor_tensor(dst, sa, sb_, op)
                            if c == HC - 1:
                                ins.then_inc(p.sem(sdve), 1)
                                p.bump(sdve, 1)
                                cnt += 1
                                e.wait_ge(p.sem(sdve), b_dve + cnt)
                nevac = 0
                for hc in range(HC):
                    e.wait_ge(p.sem(spe), b_pe + hc + 1)
                    e.tensor_copy(h_sb[:, hc * 512:(hc + 1) * 512],
                                  psZ2[hc % 2][:]).then_inc(p.sem(sdve), 1)
                    p.bump(sdve, 1)

            @blk.tensor
            def _(e):
                e.wait_ge(p.sem(sdve), b_dve + 12)
                for hc in range(HC):
                    if hc >= 2:
                        e.wait_ge(p.sem(sdve), b_dve + 12 + (hc - 2) + 1)
                    e.matmul(psZ2[hc % 2][:], g["h128"][:],
                             h_sb[:, hc * 512:(hc + 1) * 512],
                             start=True, stop=True, skip_group_check=True
                             ).then_inc(p.sem(spe), 1)
                    p.bump(spe, 1)


def emit_p1_down(p, li, tb, pass0):
    nc = p.nc
    g = p.g
    h_sb = p.g["h_sb"]
    pre = f"l{li}_"
    wd_d = p.dram[pre + "wd"]
    wbf = p.dram["wbf"]

    sld = f"p1dld_{li}_{tb}"
    sdve = f"p1ddve_{li}_{tb}"
    spe = f"p1dpe_{li}_{tb}"
    swb = f"p1dwb_{li}_{tb}"
    sst = f"p1dst_{li}_{tb}"
    b_ld, b_dve, b_pe, b_wb, b_st = (p.val(sld), p.val(sdve), p.val(spe),
                                     p.val(swb), p.val(sst))
    vconst_t = p.val("vconst")
    vo = 5 if pass0 else 1

    with contextlib.ExitStack() as ph:
        wdb = [ph.enter_context(nc.sbuf_tensor(f"p1wd{i}_{li}_{tb}", [128, 8192], BF16))
               for i in range(2)]
        wstage = ph.enter_context(nc.sbuf_tensor(f"p1dws_{li}_{tb}", [128, 2048], F32)) \
            if pass0 else None
        dn = [ph.enter_context(nc.sbuf_tensor(f"p1dn{i}_{li}_{tb}", [128, 512], F32))
              for i in range(2)]
        psD = [ph.enter_context(nc.psum_tensor(f"p1pd{i}_{li}_{tb}", [128, 512], F32))
               for i in range(2)]

        with nc.Block() as blk:
            @blk.sync
            def _(e):
                for oc in range(CD):
                    if pass0:
                        for grp in range(4):
                            n = 4 * oc + grp
                            if n >= 1:
                                e.wait_ge(p.sem(sdve), b_dve + vo * ((n - 1) // 4)
                                          + ((n - 1) % 4) + 1)
                            e.dma_start(
                                out=wstage[:],
                                in_=wd_d.ap()[oc, :, 16 * grp:16 * (grp + 1), :]
                            ).then_inc(p.sem(sld), 16)
                            p.bump(sld, 16)
                    else:
                        if oc >= 2:
                            e.wait_ge(p.sem(spe), b_pe + 64 * (oc - 1))
                        off = OFF_D + oc * 1048576
                        e.dma_start(
                            out=wdb[oc % 2][:],
                            in_=wbf.ap()[off:off + 1048576].rearrange(
                                "(q w) -> q w", q=128)
                        ).then_inc(p.sem(sld), 16)
                        p.bump(sld, 16)

            @blk.vector
            def _(e):
                e.wait_ge(p.sem("vconst"), vconst_t)
                for oc in range(CD):
                    if pass0:
                        for grp in range(4):
                            n = 4 * oc + grp
                            e.wait_ge(p.sem(sld), b_ld + 16 * (n + 1))
                            if oc >= 2 and grp == 0:
                                e.wait_ge(p.sem(spe), b_pe + 64 * (oc - 1))
                                e.wait_ge(p.sem(swb), b_wb + 16 * (oc - 1))
                            e.tensor_copy(wdb[oc % 2][:, grp * 2048:(grp + 1) * 2048],
                                          wstage[:]).then_inc(p.sem(sdve), 1)
                            p.bump(sdve, 1)
                    e.wait_ge(p.sem(spe), b_pe + 64 * (oc + 1))
                    if oc >= 2:
                        e.wait_ge(p.sem(sst), b_st + 16 * (oc - 1))
                    if li == 0:
                        e.tensor_scalar(dn[oc % 2][:], psD[oc % 2][:],
                                        g["ws_dp"][:, oc:oc + 1], None, ALU.mult
                                        ).then_inc(p.sem(sdve), 1)
                    else:
                        e.tensor_copy(dn[oc % 2][:], psD[oc % 2][:]
                                      ).then_inc(p.sem(sdve), 1)
                    p.bump(sdve, 1)

            @blk.tensor
            def _(e):
                for oc in range(CD):
                    if oc >= 2:
                        e.wait_ge(p.sem(sdve), b_dve + vo * (oc - 2) + vo)
                    for sc in range(HC):
                        if pass0:
                            grp = sc // 16
                            e.wait_ge(p.sem(sdve), b_dve + vo * oc + grp + 1)
                        elif sc == 0:
                            e.wait_ge(p.sem(sld), b_ld + 16 * (oc + 1))
                        ins = e.matmul(psD[oc % 2][:], wdb[oc % 2][:, sc * 128:(sc + 1) * 128],
                                       h_sb[:, sc * 512:(sc + 1) * 512],
                                       start=(sc == 0), stop=(sc == HC - 1),
                                       skip_group_check=True)
                        if sc == HC - 1:
                            ins.then_inc(p.sem(spe), HC)
                        p.bump(spe, 1)

            @blk.gpsimd
            def _(e):
                for oc in range(CD):
                    if pass0:
                        e.wait_ge(p.sem(sdve), b_dve + vo * oc + 4)
                        off = OFF_D + oc * 1048576
                        e.dma_start(
                            out=wbf.ap()[off:off + 1048576].rearrange(
                                "(q w) -> q w", q=128),
                            in_=wdb[oc % 2][:]).then_inc(p.sem(swb), 16)
                        p.bump(swb, 16)
                    e.wait_ge(p.sem(sdve), b_dve + vo * oc + vo)
                    e.dma_start(
                        out=p.dram["ffn_out"].ap().rearrange("(c q) t -> c q t", q=128)
                        [oc, :, tb * 512:(tb + 1) * 512],
                        in_=dn[oc % 2][:]).then_inc(p.sem(sst), 16)
                    p.bump(sst, 16)
                e.wait_ge(p.sem(sst), b_st + 16 * CD)



def emit_final_touch(p, li, x_out):
    if "p4" in p.phases:
        return
    nc = p.nc
    with nc.Block() as blk:
        @blk.gpsimd
        def _(e):
            s = p.sem(f"ft_{li}")
            e.dma_start(out=x_out.ap(), in_=p.dram["x0"].ap()).then_inc(s, 16)
            p.bump(f"ft_{li}", 16)
            e.wait_ge(s, p.val(f"ft_{li}"))


def emit_p2p3(p, li):
    nc = p.nc
    g = p.g
    pre = f"l{li}_"

    with contextlib.ExitStack() as ph:
        Q_sb = ph.enter_context(nc.sbuf_tensor(f"q_sb_{li}", [128, HPC * T], BF16))
        K_sb = ph.enter_context(nc.sbuf_tensor(f"k_sb_{li}", [128, HPC * T], BF16))
        V_tm = ph.enter_context(nc.sbuf_tensor(f"v_tm_{li}", [128, (T // 128) * HPC * HD], BF16))
        maskA = ph.enter_context(nc.sbuf_tensor(f"mA_{li}", [128, 4 * 512], F32))
        maskB = ph.enter_context(nc.sbuf_tensor(f"mB_{li}", [128, 4 * 512], F32))
        emit_p2(p, li, Q_sb, K_sb, V_tm, maskA, maskB)
        p.barrier()
        if "p3" in p.phases:
            emit_p3(p, li, Q_sb, K_sb, V_tm, maskA, maskB)
            p.barrier()


def emit_p2(p, li, Q_sb, K_sb, V_tm, maskA, maskB):
    nc = p.nc
    g = p.g
    pre = f"l{li}_"

    qkvw_d = p.dram[pre + "qkvw"]

    sld = f"p2ld_{li}"
    swl = f"p2wl_{li}"
    sdve = f"p2dve_{li}"
    sact = f"p2act_{li}"
    spe = f"p2pe_{li}"
    b_ld, b_wl, b_dve, b_act, b_pe = (p.val(sld), p.val(swl), p.val(sdve),
                                      p.val(sact), p.val(spe))
    scc = f"cc_ag_{li}"
    cc_t = p.val(scc)
    vconst_t = p.val("vconst")

    if li == 0:
        DVE_TB = 14
        PE_TB = 96 + 8
    else:
        DVE_TB = 2 * 7 + 2 * 3 + 2 * 5
        PE_TB = 96 + 8 + 6

    with contextlib.ExitStack() as ph:
        qkvw = ph.enter_context(nc.sbuf_tensor(f"p2w_{li}", [128, 6 * CD * 128], BF16))
        wstage = ph.enter_context(nc.sbuf_tensor(f"p2ws_{li}", [128, CD * 128], F32))
        xnb = [ph.enter_context(nc.sbuf_tensor(f"p2xn{i}_{li}", [128, CD * 512], BF16))
               for i in range(2)]
        vtmp = ph.enter_context(nc.sbuf_tensor(f"p2vt_{li}", [128, 512], BF16))
        gtmp = ph.enter_context(nc.sbuf_tensor(f"p2gt_{li}", [128, 512], BF16))
        swtmp = ph.enter_context(nc.sbuf_tensor(f"p2sw_{li}", [128, 512], BF16))
        t1tmp = ph.enter_context(nc.sbuf_tensor(f"p2t1_{li}", [128, 512], BF16))
        t2tmp = ph.enter_context(nc.sbuf_tensor(f"p2t2_{li}", [128, 512], BF16))
        roptmp = ph.enter_context(nc.sbuf_tensor(f"p2rp_{li}", [128, 512], BF16))
        ctab = [ph.enter_context(nc.sbuf_tensor(f"p2ct{i}_{li}", [128, 512], BF16))
                for i in range(2)]
        stab = [ph.enter_context(nc.sbuf_tensor(f"p2st{i}_{li}", [128, 512], BF16))
                for i in range(2)]
        psQ = [ph.enter_context(nc.psum_tensor(f"p2pq{i}_{li}", [128, 512], F32))
               for i in range(2)]
        psT = [ph.enter_context(nc.psum_tensor(f"p2pt{i}_{li}", [128, 128], BF16))
               for i in range(2)]
        psR = ph.enter_context(nc.psum_tensor(f"p2pr_{li}", [128, 512], F32)) \
            if li == 1 else None

        with nc.Block() as blk:
            @blk.sync
            def _(e):
                for m in range(4):
                    e.dma_start(out=maskA[:, m * 512:(m + 1) * 512],
                                in_=p.dram["maskA"].ap()[m]).then_inc(p.sem(swl), 16)
                    p.bump(swl, 16)
                    e.dma_start(out=maskB[:, m * 512:(m + 1) * 512],
                                in_=p.dram["maskB"].ap()[m]).then_inc(p.sem(swl), 16)
                    p.bump(swl, 16)
                for s in range(6):
                    if s >= 1:
                        e.wait_ge(p.sem(sdve), b_dve + s)
                    e.dma_start(out=wstage[:], in_=qkvw_d.ap()[s]
                                ).then_inc(p.sem(swl), 16)
                    p.bump(swl, 16)
                e.wait_ge(p.sem(scc), cc_t)
                for tb in range(NTB):
                    if tb >= 2:
                        e.wait_ge(p.sem(spe), b_pe + PE_TB * (tb - 1))
                    r, half = tb // 2, tb % 2
                    e.dma_start(
                        out=xnb[tb % 2][:],
                        in_=p.dram["ag_out"].ap().rearrange(
                            "r (c q) t -> q (r c) t", q=128)
                        [:, r * CD:(r + 1) * CD, half * 512:(half + 1) * 512]
                    ).then_inc(p.sem(sld), 16)
                    p.bump(sld, 16)
                    if li == 1:
                        e.dma_start(out=ctab[tb % 2][:],
                                    in_=p.dram["ropecos"].ap()[:, tb * 512:(tb + 1) * 512]
                                    ).then_inc(p.sem(sld), 16)
                        p.bump(sld, 16)
                        e.dma_start(out=stab[tb % 2][:],
                                    in_=p.dram["ropesin"].ap()[:, tb * 512:(tb + 1) * 512]
                                    ).then_inc(p.sem(sld), 16)
                        p.bump(sld, 16)

            @blk.vector
            def _(e):
                e.wait_ge(p.sem("vconst"), vconst_t)
                for s in range(6):
                    e.wait_ge(p.sem(swl), b_wl + 128 + 16 * (s + 1))
                    e.tensor_copy(qkvw[:, s * 2048:(s + 1) * 2048], wstage[:]
                                  ).then_inc(p.sem(sdve), 1)
                    p.bump(sdve, 1)
                ld_per_tb = 3 if li == 1 else 1
                for tb in range(NTB):
                    dvetb = b_dve + 6 + DVE_TB * tb
                    n = 0

                    def bump1(ins):
                        nonlocal n
                        n += 1
                        ins.then_inc(p.sem(sdve), 1)
                        p.bump(sdve, 1)

                    def wself():
                        e.wait_ge(p.sem(sdve), dvetb + n)

                    col = lambda h: h * T + tb * 512
                    if li == 0:
                        for s in range(6):
                            e.wait_ge(p.sem(spe), b_pe + PE_TB * tb + 16 * (s + 1))
                            ps = psQ[s % 2][:]
                            if s < 2:
                                dst = Q_sb[:, col(s):col(s) + 512]
                                bias = g["bq"][:, s:s + 1]
                            elif s < 4:
                                dst = K_sb[:, col(s - 2):col(s - 2) + 512]
                                bias = g["bk"][:, s - 2:s - 2 + 1]
                            else:
                                dst = vtmp[:]
                                bias = g["bv"][:, s - 4:s - 4 + 1]
                            bump1(e.tensor_scalar(dst, ps, bias, None, ALU.add))
                            if s >= 4:
                                wself()
                                h = s - 4
                                for sub in range(4):
                                    kc = tb * 4 + sub
                                    e.wait_ge(p.sem(spe),
                                              b_pe + PE_TB * tb + 96 + 4 * h + sub + 1)
                                    bump1(e.tensor_copy(
                                        V_tm[:, kc * 256 + h * 128: kc * 256 + h * 128 + 128],
                                        psT[sub % 2][:]))
                    else:
                        pe_off = {0: 16, 1: 34, 2: 51, 3: 69, 4: 86, 5: 102}
                        pe_extra = {0: (17, 18), 2: (52, 53)}
                        pe_rot = {1: 35, 3: 70}
                        for s in range(6):
                            e.wait_ge(p.sem(spe), b_pe + PE_TB * tb + pe_off[s])
                            ps = psQ[s % 2]
                            dstT = Q_sb if s < 2 else (K_sb if s < 4 else None)
                            hcolA = col(0)
                            hcolB = col(1)
                            if s in (0, 2):
                                e.wait_ge(p.sem(sld), b_ld + 16 * ld_per_tb * tb + 48)
                                bump1(e.tensor_copy(gtmp[:], ps[:]))
                                wself()
                                e.wait_ge(p.sem(spe), b_pe + PE_TB * tb + pe_extra[s][0])
                                bump1(e.tensor_copy(swtmp[:], psR[:]))
                                bump1(e.tensor_tensor(t1tmp[:], gtmp[:], ctab[tb % 2][:],
                                                      ALU.mult))
                                wself()
                                bump1(e.tensor_tensor(t2tmp[:], swtmp[:], stab[tb % 2][:],
                                                      ALU.mult))
                                wself()
                                bump1(e.tensor_tensor(roptmp[:], t1tmp[:], t2tmp[:],
                                                      ALU.add))
                                wself()
                                bump1(e.tensor_copy(dstT[0:64, hcolA:hcolA + 512],
                                                    roptmp[0:64, :]))
                                e.wait_ge(p.sem(spe), b_pe + PE_TB * tb + pe_extra[s][1])
                                bump1(e.tensor_copy(dstT[0:64, hcolB:hcolB + 512],
                                                    psR[0:64, :]))
                            elif s in (1, 3):
                                bump1(e.tensor_copy(gtmp[:], ps[:]))
                                wself()
                                e.wait_ge(p.sem(spe), b_pe + PE_TB * tb + pe_rot[s])
                                bump1(e.tensor_copy(dstT[64:128, hcolA:hcolA + 512],
                                                    psR[64:128, :]))
                                bump1(e.tensor_copy(dstT[64:128, hcolB:hcolB + 512],
                                                    gtmp[64:128, :]))
                            else:
                                h = s - 4
                                bump1(e.tensor_copy(vtmp[:], ps[:]))
                                wself()
                                for sub in range(4):
                                    kc = tb * 4 + sub
                                    e.wait_ge(p.sem(spe),
                                              b_pe + PE_TB * tb + 102 + 4 * h + sub + 1)
                                    bump1(e.tensor_copy(
                                        V_tm[:, kc * 256 + h * 128: kc * 256 + h * 128 + 128],
                                        psT[sub % 2][:]))
                    assert n == DVE_TB, (li, tb, n)

            @blk.tensor
            def _(e):
                e.wait_ge(p.sem(sdve), b_dve + 6)
                ld_per_tb = 3 if li == 1 else 1
                for tb in range(NTB):
                    e.wait_ge(p.sem(sld), b_ld + 16 * ld_per_tb * tb + 16)
                    dvetb = b_dve + 6 + DVE_TB * tb
                    for s in range(6):
                        if tb > 0 or s >= 2:
                            if s >= 2:
                                pn = dve_evac_done(li, tb, s - 2)
                            else:
                                pn = dve_evac_done(li, tb - 1, s + 4)
                            e.wait_ge(p.sem(sdve), b_dve + 6 + pn)
                        for cc in range(CD):
                            ins = e.matmul(psQ[s % 2][:],
                                           qkvw[:, s * 2048 + cc * 128: s * 2048 + (cc + 1) * 128],
                                           xnb[tb % 2][:, cc * 512:(cc + 1) * 512],
                                           start=(cc == 0), stop=(cc == CD - 1),
                                           skip_group_check=True)
                            if cc == CD - 1:
                                ins.then_inc(p.sem(spe), CD)
                            p.bump(spe, 1)
                        if li == 1 and s in (0, 2):
                            e.wait_ge(p.sem(sdve),
                                      b_dve + 6 + dve_set_start(li, tb, s) + 1)
                            e.matmul(psR[:], g["mswap"][:], gtmp[:],
                                     start=True, stop=True, skip_group_check=True
                                     ).then_inc(p.sem(spe), 1)
                            p.bump(spe, 1)
                            e.wait_ge(p.sem(sdve),
                                      b_dve + 6 + dve_set_start(li, tb, s) + 5)
                            e.matmul(psR[:], g["mrot"][:], roptmp[:],
                                     start=True, stop=True, skip_group_check=True
                                     ).then_inc(p.sem(spe), 1)
                            p.bump(spe, 1)
                        if li == 1 and s in (1, 3):
                            e.wait_ge(p.sem(sdve),
                                      b_dve + 6 + dve_set_start(li, tb, s) + 1)
                            e.matmul(psR[:], g["mrot"][:], gtmp[:],
                                     start=True, stop=True, skip_group_check=True
                                     ).then_inc(p.sem(spe), 1)
                            p.bump(spe, 1)
                    for h in range(2):
                        s = 4 + h
                        e.wait_ge(p.sem(sdve), b_dve + 6 + dve_vtmp_done(li, tb, h))
                        for sub in range(4):
                            n96 = 4 * h + sub
                            if n96 >= 2:
                                e.wait_ge(p.sem(sdve),
                                          b_dve + 6 + dve_pst_done(li, tb, n96 - 2))
                            e.transpose(psT[sub % 2][:],
                                        vtmp[:, sub * 128:(sub + 1) * 128],
                                        g["id_bf16"][:]).then_inc(p.sem(spe), 1)
                            p.bump(spe, 1)

    if "qkv" in p.dbg:
        with nc.Block() as blk:
            @blk.gpsimd
            def _(e):
                s = p.sem(f"p2dbg_{li}")
                e.wait_ge(p.sem(spe), p.val(spe))
                e.wait_ge(p.sem(sdve), p.val(sdve))
                e.dma_start(out=p.dram["dbg_q"].ap(), in_=Q_sb[:]).then_inc(p.sem(f"p2dbg_{li}"), 16)
                p.bump(f"p2dbg_{li}", 16)
                e.dma_start(out=p.dram["dbg_k"].ap(), in_=K_sb[:]).then_inc(p.sem(f"p2dbg_{li}"), 16)
                p.bump(f"p2dbg_{li}", 16)
                e.dma_start(out=p.dram["dbg_v"].ap(), in_=V_tm[:]).then_inc(p.sem(f"p2dbg_{li}"), 16)
                p.bump(f"p2dbg_{li}", 16)
                e.wait_ge(p.sem(f"p2dbg_{li}"), p.val(f"p2dbg_{li}"))


_L1_ENDS = [7, 10, 17, 20, 25, 30]
_L1_STARTS = [0, 7, 10, 17, 20, 25]
_L0_ENDS = [1, 2, 3, 4, 9, 14]
_L0_STARTS = [0, 1, 2, 3, 4, 9]


def dve_evac_done(li, tb, s):
    if tb < 0:
        return 0
    DVE_TB = 14 if li == 0 else 30
    base = DVE_TB * tb
    return base + (_L0_ENDS if li == 0 else _L1_ENDS)[s]


def dve_set_start(li, tb, s):
    DVE_TB = 14 if li == 0 else 30
    return DVE_TB * tb + (_L0_STARTS if li == 0 else _L1_STARTS)[s]


def dve_vtmp_done(li, tb, h):
    return dve_set_start(li, tb, 4 + h) + 1


def dve_pst_done(li, tb, n96):
    h, sub = n96 // 4, n96 % 4
    return dve_set_start(li, tb, 4 + h) + 1 + sub + 1



def emit_p3(p, li, Q_sb, K_sb, V_tm, maskA, maskB):
    nc = p.nc
    g = p.g
    sdve = f"p3dve_{li}"
    sact = f"p3act_{li}"
    spe = f"p3pe_{li}"
    sst = f"p3st_{li}"
    scc2 = f"cc_a2a_{li}"

    with contextlib.ExitStack() as ph:
        probsT = [ph.enter_context(nc.sbuf_tensor(f"p3pr{i}_{li}", [128, 512], BF16))
                  for i in range(3)]
        zr = ph.enter_context(nc.sbuf_tensor(f"p3zr_{li}", [1, 512], F32))
        bc = ph.enter_context(nc.sbuf_tensor(f"p3bc_{li}", [128, 512], F32))
        ybuf = [ph.enter_context(nc.sbuf_tensor(f"p3y{i}_{li}", [128, 512], BF16))
                for i in range(2)]
        psB = [ph.enter_context(nc.psum_tensor(f"p3pb{i}_{li}", [128, 512], F32))
               for i in range(3)]
        psZ = ph.enter_context(nc.psum_tensor(f"p3pz_{li}", [1, 512], F32))
        psPV = ph.enter_context(nc.psum_tensor(f"p3pv_{li}", [128, 512], F32))
        psBC = ph.enter_context(nc.psum_tensor(f"p3pbc_{li}", [128, 512], F32))

        import os
        blocks = [(h, B) for h in range(HPC) for B in range(NQB)]
        nblk = int(os.environ.get("P3_BLOCKS", "0"))
        if nblk:
            blocks = blocks[:nblk]

        def b_cnt(B):
            return 4 * B + 4

        pe_c = [0]
        dve_c = [0]
        act_c = [0]
        st_c = [0]
        for (h, B) in blocks:
            pe_c.append(pe_c[-1] + 3 * b_cnt(B) + 2)
            dve_c.append(dve_c[-1] + 7)
            act_c.append(act_c[-1] + b_cnt(B))
            st_c.append(st_c[-1] + 16)
        b_pe, b_dve, b_act, b_st = p.val(spe), p.val(sdve), p.val(sact), p.val(sst)

        def mm1_at(j):
            return 1 if j == 0 else 3 * j - 1

        def pv_at(B, j):
            nb = b_cnt(B)
            return 3 * (j + 1) + 1 if j < nb - 1 else 3 * nb

        def ztail_at(B):
            return 3 * b_cnt(B) - 1

        def qcol(h, B):
            return h * T + B * 512

        with nc.Block() as blk:
            @blk.tensor
            def _(e):
                for bi, (h, B) in enumerate(blocks):
                    npe = b_pe + pe_c[bi]
                    ndve = b_dve + dve_c[bi]
                    nact = b_act + act_c[bi]
                    nb = b_cnt(B)
                    if bi > 0:
                        e.wait_ge(p.sem(sdve), b_dve + dve_c[bi])
                    for j in range(nb):
                        if j >= 3:
                            e.wait_ge(p.sem(sact), nact + j - 2)
                        e.matmul(psB[j % 3][:],
                                 K_sb[:, h * T + 128 * j: h * T + 128 * j + 128],
                                 Q_sb[:, qcol(h, B): qcol(h, B) + 512],
                                 start=True, stop=True, skip_group_check=True
                                 ).then_inc(p.sem(spe), 1)
                        p.bump(spe, 1)
                        if j >= 1:
                            jj = j - 1
                            e.wait_ge(p.sem(sact), nact + jj + 1)
                            e.matmul(psZ[:], g["ones_128x1"][:], probsT[jj % 3][:],
                                     start=(jj == 0), stop=False,
                                     skip_group_check=True).then_inc(p.sem(spe), 1)
                            p.bump(spe, 1)
                            e.matmul(psPV[:],
                                     V_tm[:, jj * 256 + h * 128: jj * 256 + h * 128 + 128],
                                     probsT[jj % 3][:],
                                     start=(jj == 0), stop=False,
                                     skip_group_check=True).then_inc(p.sem(spe), 1)
                            p.bump(spe, 1)
                    jj = nb - 1
                    e.wait_ge(p.sem(sact), nact + jj + 1)
                    e.matmul(psZ[:], g["ones_128x1"][:], probsT[jj % 3][:],
                             start=(jj == 0), stop=True,
                             skip_group_check=True).then_inc(p.sem(spe), 1)
                    p.bump(spe, 1)
                    e.matmul(psPV[:],
                             V_tm[:, jj * 256 + h * 128: jj * 256 + h * 128 + 128],
                             probsT[jj % 3][:],
                             start=(jj == 0), stop=True,
                             skip_group_check=True).then_inc(p.sem(spe), 1)
                    p.bump(spe, 1)
                    e.wait_ge(p.sem(sdve), ndve + 5)
                    e.matmul(psBC[:], g["ones_1x128_f32"][:], zr[:],
                             start=True, stop=True, skip_group_check=True
                             ).then_inc(p.sem(spe), 1)
                    p.bump(spe, 1)
                    e.matmul(psBC[:], g["ones_1x128_f32"][:], zr[:],
                             start=True, stop=True, skip_group_check=True
                             ).then_inc(p.sem(spe), 1)
                    p.bump(spe, 1)
                    assert p.val(spe) == b_pe + pe_c[bi + 1], (bi, p.val(spe))

            @blk.scalar
            def _(e):
                for bi, (h, B) in enumerate(blocks):
                    npe = b_pe + pe_c[bi]
                    ndve = b_dve + dve_c[bi]
                    nb = b_cnt(B)
                    for j in range(nb):
                        if j >= 4 * B:
                            e.wait_ge(p.sem(sdve), ndve + (j - 4 * B) + 1)
                        else:
                            e.wait_ge(p.sem(spe), npe + mm1_at(j))
                        if j >= 3:
                            e.wait_ge(p.sem(spe), npe + pv_at(B, j - 3))
                        elif bi > 0:
                            e.wait_ge(p.sem(spe), b_pe + pe_c[bi])
                        e.activation(probsT[j % 3][:], psB[j % 3][:], AF.Exp,
                                     scale=SC).then_inc(p.sem(sact), 1)
                        p.bump(sact, 1)

            @blk.vector
            def _(e):
                for bi, (h, B) in enumerate(blocks):
                    npe = b_pe + pe_c[bi]
                    ndve = b_dve + dve_c[bi]
                    nact = b_act + act_c[bi]
                    nb = b_cnt(B)
                    for m in range(4):
                        j = 4 * B + m
                        e.wait_ge(p.sem(spe), npe + mm1_at(j))
                        e.tensor_tensor(psB[j % 3][:], psB[j % 3][:],
                                        maskB[:, 512 * m: 512 * m + 512], ALU.add
                                        ).then_inc(p.sem(sdve), 1)
                        p.bump(sdve, 1)
                    e.wait_ge(p.sem(spe), npe + ztail_at(B) + 1)
                    e.reciprocal(zr[:], psZ[:]).then_inc(p.sem(sdve), 1)
                    p.bump(sdve, 1)
                    e.wait_ge(p.sem(spe), b_pe + pe_c[bi + 1])
                    e.tensor_copy(bc[:], psBC[:]).then_inc(p.sem(sdve), 1)
                    p.bump(sdve, 1)
                    e.wait_ge(p.sem(sdve), ndve + 6)
                    if bi >= 2:
                        e.wait_ge(p.sem(sst), b_st + st_c[bi - 1])
                    e.tensor_tensor(ybuf[bi % 2][:], psPV[:], bc[:], ALU.mult
                                    ).then_inc(p.sem(sdve), 1)
                    p.bump(sdve, 1)

            @blk.gpsimd
            def _(e):
                for bi, (h, B) in enumerate(blocks):
                    e.wait_ge(p.sem(sdve), b_dve + dve_c[bi + 1])
                    e.dma_start(
                        out=p.dram["a2a_in"].ap()[B // 2,
                                                  h * 128:(h + 1) * 128,
                                                  (B % 2) * 512:(B % 2) * 512 + 512],
                        in_=ybuf[bi % 2][:]).then_inc(p.sem(sst), 16)
                    p.bump(sst, 16)
                e.wait_ge(p.sem(sst), b_st + st_c[-1])
                with allow_ici_all_to_all():
                    e.collective_compute(
                        "AllToAll",
                        ALU.bypass,
                        replica_groups=[list(range(NCORE))],
                        ins=[p.dram["a2a_in"].ap().rearrange("r x t -> (r x) t").opt()],
                        outs=[p.dram["a2a_out"].ap().rearrange("r x t -> (r x) t").opt()],
                    ).then_inc(p.sem(scc2), 1)
                p.bump(scc2, 1)
                if "yattn" in p.dbg:
                    e.wait_ge(p.sem(scc2), p.val(scc2))
                    e.dma_start(out=p.dram["dbg_yattn"].ap().rearrange("r x t -> (r x) t"),
                                in_=p.dram["a2a_out"].ap().rearrange("r x t -> (r x) t")
                                ).then_inc(p.sem(sst), 16)
                    p.bump(sst, 16)
                    e.wait_ge(p.sem(sst), p.val(sst))



def emit_p4(p, li, x_in, x_out):
    nc = p.nc
    g = p.g
    pre = f"l{li}_"
    wo_d = p.dram[pre + "wo"]

    sld = f"p4ld_{li}"
    swl = f"p4wl_{li}"
    sdve = f"p4dve_{li}"
    spe = f"p4pe_{li}"
    sst = f"p4st_{li}"
    b_ld, b_wl, b_dve, b_pe, b_st = (p.val(sld), p.val(swl), p.val(sdve),
                                     p.val(spe), p.val(sst))
    scc2 = f"cc_a2a_{li}"
    cc2_t = p.val(scc2)

    with contextlib.ExitStack() as ph:
        yab = ph.enter_context(nc.sbuf_tensor(f"p4y_{li}", [128, CD * 512], BF16))
        fm = ph.enter_context(nc.sbuf_tensor(f"p4fm_{li}", [128, CD * 512], F32))
        wostage = ph.enter_context(nc.sbuf_tensor(f"p4ws_{li}", [128, 2048], F32))
        wob = [ph.enter_context(nc.sbuf_tensor(f"p4wb{i}_{li}", [128, 2048], BF16))
               for i in range(2)]
        ffnb = [ph.enter_context(nc.sbuf_tensor(f"p4fb{i}_{li}", [128, 512], F32))
                for i in range(2)]
        xold = [ph.enter_context(nc.sbuf_tensor(f"p4xo{i}_{li}", [128, D], F32))
                for i in range(2)]
        xnew = [ph.enter_context(nc.sbuf_tensor(f"p4xn{i}_{li}", [128, D], F32))
                for i in range(2)]
        psO = [ph.enter_context(nc.psum_tensor(f"p4po{i}_{li}", [128, 512], F32))
               for i in range(2)]
        psTr = [ph.enter_context(nc.psum_tensor(f"p4pt{i}_{li}", [128, 128], F32))
                for i in range(2)]

        PE_TB = 256 + 64
        DVE_TB = 32 + 64

        with nc.Block() as blk:
            @blk.sync
            def _(e):
                e.wait_ge(p.sem(scc2), cc2_t)
                for tb in range(2):
                    if tb >= 1:
                        e.wait_ge(p.sem(spe), b_pe + PE_TB * tb)
                    e.dma_start(
                        out=yab[:],
                        in_=p.dram["a2a_out"].ap().rearrange(
                            "r (q x) t -> x (r q) t", x=128)
                        [:, :, tb * 512:(tb + 1) * 512]).then_inc(p.sem(sld), 16)
                    p.bump(sld, 16)
                    for oc in range(CD):
                        if tb > 0 or oc >= 2:
                            prev = (tb, oc - 2) if oc >= 2 else (tb - 1, CD - 2)
                            e.wait_ge(p.sem(sdve),
                                      b_dve + DVE_TB * prev[0] + 2 * prev[1] + 2)
                        e.dma_start(
                            out=ffnb[oc % 2][:],
                            in_=p.dram["ffn_out"].ap().rearrange(
                                "(c q) t -> c q t", q=128)
                            [oc, :, tb * 512:(tb + 1) * 512]).then_inc(p.sem(sld), 16)
                        p.bump(sld, 16)
                    for tt in range(4):
                        if tb > 0 or tt >= 2:
                            prev = (tb, tt - 2) if tt >= 2 else (tb - 1, 2)
                            e.wait_ge(p.sem(sdve),
                                      b_dve + DVE_TB * prev[0] + 32 + 16 * (prev[1] + 1))
                        i = tb * 4 + tt
                        e.dma_start(out=xold[tt % 2][:],
                                    in_=x_in.ap()[i * 128:(i + 1) * 128, :]
                                    ).then_inc(p.sem(sld), 16)
                        p.bump(sld, 16)

            @blk.gpsimd
            def _(e):
                for tb in range(2):
                    for oc in range(CD):
                        n = tb * CD + oc
                        if n >= 1:
                            pt, po = (n - 1) // CD, (n - 1) % CD
                            e.wait_ge(p.sem(sdve), b_dve + DVE_TB * pt + 2 * po + 1)
                        if n >= 2:
                            pt, po = (n - 2) // CD, (n - 2) % CD
                            e.wait_ge(p.sem(spe), b_pe + PE_TB * pt + 16 * (po + 1))
                        e.dma_start(
                            out=wostage[:],
                            in_=wo_d.ap()[oc]).then_inc(p.sem(swl), 16)
                        p.bump(swl, 16)
                e.wait_ge(p.sem(sst), b_st + 16 * 8)
                if li == 0 and "x1" in p.dbg:
                    s = p.sem(f"p4dbg_{li}")
                    e.dma_start(out=p.dram["dbg_x1"].ap(), in_=x_out.ap()
                                ).then_inc(s, 16)
                    p.bump(f"p4dbg_{li}", 16)
                    e.wait_ge(s, p.val(f"p4dbg_{li}"))

            @blk.vector
            def _(e):
                for tb in range(2):
                    for oc in range(CD):
                        n = tb * CD + oc
                        e.wait_ge(p.sem(swl), b_wl + 16 * (n + 1))
                        e.tensor_copy(wob[n % 2][:], wostage[:]).then_inc(p.sem(sdve), 1)
                        p.bump(sdve, 1)
                        e.wait_ge(p.sem(spe), b_pe + PE_TB * tb + 16 * (oc + 1))
                        e.wait_ge(p.sem(sld), b_ld + 16 * (21 * tb + 1 + oc + 1))
                        if li == 0:
                            e.scalar_tensor_tensor(
                                fm[:, oc * 512:(oc + 1) * 512], psO[oc % 2][:],
                                g["bo"][:, oc:oc + 1], ffnb[oc % 2][:],
                                ALU.add, ALU.add).then_inc(p.sem(sdve), 1)
                        else:
                            e.tensor_tensor(
                                fm[:, oc * 512:(oc + 1) * 512], psO[oc % 2][:],
                                ffnb[oc % 2][:], ALU.add).then_inc(p.sem(sdve), 1)
                        p.bump(sdve, 1)
                    for tt in range(4):
                        e.wait_ge(p.sem(sld), b_ld + 16 * (21 * tb + 17 + tt + 1))
                        i = tb * 4 + tt
                        if i >= 2:
                            e.wait_ge(p.sem(sst), b_st + 16 * (i - 1))
                        for oc in range(CD):
                            ntr = 16 * tt + oc
                            e.wait_ge(p.sem(spe),
                                      b_pe + PE_TB * tb + 256 + ntr + 1)
                            e.tensor_tensor(
                                xnew[tt % 2][:, oc * 128:(oc + 1) * 128],
                                psTr[ntr % 2][:],
                                xold[tt % 2][:, oc * 128:(oc + 1) * 128],
                                ALU.add).then_inc(p.sem(sdve), 1)
                            p.bump(sdve, 1)


            @blk.scalar
            def _(e):
                for tb in range(2):
                    for tt in range(4):
                        i = tb * 4 + tt
                        e.wait_ge(p.sem(sdve),
                                  b_dve + DVE_TB * tb + 32 + 16 * (tt + 1))
                        if i >= 2:
                            e.wait_ge(p.sem(sst), b_st + 16 * (i - 1))
                        e.dma_start(out=x_out.ap()[i * 128:(i + 1) * 128, :],
                                    in_=xnew[tt % 2][:]).then_inc(p.sem(sst), 16)
                        p.bump(sst, 16)

            @blk.tensor
            def _(e):
                for tb in range(2):
                    e.wait_ge(p.sem(sld), b_ld + 16 * (21 * tb + 1))
                    for oc in range(CD):
                        n = tb * CD + oc
                        e.wait_ge(p.sem(sdve), b_dve + DVE_TB * tb + 2 * oc + 1)
                        if n >= 2:
                            pt, po = (n - 2) // CD, (n - 2) % CD
                            e.wait_ge(p.sem(sdve), b_dve + DVE_TB * pt + 2 * po + 2)
                        for c in range(CD):
                            ins = e.matmul(psO[oc % 2][:],
                                           wob[n % 2][:, c * 128:(c + 1) * 128],
                                           yab[:, c * 512:(c + 1) * 512],
                                           start=(c == 0), stop=(c == CD - 1),
                                           skip_group_check=True)
                            if c == CD - 1:
                                ins.then_inc(p.sem(spe), CD)
                            p.bump(spe, 1)
                    for tt in range(4):
                        for oc in range(CD):
                            ntr = 16 * tt + oc
                            e.wait_ge(p.sem(sdve), b_dve + DVE_TB * tb + 2 * oc + 2)
                            if ntr >= 2:
                                e.wait_ge(p.sem(sdve),
                                          b_dve + DVE_TB * tb + 32 + ntr - 1)
                            e.transpose(psTr[ntr % 2][:],
                                        fm[:, oc * 512 + tt * 128: oc * 512 + tt * 128 + 128],
                                        g["id_f32"][:]).then_inc(p.sem(spe), 1)
                            p.bump(spe, 1)


def kernel(**inp):
    prog = build()
    in_maps = [prep_inputs(inp, c) for c in range(NCORE)]
    res = run_bass_kernel_spmd(prog.nc, in_maps, core_ids=list(range(NCORE)))
    out = np.concatenate([res.results[c]["y"] for c in range(NCORE)], axis=0)
    return out.reshape(1, T, D).astype(np.float32)
